# version 08
# baseline (speedup 1.0000x reference)
import os as _os
import numpy as np
import concourse.bacc as bacc_mod
import concourse.mybir as mybir
from concourse.bass_utils import run_bass_kernel_spmd

F32 = mybir.dt.float32
BF16 = mybir.dt.bfloat16
I16 = mybir.dt.int16
I32 = mybir.dt.int32
OP = mybir.AluOpType
AF = mybir.ActivationFunctionType
AX = mybir.AxisListType.X

S = 2048
D = 1024
NH = 16
F = 4096
T = 16384
SEG = 2080
EPS = 1e-5
BIGE = 65536.0


PHASE_LIMIT = _os.environ.get("KPHASE", "Z")
KDEBUG = _os.environ.get("KDEBUG", "") == "1"


def build_nc():
    nc = bacc_mod.Bacc("TRN2", target_bir_lowering=False)

    x_in = nc.dram_tensor("x_in", [S, D], F32, kind="ExternalInput").ap()
    wq_in = nc.dram_tensor("wq_in", [D, D], F32, kind="ExternalInput").ap()
    wk_in = nc.dram_tensor("wk_in", [D, D], F32, kind="ExternalInput").ap()
    wv_in = nc.dram_tensor("wv_in", [D, D], F32, kind="ExternalInput").ap()
    wo_in = nc.dram_tensor("wo_in", [D, D], F32, kind="ExternalInput").ap()
    rw_in = nc.dram_tensor("rw_in", [D, 8], F32, kind="ExternalInput").ap()
    w1_in = nc.dram_tensor("w1_in", [D, F], F32, kind="ExternalInput").ap()
    b1_in = nc.dram_tensor("b1_in", [F], F32, kind="ExternalInput").ap()
    w2_in = nc.dram_tensor("w2_in", [F, D], F32, kind="ExternalInput").ap()
    b2_in = nc.dram_tensor("b2_in", [1, D], F32, kind="ExternalInput").ap()
    masklt_in = nc.dram_tensor("masklt_in", [8, 16], F32, kind="ExternalInput").ap()
    onehot_in = nc.dram_tensor("onehot_in", [8, 8], F32, kind="ExternalInput").ap()
    ramp_in = nc.dram_tensor("ramp_in", [128, 16], F32, kind="ExternalInput").ap()
    out_t = nc.dram_tensor("out", [S, D], F32, kind="ExternalOutput").ap()

    x1f = nc.dram_tensor("x1f", [S, D], F32)
    xg_i = nc.dram_tensor("xg_i", [S * D], BF16)
    xg_o = nc.dram_tensor("xg_o", [T * D], BF16)
    cnt_i = nc.dram_tensor("cnt_i", [8], F32)
    cnt_o = nc.dram_tensor("cnt_o", [64], F32)
    listtab = nc.dram_tensor("listtab", [8 * SEG, 64], F32)
    seg_t = nc.dram_tensor("seg_t", [SEG * 64], F32)
    rt = nc.dram_tensor("rt", [2, S], F32)
    yg_i = nc.dram_tensor("yg_i", [S * D], BF16)
    yg_o = nc.dram_tensor("yg_o", [T * D], BF16)

    ident = nc.inline_tensor(np.eye(128, dtype=np.float32), name="cident").ap()
    sl16_np = np.zeros((16, 16), np.float32)
    for k in range(16):
        sl16_np[k, k + 1:] = 1.0
    sl16_d = nc.inline_tensor(sl16_np, name="csl16").ap()
    masks_np = np.zeros((128, 4, 512), np.float32)
    for dd in range(4):
        for p in range(128):
            if p + dd * 128 < 512:
                masks_np[p, dd, p + dd * 128:] = 1.0
    masks_d = nc.inline_tensor(masks_np, name="cmasks").ap()
    erowp_d = nc.inline_tensor(
        np.tile(np.arange(8, dtype=np.float32)[None, :] + BIGE, (128, 1)),
        name="cerowp").ap()
    erow_d = nc.inline_tensor(
        np.tile(np.arange(8, dtype=np.float32)[None, :], (128, 1)), name="cerow").ap()
    k512_d = nc.inline_tensor(
        np.array([[0.0, 512.0, 1024.0, 1536.0, 0.0, 0.0, 0.0, 0.0]], np.float32),
        name="ck512").ap()

    from contextlib import ExitStack
    with ExitStack() as stk:
        ex = stk.enter_context

        ident_sb = ex(nc.sbuf_tensor("ident_sb", [128, 128], F32))
        ones_sb = ex(nc.sbuf_tensor("ones_sb", [128, 64], F32))
        onesb_sb = ex(nc.sbuf_tensor("onesb_sb", [2, 128], BF16))
        erow_sb = ex(nc.sbuf_tensor("erow_sb", [128, 8], F32))
        erowp_sb = ex(nc.sbuf_tensor("erowp_sb", [128, 8], F32))
        k512_sb = ex(nc.sbuf_tensor("k512_sb", [1, 8], F32))
        gate_tm = ex(nc.sbuf_tensor("gate_tm", [128, 16], F32))
        gact_tm = ex(nc.sbuf_tensor("gact_tm", [128, 16], F32))
        emask_tm = ex(nc.sbuf_tensor("emask_tm", [128, 16, 8], F32))
        small = ex(nc.sbuf_tensor("small", [128, 32], F32))
        lnv_sb = ex(nc.sbuf_tensor("lnv_sb", [128, 8], F32))
        posm_sb = ex(nc.sbuf_tensor("posm_sb", [128, 128], I16))
        gidx_sb = ex(nc.sbuf_tensor("gidx_sb", [128, 128], I16))
        xidx_sb = ex(nc.sbuf_tensor("xidx_sb", [128, 128], I16))
        ccnt_sb = ex(nc.sbuf_tensor("ccnt_sb", [1, 8], I32))
        ccntf_sb = ex(nc.sbuf_tensor("ccntf_sb", [1, 8], F32))
        b1_sb = ex(nc.sbuf_tensor("b1_sb", [128, 32], F32))
        b2hl_sb = ex(nc.sbuf_tensor("b2hl_sb", [2, D], BF16))
        meta_sb = ex(nc.sbuf_tensor("meta_sb", [8, 24], F32))
        rvec = ex(nc.sbuf_tensor("rvec", [16, 64], F32))
        ramp_sb = ex(nc.sbuf_tensor("ramp_sb", [128, 16], F32))
        cnts_ag = ex(nc.sbuf_tensor("cnts_ag", [8, 8], F32))

        sc_ps = [ex(nc.psum_tensor("sc_ps%d" % i, [128, 512], F32)) for i in range(2)]
        z_ps = [ex(nc.psum_tensor("z_ps%d" % i, [65, 512], F32)) for i in range(2)]
        b_ps = ex(nc.psum_tensor("b_ps", [64, 512], F32))
        ao_ps = [ex(nc.psum_tensor("ao_ps%d" % i, [128, 512], F32)) for i in range(2)]
        ms_ps = ex(nc.psum_tensor("ms_ps", [128, 512], F32))

        sem_names = ["syi%d" % i for i in range(16)] + ["ldc", "lx0e", "lx0o", "lb2", "lx1", "gpw", "gpwt", "gpc",
                     "gsc", "gw1", "gw2", "lrt",
                     "gxg0", "gxg1", "gxg2", "gxg3", "gyg0", "gyg1", "gyg2", "gyg3",
                     "sx0e", "sx0o", "sxf", "sxg", "sct", "srt",
                     "soute", "souto",
                     "pe", "dv", "ac", "cc"]
        sems = {n: ex(nc.semaphore(n)) for n in sem_names}
        cnt = {n: 0 for n in sem_names}
        plan = {k: [] for k in ("tensor", "vector", "scalar", "gpsimd", "sync")}

        def emit(eng, fn):
            plan[eng].append(fn)

        def dma(eng, sem, out_ap, in_ap, **kw):
            cnt[sem] += 16
            semh = sems[sem]
            emit(eng, lambda e, o=out_ap, i=in_ap: e.dma_start(out=o, in_=i, **kw)
                 .then_inc(semh, 16))
            return cnt[sem]

        def wait(eng, sem, val):
            semh = sems[sem]
            emit(eng, lambda e, v=val: e.wait_ge(semh, v))

        ENG_SEM = {"vector": "dv", "scalar": "ac"}

        def opi(eng, sem, fn):
            own = ENG_SEM.get(eng)
            if own is not None and cnt[own] > 0:
                semo = sems[own]
                emit(eng, lambda e, v=cnt[own]: e.wait_ge(semo, v))
            cnt[sem] += 1
            semh = sems[sem]
            emit(eng, lambda e, f=fn: f(e).then_inc(semh, 1))
            return cnt[sem]

        def op0(eng, fn):
            emit(eng, lambda e, f=fn: f(e))

        marks = {}

        def mark(name):
            marks[name] = ({k: len(v) for k, v in plan.items()}, dict(cnt))

        def phase_barrier(engines):
            snap = [(nm, cnt[nm]) for nm in sem_names if cnt[nm] > 0]
            for eng in engines:
                for nm, v in snap:
                    wait(eng, nm, v)

        stAB = ExitStack()
        v_tok = stAB.enter_context(nc.sbuf_tensor("v_tok", [128, 16, NH * 65], BF16))
        q_sb = stAB.enter_context(nc.sbuf_tensor("q_sb", [128, 8, S], BF16))
        k_sb = stAB.enter_context(nc.sbuf_tensor("k_sb", [128, 8, S], BF16))
        wo_sb = stAB.enter_context(nc.sbuf_tensor("wo_sb", [128, 8, D], BF16))
        masks_sb = stAB.enter_context(nc.sbuf_tensor("masks_sb", [128, 4, 512], BF16))
        rw_sb = stAB.enter_context(nc.sbuf_tensor("rw_sb", [128, 8, 8], F32))

        stA = ExitStack()
        wtmp_sb = stA.enter_context(nc.sbuf_tensor("wtmp_sb", [128, 8, D], BF16))
        x0t_sb = stA.enter_context(nc.sbuf_tensor("x0t_sb", [128, 8, S], BF16))
        x0_sb = stA.enter_context(nc.sbuf_tensor("x0_sb", [128, 2, D], F32))
        b2s_sb = stA.enter_context(nc.sbuf_tensor("b2s_sb", [1, 2 * D], F32))
        b1t_sb = stA.enter_context(nc.sbuf_tensor("b1t_sb", [32, 128], F32))
        b2w_sb = stA.enter_context(nc.sbuf_tensor("b2w_sb", [1, 2, D], BF16))

        dma("sync", "ldc", ident_sb[:, :], ident[:, :])
        dma("sync", "ldc", erow_sb[:, :], erow_d[:, :])
        dma("sync", "ldc", erowp_sb[:, :], erowp_d[:, :])
        dma("sync", "ldc", k512_sb[:, :], k512_d[:, :])
        dma("sync", "ldc", rw_sb[:, :, :], rw_in.rearrange("(c p) e -> p c e", p=128))
        dma("sync", "ldc", meta_sb[:, 0:16], masklt_in[:, :])
        dma("sync", "ldc", meta_sb[:, 16:24], onehot_in[:, :])
        dma("sync", "ldc", b1t_sb[:, :], b1_in.rearrange("(a p) -> a p", p=128))
        dma("sync", "ldc", b2s_sb[0:1, 0:D], b2_in[:, :])
        dma("sync", "ldc", rvec[0:16, 48:64], sl16_d[:, :])
        T_LDC = dma("sync", "ldc", ramp_sb[:, :], ramp_in[:, :])
        assert T_LDC == 176
        dma("gpsimd", "gpw", masks_sb[:, :, :], masks_d[:, :, :])
        T_GPW = dma("gpsimd", "gpw", wo_sb[:, :, :], wo_in.rearrange("(c p) e -> p c e", p=128))

        opi("vector", "dv", lambda e: e.memset(ones_sb[:, :], 1.0))
        opi("vector", "dv", lambda e: e.memset(onesb_sb[:, :], 1.0))
        t_vones = opi("vector", "dv", lambda e: e.memset(
            v_tok.ap().rearrange("p a (h c) -> p a h c", c=65)[:, :, :, 64:65], 1.0))
        wait("vector", "ldc", T_LDC)
        opi("vector", "dv", lambda e: e.tensor_copy(b2w_sb[0:1, 0, :], b2s_sb[0:1, 0:D]))
        opi("vector", "dv", lambda e: e.tensor_copy(b2s_sb[0:1, D:2 * D], b2w_sb[0:1, 0, :]))
        opi("vector", "dv", lambda e: e.tensor_sub(
            b2s_sb[0:1, D:2 * D], b2s_sb[0:1, 0:D], b2s_sb[0:1, D:2 * D]))
        t_b2pre = opi("vector", "dv", lambda e: e.tensor_copy(
            b2w_sb[0:1, 1, :], b2s_sb[0:1, D:2 * D]))
        wait("sync", "dv", t_b2pre)
        dma("sync", "lb2", b2hl_sb[0:1, :], b2w_sb[0:1, 0, :])
        t_b2l_dma = dma("sync", "lb2", b2hl_sb[1:2, :], b2w_sb[0:1, 1, :])

        wait("tensor", "ldc", T_LDC)
        tpe_b1 = opi("tensor", "pe", lambda e: e.transpose(
            ms_ps[:, 0:32], b1t_sb[:, :], ident_sb[0:32, 0:32]))
        wait("vector", "pe", tpe_b1)
        tdv_b1 = opi("vector", "dv", lambda e: e.tensor_copy(b1_sb[:, :], ms_ps[:, 0:32]))
        wait("tensor", "dv", tdv_b1)
        t_xtr_done = [0] * 16
        for tb in range(16):
            if tb >= 2:
                wait("sync", "dv", t_xtr_done[tb - 2])
            t_ld = dma("sync", "lx0e" if tb % 2 == 0 else "lx0o",
                       x0_sb[:, tb % 2, :], x_in[tb * 128:(tb + 1) * 128, :])
            wait("tensor", "lx0e" if tb % 2 == 0 else "lx0o", 16 * (tb // 2 + 1))
            for dc in range(8):
                tpe = opi("tensor", "pe", lambda e, s=x0_sb[:, tb % 2, dc * 128:(dc + 1) * 128]:
                          e.transpose(ms_ps[:, 0:128], s, ident_sb[:, :]))
                wait("vector", "pe", tpe)
                tdv = opi("vector", "dv", lambda e, d=x0t_sb[:, dc, tb * 128:(tb + 1) * 128]:
                          e.tensor_copy(d, ms_ps[:, 0:128]))
                wait("tensor", "dv", tdv)
            t_xtr_done[tb] = cnt["dv"]

        dma("gpsimd", "gpwt", wtmp_sb[:, :, :],
            wv_in.rearrange("(c p) a -> p c a", p=128))
        wait("tensor", "gpwt", 16)
        wait("tensor", "dv", t_vones)
        for tb in range(16):
            for half in range(2):
                ps = sc_ps[half]
                for dc in range(8):
                    fn = (lambda e, l=x0t_sb[:, dc, tb * 128:(tb + 1) * 128],
                          r=wtmp_sb[:, dc, half * 512:(half + 1) * 512], p=ps,
                          s=(dc == 0), t=(dc == 7):
                          e.matmul(p[:, :], l, r, start=s, stop=t))
                    if dc < 7:
                        op0("tensor", fn)
                    else:
                        tpe = opi("tensor", "pe", fn)
                dst = v_tok.ap().rearrange("p a (h c) -> p a h c", c=65)[
                    :, tb, half * 8:(half + 1) * 8, 0:64]
                wait("vector", "pe", tpe)
                tdv = opi("vector", "dv", lambda e, d=dst, p=ps: e.tensor_copy(d, p[:, :]))
                wait("tensor", "dv", tdv)
        for wi, (w_dram, dsts) in enumerate(((wq_in, q_sb), (wk_in, k_sb))):
            wait("gpsimd", "dv", cnt["dv"])
            dma("gpsimd", "gpwt", wtmp_sb[:, :, :],
                w_dram.rearrange("(c p) a -> p c a", p=128))
            wait("tensor", "gpwt", 16 * (wi + 2))
            for ac in range(8):
                for tt in range(4):
                    ps = sc_ps[tt % 2]
                    for dc in range(8):
                        fn = (lambda e, l=wtmp_sb[:, dc, ac * 128:(ac + 1) * 128],
                              r=x0t_sb[:, dc, tt * 512:(tt + 1) * 512], p=ps,
                              s=(dc == 0), t=(dc == 7):
                              e.matmul(p[:, :], l, r, start=s, stop=t))
                        if dc < 7:
                            op0("tensor", fn)
                        else:
                            tpe = opi("tensor", "pe", fn)
                    wait("vector", "pe", tpe)
                    tdv = opi("vector", "dv", lambda e, d=dsts[:, ac, tt * 512:(tt + 1) * 512],
                              p=ps: e.tensor_copy(d, p[:, :]))
                    wait("tensor", "dv", tdv)
        stA.close()

        mark("B")
        strip = stAB.enter_context(nc.sbuf_tensor("strip", [128, 16, 512], BF16))
        z_sb = stAB.enter_context(nc.sbuf_tensor("z_sb", [128, 8, 512], BF16))
        recip_sb = stAB.enter_context(nc.sbuf_tensor("recip_sb", [1, 512], F32))
        bsb_sb = stAB.enter_context(nc.sbuf_tensor("bsb_sb", [64, 512], F32))
        x1_sb = stAB.enter_context(nc.sbuf_tensor("x1_sb", [128, D], F32))
        x1b_sb = stAB.enter_context(nc.sbuf_tensor("x1b_sb", [128, D], BF16))
        x1t_sb = stAB.enter_context(nc.sbuf_tensor("x1t_sb", [128, 8, 128], F32))
        sq_sb = stAB.enter_context(nc.sbuf_tensor("sq_sb", [128, D], F32))

        wait("vector", "lb2", 32)
        wait("scalar", "lb2", 32)
        wait("tensor", "gpw", T_GPW)
        t_eq_prev = 0
        for j in range(4):
            nch = 4 * j + 4
            for a in range(NH):
                par = (j * 16 + a) % 2
                rowb = 64 * (a % 2)
                ahc = a // 2
                t_exp = [0] * nch
                wait("tensor", "ac", cnt["ac"])
                for i in range(nch):
                    ps = sc_ps[i % 2]
                    if i >= 2:
                        wait("tensor", "ac", t_exp[i - 2])
                    tpe = opi("tensor", "pe", lambda e,
                              l=q_sb[rowb:rowb + 64, ahc, i * 128:(i + 1) * 128],
                              r=k_sb[rowb:rowb + 64, ahc, j * 512:(j + 1) * 512],
                              p=ps: e.matmul(p[:, :], l, r, start=True, stop=True))
                    wait("scalar", "pe", tpe)
                    t_exp[i] = opi("scalar", "ac", lambda e, d=strip[:, i, :], p=ps:
                                   e.activation(d, p[:, :], AF.Exp, bias=0.0, scale=0.125))
                wait("vector", "ac", t_exp[nch - 1])
                for dd in range(4):
                    opi("vector", "dv", lambda e, s=strip[:, 4 * j + dd, :],
                        m=masks_sb[:, dd, :]: e.tensor_tensor(s, s, m, OP.mult))
                t_mask = cnt["dv"]
                zp = z_ps[par]
                wait("tensor", "dv", t_mask)
                for i in range(nch):
                    fn = (lambda e, l=v_tok[:, i, a * 65:(a + 1) * 65],
                          r=strip[:, i, :], p=zp, s=(i == 0), t=(i == nch - 1):
                          e.matmul(p[:, :], l, r, start=s, stop=t))
                    if i < nch - 1:
                        op0("tensor", fn)
                    else:
                        tpe_z = opi("tensor", "pe", fn)
                wait("vector", "pe", tpe_z)
                t_rc = opi("vector", "dv", lambda e, p=zp:
                           e.reciprocal(recip_sb[0:1, :], p[64:65, :]))
                wait("tensor", "dv", t_rc)
                tpe_b = opi("tensor", "pe", lambda e: e.matmul(
                    b_ps[:, :], ones_sb[0:1, 0:64], recip_sb[0:1, :],
                    start=True, stop=True))
                wait("vector", "pe", tpe_b)
                opi("vector", "dv", lambda e: e.tensor_copy(bsb_sb[:, :], b_ps[:, :]))
                t_zn = opi("vector", "dv", lambda e, d=z_sb[rowb:rowb + 64, ahc, :],
                           p=zp: e.tensor_tensor(d, p[0:64, :], bsb_sb[:, :], OP.mult))
                wait("tensor", "dv", t_zn)

            wait("tensor", "gpw", T_GPW)
            for s4 in range(4):
                tb = j * 4 + s4
                for half in range(2):
                    ps = ao_ps[half]
                    for ahc in range(8):
                        fn = (lambda e, l=z_sb[:, ahc, s4 * 128:(s4 + 1) * 128],
                              r=wo_sb[:, ahc, half * 512:(half + 1) * 512], p=ps,
                              s=(ahc == 0), t=(ahc == 7):
                              e.matmul(p[:, :], l, r, start=s, stop=t))
                        if ahc < 7:
                            op0("tensor", fn)
                        else:
                            tpe_ao = opi("tensor", "pe", fn)
                if tb > 0:
                    wait("scalar", "pe", t_rtr_last)
                    wait("scalar", "sxf", 16 * tb)
                sx0 = "sx0e" if tb % 2 == 0 else "sx0o"
                t_x0r = dma("scalar", sx0, x1_sb[:, :], x_in[tb * 128:(tb + 1) * 128, :])
                wait("vector", sx0, 16 * (tb // 2 + 1))
                wait("vector", "pe", tpe_ao)
                for half in range(2):
                    opi("vector", "dv", lambda e, h=half: e.tensor_tensor(
                        x1_sb[:, h * 512:(h + 1) * 512], x1_sb[:, h * 512:(h + 1) * 512],
                        ao_ps[h][:, :], OP.add))
                t_res = cnt["dv"]
                wait("tensor", "dv", t_res)
                opi("vector", "dv", lambda e: e.reduce_sum(
                    lnv_sb[:, 0:1], x1_sb[:, :], axis=AX))
                opi("vector", "dv", lambda e: e.tensor_scalar(
                    lnv_sb[:, 1:2], lnv_sb[:, 0:1], 1.0 / D, None, OP.mult))
                t_sq = opi("vector", "dv", lambda e: e.scalar_tensor_tensor(
                    sq_sb[:, :], x1_sb[:, :], 1.0, x1_sb[:, :], OP.mult, OP.mult,
                    accum_out=lnv_sb[:, 2:3]))
                opi("vector", "dv", lambda e: e.tensor_scalar(
                    lnv_sb[:, 3:4], lnv_sb[:, 2:3], 1.0 / D, EPS, OP.mult, OP.add))
                opi("vector", "dv", lambda e: e.tensor_tensor(
                    lnv_sb[:, 4:5], lnv_sb[:, 1:2], lnv_sb[:, 1:2], OP.mult))
                t_var = opi("vector", "dv", lambda e: e.tensor_sub(
                    lnv_sb[:, 3:4], lnv_sb[:, 3:4], lnv_sb[:, 4:5]))
                wait("scalar", "dv", t_var)
                opi("scalar", "ac", lambda e: e.activation(
                    lnv_sb[:, 5:6], lnv_sb[:, 3:4], AF.Ln, bias=0.0, scale=1.0))
                t_rs = opi("scalar", "ac", lambda e: e.activation(
                    lnv_sb[:, 6:7], lnv_sb[:, 5:6], AF.Exp, bias=0.0, scale=-0.5))
                wait("vector", "ac", t_rs)
                t_x1 = opi("vector", "dv", lambda e: e.tensor_scalar(
                    x1_sb[:, :], x1_sb[:, :], lnv_sb[:, 1:2], lnv_sb[:, 6:7],
                    OP.subtract, OP.mult))
                wait("scalar", "dv", t_x1)
                dma("scalar", "sxf", x1f[tb * 128:(tb + 1) * 128, :], x1_sb[:, :])
                if tb > 0:
                    wait("vector", "sxg", 16 * tb)
                t_x1b = opi("vector", "dv", lambda e: e.tensor_copy(
                    x1b_sb[:, :], x1_sb[:, :]))
                wait("scalar", "dv", t_x1b)
                dma("scalar", "sxg",
                    xg_i.rearrange("(t d) -> t d", d=D)[tb * 128:(tb + 1) * 128, :],
                    x1b_sb[:, :])
                wait("tensor", "dv", t_x1)
                for dc in range(8):
                    tpe_t = opi("tensor", "pe", lambda e,
                                s=x1_sb[:, dc * 128:(dc + 1) * 128]:
                                e.transpose(ms_ps[:, 0:128], s, ident_sb[:, :]))
                    wait("vector", "pe", tpe_t)
                    tdv_t = opi("vector", "dv", lambda e, d=x1t_sb[:, dc, :]:
                                e.tensor_copy(d, ms_ps[:, 0:128]))
                    wait("tensor", "dv", tdv_t)
                for dc in range(8):
                    fn = (lambda e, l=x1t_sb[:, dc, :], r=rw_sb[:, dc, :],
                          s=(dc == 0), t=(dc == 7):
                          e.matmul(ms_ps[:, 128:136], l, r, start=s, stop=t))
                    if dc < 7:
                        op0("tensor", fn)
                    else:
                        tpe_rt = opi("tensor", "pe", fn)
                t_rtr_last = tpe_rt
                wait("vector", "pe", tpe_rt)
                opi("vector", "dv", lambda e: e.reduce_max(
                    small[:, 12:13], ms_ps[:, 128:136], axis=AX, negate=True))
                t_nm = cnt["dv"]
                wait("scalar", "dv", t_nm)
                t_ex = opi("scalar", "ac", lambda e: e.activation(
                    small[:, 24:32], ms_ps[:, 128:136], AF.Exp, bias=small[:, 12:13],
                    scale=1.0, accum_out=small[:, 8:9]))
                wait("vector", "ac", t_ex)
                opi("vector", "dv", lambda e, tb_=tb: e.reciprocal(
                    gate_tm[:, tb_:tb_ + 1], small[:, 8:9]))
                opi("vector", "dv", lambda e: e.tensor_scalar(
                    small[:, 0:8], ms_ps[:, 128:136], small[:, 12:13], None, OP.add))
                opi("vector", "dv", lambda e: e.tensor_scalar(
                    small[:, 0:8], small[:, 0:8], 0.0, -BIGE, OP.is_ge, OP.mult))
                opi("vector", "dv", lambda e: e.tensor_tensor(
                    small[:, 0:8], small[:, 0:8], erowp_sb[:, :], OP.add))
                opi("vector", "dv", lambda e: e.tensor_reduce(
                    small[:, 9:10], small[:, 0:8], AX, OP.min))
                t_eq_prev = opi("vector", "dv", lambda e, tb_=tb: e.tensor_scalar(
                    emask_tm[:, tb_, :], erow_sb[:, :], small[:, 9:10], None,
                    OP.is_equal))

        mark("C")
        stC = ExitStack()
        emt = stC.enter_context(nc.sbuf_tensor("emt", [16, 8, 128], F32))
        posg = stC.enter_context(nc.sbuf_tensor("posg", [16, 8, 128], F32))
        racc = stC.enter_context(nc.sbuf_tensor("racc", [16, 4, 128], F32))
        rta_sb = stC.enter_context(nc.sbuf_tensor("rta_sb", [128, 2, 16], F32))
        vals_sb = stC.enter_context(nc.sbuf_tensor("vals_sb", [128, 16, 64], F32))
        zer_sb = stC.enter_context(nc.sbuf_tensor("zer_sb", [128, 2080], F32))
        lsta_sb = stC.enter_context(nc.sbuf_tensor("lsta_sb", [128, 16, 64], F32))

        wait("gpsimd", "sxg", 256)
        t_agx = opi("gpsimd", "cc", lambda e: e.collective_compute(
            "AllGather", OP.bypass, replica_groups=[list(range(8))],
            ins=[xg_i.ap().opt()], outs=[xg_o.ap().opt()])._wait_ge(sems["sxg"], 256))


        wait("tensor", "dv", t_eq_prev)
        for e8 in range(8):
            tpe = opi("tensor", "pe", lambda e, s=emask_tm[:, :, e8]:
                      e.transpose(ms_ps[0:16, 0:128], s, ident_sb[:, :]))
            wait("vector", "pe", tpe)
            tdv = opi("vector", "dv", lambda e, d=emt[:, e8, :]:
                      e.tensor_copy(d, ms_ps[0:16, 0:128]))
            wait("tensor", "dv", tdv)
        for e8 in range(8):
            opi("vector", "dv", lambda e, e_=e8: e.reduce_sum(
                rvec[0:16, e_:e_ + 1], emt[:, e_, :], axis=AX))
        t_tot = cnt["dv"]
        wait("tensor", "dv", t_tot)
        tpe_c = opi("tensor", "pe", lambda e: e.matmul(
            ms_ps[0:8, 136:137], rvec[0:16, 0:8], ones_sb[0:16, 0:1],
            start=True, stop=True))
        wait("vector", "pe", tpe_c)
        t_cv = opi("vector", "dv", lambda e: e.tensor_copy(
            small[0:8, 10:11], ms_ps[0:8, 136:137]))
        wait("scalar", "dv", t_cv)
        dma("scalar", "sct", cnt_i.rearrange("(a c) -> a c", c=1), small[0:8, 10:11])
        wait("gpsimd", "sct", 16)
        t_agc = opi("gpsimd", "cc", lambda e: e.collective_compute(
            "AllGather", OP.bypass, replica_groups=[list(range(8))],
            ins=[cnt_i.ap().opt()], outs=[cnt_o.ap().opt()])._wait_ge(sems["sct"], 16))
        wait("gpsimd", "cc", t_agc)
        mark("C1")
        t_gpc = dma("gpsimd", "gpc", cnts_ag[:, :], cnt_o.rearrange("(c e) -> c e", e=8))
        wait("gpsimd", "gpc", t_gpc)
        wait("tensor", "gpc", t_gpc)
        op0("tensor", lambda e: e.matmul(
            ms_ps[0:16, 144:152], rvec[0:16, 48:64], rvec[0:16, 0:8],
            start=True, stop=False))
        tpe_o = opi("tensor", "pe", lambda e: e.matmul(
            ms_ps[0:16, 144:152], meta_sb[0:8, 0:16], cnts_ag[:, :],
            start=False, stop=True))
        wait("vector", "pe", tpe_o)
        opi("vector", "dv", lambda e: e.tensor_copy(
            rvec[0:16, 8:16], ms_ps[0:16, 144:152]))
        opi("vector", "dv", lambda e: e.memset(racc[:, 3, :], 0.0))
        for e8 in range(8):
            opi("vector", "dv", lambda e, e_=e8: e.tensor_tensor_scan(
                posg[:, e_, :], emt[:, e_, :], racc[0:16, 3, :],
                rvec[0:16, 8 + e_:9 + e_], OP.add, OP.add))
        opi("vector", "dv", lambda e: e.memset(racc[:, 0, :], 0.0))
        opi("vector", "dv", lambda e: e.memset(racc[:, 1, :], 0.0))
        opi("vector", "dv", lambda e: e.memset(racc[:, 2, :], 0.0))
        t_fit = 0
        for e8 in range(8):
            opi("vector", "dv", lambda e, e_=e8: e.tensor_scalar(
                racc[:, 3, :], posg[:, e_, :], float(e_ * SEG - 1),
                float(e_ * SEG + SEG - 1), OP.add, OP.min))
            opi("vector", "dv", lambda e, e_=e8: e.tensor_tensor(
                racc[:, 3, :], racc[:, 3, :], emt[:, e_, :], OP.mult))
            opi("vector", "dv", lambda e: e.tensor_tensor(
                racc[:, 0, :], racc[:, 0, :], racc[:, 3, :], OP.add))
            opi("vector", "dv", lambda e, e_=e8: e.tensor_scalar(
                racc[:, 3, :], posg[:, e_, :], float(e_ * 2048 - 1),
                float(e_ * 2048 + 2047), OP.add, OP.min))
            opi("vector", "dv", lambda e, e_=e8: e.tensor_tensor(
                racc[:, 3, :], racc[:, 3, :], emt[:, e_, :], OP.mult))
            opi("vector", "dv", lambda e: e.tensor_tensor(
                racc[:, 1, :], racc[:, 1, :], racc[:, 3, :], OP.add))
            opi("vector", "dv", lambda e, e_=e8: e.tensor_scalar(
                racc[:, 3, :], posg[:, e_, :], 2048.0, None, OP.is_le))
            opi("vector", "dv", lambda e, e_=e8: e.tensor_tensor(
                racc[:, 3, :], racc[:, 3, :], emt[:, e_, :], OP.mult))
            t_fit = opi("vector", "dv", lambda e: e.tensor_tensor(
                racc[:, 2, :], racc[:, 2, :], racc[:, 3, :], OP.add))
        wait("scalar", "dv", t_fit)
        dma("scalar", "srt", rt[0:1, :].rearrange("x (p c) -> p x c", c=128),
            racc[:, 0:1, :])
        dma("scalar", "srt", rt[1:2, :].rearrange("x (p c) -> p x c", c=128),
            racc[:, 1:2, :])
        wait("sync", "srt", 32)
        mark("C2")
        dma("sync", "lrt", rta_sb[:, :, :],
            rt.rearrange("x (b g) -> b x g", g=16))
        wait("tensor", "lrt", 16)
        mark("C2a")
        for which, dst_i16 in ((0, posm_sb), (1, gidx_sb)):
            tpe = opi("tensor", "pe", lambda e, s=rta_sb[:, which, :]:
                      e.transpose(ms_ps[0:16, 0:128], s, ident_sb[:, :]))
            wait("vector", "pe", tpe)
            tdv = opi("vector", "dv", lambda e, d=dst_i16[0:16, :]:
                      e.tensor_copy(d, ms_ps[0:16, 0:128]))
            wait("tensor", "dv", tdv)
        mark("C2b")
        tpe_f = opi("tensor", "pe", lambda e: e.transpose(
            ms_ps[:, 384:400], racc[0:16, 2, :], ident_sb[0:16, 0:16]))
        wait("vector", "pe", tpe_f)
        t_ge = opi("vector", "dv", lambda e: e.tensor_tensor(
            gact_tm[:, :], gate_tm[:, :], ms_ps[:, 384:400], OP.mult))
        if KDEBUG:
            dbgs = {
                "dbg_posm": (posm_sb[0:16, :], I16, [16, 128]),
                "dbg_gidx": (gidx_sb[0:16, :], I16, [16, 128]),
                "dbg_xon": (None, None, None),
            }
            del dbgs["dbg_xon"]
            dbg_list = [
                ("dbg_posm", posm_sb[0:16, :], I16, [16, 128]),
                ("dbg_gidx", gidx_sb[0:16, :], I16, [16, 128]),
                ("dbg_racc", racc[:, :, :], F32, [16, 4, 128]),
                ("dbg_posg", posg[:, :, :], F32, [16, 8, 128]),
                ("dbg_emt", emt[:, :, :], F32, [16, 8, 128]),
                ("dbg_rvec", rvec[0:16, 0:16], F32, [16, 16]),
                ("dbg_cnts", cnts_ag[:, :], F32, [8, 8]),
                ("dbg_small", small[:, 8:10], F32, [128, 2]),
                ("dbg_gate", gate_tm[:, :], F32, [128, 16]),
                ("dbg_gact", gact_tm[:, :], F32, [128, 16]),
                ("dbg_emask", emask_tm[:, :, :], F32, [128, 16, 8]),
            ]
            wait("sync", "dv", cnt["dv"])
            wait("sync", "ac", cnt["ac"])
            for nm, ap_, dt_, shp in dbg_list:
                od = nc.dram_tensor(nm, shp, dt_, kind="ExternalOutput").ap()
                dma("sync", "lx1", od[tuple(slice(None) for _ in shp)], ap_)
            wait("sync", "lx1", cnt["lx1"])
        mark("C2c")
        wait("tensor", "dv", t_ge)
        tpe_g = opi("tensor", "pe", lambda e: e.matmul(
            ms_ps[0:8, 408:409], cnts_ag[:, :], ones_sb[0:8, 0:1],
            start=True, stop=True))
        wait("vector", "pe", tpe_g)
        t_gt = opi("vector", "dv", lambda e: e.tensor_copy(
            small[0:8, 13:14], ms_ps[0:8, 408:409]))
        wait("tensor", "dv", t_gt)
        mark("C2d")
        tpe_m = opi("tensor", "pe", lambda e: e.matmul(
            ms_ps[0:1, 400:408], small[0:8, 13:14], meta_sb[0:8, 16:24],
            start=True, stop=True))
        wait("vector", "pe", tpe_m)
        opi("vector", "dv", lambda e: e.tensor_copy(small[0:1, 8:16],
                                                    ms_ps[0:1, 400:408]))
        mark("C2e")
        opi("vector", "dv", lambda e: e.tensor_scalar(
            ccntf_sb[:, :], k512_sb[:, :], -1.0, small[0:1, 8:9], OP.mult, OP.add))
        opi("vector", "dv", lambda e: e.tensor_scalar_max(
            ccntf_sb[:, :], ccntf_sb[:, :], 0.0))
        opi("vector", "dv", lambda e: e.tensor_scalar_min(
            ccntf_sb[:, :], ccntf_sb[:, :], 512.0))
        t_ccnt = opi("vector", "dv", lambda e: e.tensor_copy(ccnt_sb[:, :],
                                                             ccntf_sb[:, :]))
        mark("C3")
        opi("vector", "dv", lambda e: e.memset(zer_sb[:, :], 0.0))
        opi("vector", "dv", lambda e: e.memset(vals_sb[:, :, :], 0.0))
        t_vals = opi("vector", "dv", lambda e: e.tensor_copy(vals_sb[:, :, 0],
                                                             ramp_sb[:, :]))
        wait("gpsimd", "dv", cnt["dv"])
        for qq in range(4):
            dma("gpsimd", "gpc",
                listtab.rearrange("t e -> (t e)").rearrange(
                    "(q p f) -> q p f", q=4, p=128)[qq, :, :],
                zer_sb[:, :])
        for rr in range(1, 8):
            dma("gpsimd", "gpc", posm_sb[16 * rr:16 * (rr + 1), :], posm_sb[0:16, :])
            dma("gpsimd", "gpc", gidx_sb[16 * rr:16 * (rr + 1), :], gidx_sb[0:16, :])
        wait("gpsimd", "gpc", cnt["gpc"])
        for kk in range(4):
            cnt["gsc"] += 16
            semh = sems["gsc"]
            emit("gpsimd", lambda e, k_=kk, sh=semh, w=16 * kk: e.dma_scatter_add(
                listtab.ap(), vals_sb[:, k_ * 4:(k_ + 1) * 4, :],
                posm_sb[:, k_ * 32:(k_ + 1) * 32], 512, 512, 64)
                .then_inc(sh, 16)._wait_ge(sh, w))
        wait("gpsimd", "gsc", 64)
        t_rs = opi("gpsimd", "cc", lambda e: e.collective_compute(
            "ReduceScatter", OP.add, replica_groups=[list(range(8))],
            ins=[listtab.ap().rearrange("t e -> (t e)").opt()],
            outs=[seg_t.ap().opt()])._wait_ge(sems["gsc"], 64))
        wait("gpsimd", "cc", t_rs)
        wait("gpsimd", "lb2", 32)
        t_lst = dma("gpsimd", "gpc", lsta_sb[:, :, :],
                    seg_t.rearrange("(b a e) -> b a e", a=16, e=64)[0:128, :, :])
        wait("gpsimd", "gpc", t_lst)
        wait("tensor", "gpc", t_lst)
        tpe_l = opi("tensor", "pe", lambda e: e.transpose(
            ms_ps[0:16, 0:128], lsta_sb[:, :, 0], ident_sb[:, :]))
        wait("vector", "pe", tpe_l)
        t_xi = opi("vector", "dv", lambda e: e.tensor_scalar(
            xidx_sb[0:16, :], ms_ps[0:16, 0:128], -1.0, None, OP.add))
        wait("gpsimd", "dv", t_xi)
        for rr in range(1, 8):
            dma("gpsimd", "gpc", xidx_sb[16 * rr:16 * (rr + 1), :], xidx_sb[0:16, :])
        t_xirep = cnt["gpc"]
        wait("gpsimd", "gpc", t_xirep)
        stC.close()
        stAB.close()

        mark("D")
        stD = ExitStack()
        w1_sb = stD.enter_context(nc.sbuf_tensor("w1_sb", [128, 8, F], BF16))
        w2_sb = stD.enter_context(nc.sbuf_tensor("w2_sb", [128, 32, D], BF16))
        xet_sb = stD.enter_context(nc.sbuf_tensor("xet_sb", [128, 4, 8, 512], BF16))
        h_sb = stD.enter_context(nc.sbuf_tensor("h_sb", [128, 32, 512], BF16))
        yt_sb = stD.enter_context(nc.sbuf_tensor("yt_sb", [128, 2, D], BF16))

        phase_barrier(["gpsimd", "vector", "scalar", "tensor", "sync"])
        dma("gpsimd", "gw1", w1_sb[:, :, :],
            w1_in.rearrange("(c p) f -> p c f", p=128))
        dma("gpsimd", "gw2", w2_sb[:, :, :],
            w2_in.rearrange("(c p) d -> p c d", p=128))
        t_xz = opi("vector", "dv", lambda e: e.memset(xet_sb[:, :, :, :], 0.0))
        wait("gpsimd", "cc", t_agx)
        wait("gpsimd", "dv", t_xz)
        for kk in range(4):
            gname = "gxg%d" % kk
            cnt[gname] += 16
            semh = sems[gname]

            def _mk_gather(e, k_=kk, sh=semh):
                cv = e.value_load(ccnt_sb[0:1, k_:k_ + 1], min_val=0, max_val=512)
                return e.dma_gather(
                    xet_sb[:, k_, :, :], xg_o.rearrange("(t d) -> t d", d=D),
                    xidx_sb[:, k_ * 32:(k_ + 1) * 32], 512, cv, D,
                    transpose=True).then_inc(sh, 16)
            emit("gpsimd", _mk_gather)

        wait("tensor", "gw1", 16)
        wait("tensor", "lb2", 32)
        t_yst = [0] * 16
        for kk in range(4):
            wait("tensor", "gxg%d" % kk, 16)
            t_hrelu = [0] * 32
            for fb in range(32):
                ps = sc_ps[fb % 2]
                if fb >= 2:
                    wait("tensor", "ac", t_hrelu[fb - 2])
                for dc in range(8):
                    fn = (lambda e, l=w1_sb[:, dc, fb * 128:(fb + 1) * 128],
                          r=xet_sb[:, kk, dc, :], p=ps, s=(dc == 0), t=(dc == 7):
                          e.matmul(p[:, :], l, r, start=s, stop=t))
                    if dc < 7:
                        op0("tensor", fn)
                    else:
                        tpe_h = opi("tensor", "pe", fn)
                wait("scalar", "pe", tpe_h)
                t_hrelu[fb] = opi("scalar", "ac", lambda e, d=h_sb[:, fb, :],
                                  p=ps, b=b1_sb[:, fb:fb + 1]: e.activation(
                                      d, p[:, :], AF.Relu, bias=b, scale=1.0))
            if kk == 0:
                wait("tensor", "gw2", 16)
            wait("tensor", "ac", t_hrelu[31])
            for sb4 in range(4):
                islot = kk * 4 + sb4
                if islot >= 2:
                    wait("tensor", "syi%d" % (islot - 2), 16)
                for half in range(2):
                    ps = ao_ps[half]
                    for fb in range(32):
                        op0("tensor", lambda e, l=h_sb[:, fb, sb4 * 128:(sb4 + 1) * 128],
                            r=w2_sb[:, fb, half * 512:(half + 1) * 512], p=ps,
                            s=(fb == 0), t=False:
                            e.matmul(p[:, :], l, r, start=s, stop=t))
                    tpe_y = opi("tensor", "pe",
                                lambda e, r=b2hl_sb[:, half * 512:(half + 1) * 512],
                                p=ps: e.matmul(p[:, :], onesb_sb[:, :], r,
                                               start=False, stop=True))
                wait("vector", "pe", tpe_y)
                tdv_y = 0
                for half in range(2):
                    tdv_y = opi("vector", "dv", lambda e,
                                d=yt_sb[:, islot % 2, half * 512:(half + 1) * 512],
                                p=ao_ps[half]: e.tensor_copy(d, p[:, :]))
                wait("tensor", "dv", tdv_y)
                wait("scalar", "dv", tdv_y)
                dma("scalar", "syi%d" % islot,
                    yg_i.rearrange("(t d) -> t d", d=D)[islot * 128:(islot + 1) * 128, :],
                    yt_sb[:, islot % 2, :])
        stD.close()

        mark("E")
        stE = ExitStack()
        ytok_sb = stE.enter_context(nc.sbuf_tensor("ytok_sb", [128, 16, D], BF16))
        x2_sb = stE.enter_context(nc.sbuf_tensor("x2_sb", [128, D], F32))
        sq2_sb = stE.enter_context(nc.sbuf_tensor("sq2_sb", [128, D], F32))

        for ii in range(16):
            wait("gpsimd", "syi%d" % ii, 16)
        t_gss = opi("gpsimd", "cc", lambda e: e.nop())
        t_agy = opi("gpsimd", "cc", lambda e: e.collective_compute(
            "AllGather", OP.bypass, replica_groups=[list(range(8))],
            ins=[yg_i.ap().opt()], outs=[yg_o.ap().opt()])._wait_ge(sems["cc"], t_gss))
        wait("gpsimd", "cc", t_agy)
        phase_barrier(["gpsimd", "vector", "scalar", "tensor", "sync"])
        for kk in range(4):
            gname = "gyg%d" % kk
            cnt[gname] += 16
            semh = sems[gname]
            emit("gpsimd", lambda e, k_=kk, sh=semh: e.dma_gather(
                ytok_sb[:, k_ * 4:(k_ + 1) * 4, :], yg_o.rearrange("(t d) -> t d", d=D),
                gidx_sb[:, k_ * 32:(k_ + 1) * 32], 512, 512, D).then_inc(sh, 16))

        for tb in range(16):
            if tb > 0:
                wait("sync", "soute" if (tb - 1) % 2 == 0 else "souto",
                     16 * ((tb - 1) // 2 + 1))
            t_x1r = dma("sync", "lx1", x2_sb[:, :], x1f[tb * 128:(tb + 1) * 128, :])
            wait("vector", "lx1", t_x1r)
            wait("vector", "gyg%d" % (tb // 4), 16)
            opi("vector", "dv", lambda e, tb_=tb: e.scalar_tensor_tensor(
                x2_sb[:, :], ytok_sb[:, tb_, :], gact_tm[:, tb_:tb_ + 1], x2_sb[:, :],
                OP.mult, OP.add))
            opi("vector", "dv", lambda e: e.reduce_sum(
                lnv_sb[:, 0:1], x2_sb[:, :], axis=AX))
            opi("vector", "dv", lambda e: e.tensor_scalar(
                lnv_sb[:, 1:2], lnv_sb[:, 0:1], 1.0 / D, None, OP.mult))
            opi("vector", "dv", lambda e: e.scalar_tensor_tensor(
                sq2_sb[:, :], x2_sb[:, :], 1.0, x2_sb[:, :], OP.mult, OP.mult,
                accum_out=lnv_sb[:, 2:3]))
            opi("vector", "dv", lambda e: e.tensor_scalar(
                lnv_sb[:, 3:4], lnv_sb[:, 2:3], 1.0 / D, EPS, OP.mult, OP.add))
            opi("vector", "dv", lambda e: e.tensor_tensor(
                lnv_sb[:, 4:5], lnv_sb[:, 1:2], lnv_sb[:, 1:2], OP.mult))
            t_var2 = opi("vector", "dv", lambda e: e.tensor_sub(
                lnv_sb[:, 3:4], lnv_sb[:, 3:4], lnv_sb[:, 4:5]))
            wait("scalar", "dv", t_var2)
            opi("scalar", "ac", lambda e: e.activation(
                lnv_sb[:, 5:6], lnv_sb[:, 3:4], AF.Ln, bias=0.0, scale=1.0))
            t_rs2 = opi("scalar", "ac", lambda e: e.activation(
                lnv_sb[:, 6:7], lnv_sb[:, 5:6], AF.Exp, bias=0.0, scale=-0.5))
            wait("vector", "ac", t_rs2)
            t_x2 = opi("vector", "dv", lambda e: e.tensor_scalar(
                x2_sb[:, :], x2_sb[:, :], lnv_sb[:, 1:2], lnv_sb[:, 6:7],
                OP.subtract, OP.mult))
            wait("scalar", "dv", t_x2)
            dma("scalar", "soute" if tb % 2 == 0 else "souto",
                out_t[tb * 128:(tb + 1) * 128, :], x2_sb[:, :])
        stE.close()

        wait("scalar", "soute", cnt["soute"])
        wait("scalar", "souto", cnt["souto"])
        for nm in ("gxg0", "gxg1", "gxg2", "gxg3", "gyg0", "gyg1", "gyg2", "gyg3",
                   "gw1", "gw2", "gpc", "gsc", "gpw", "gpwt"):
            wait("gpsimd", nm, cnt[nm])
        wait("gpsimd", "cc", cnt["cc"])
        for nm in ("ldc", "lx0e", "lx0o", "lb2", "lx1", "srt", "sxf", "sxg"):
            wait("sync", nm, cnt[nm])
        wait("scalar", "sx0e", cnt["sx0e"])
        wait("scalar", "sx0o", cnt["sx0o"])
        for ii in range(16):
            wait("scalar", "syi%d" % ii, cnt["syi%d" % ii])
        wait("scalar", "sct", cnt["sct"])

        if PHASE_LIMIT in marks:
            cut, csnap = marks[PHASE_LIMIT]
            for k in plan:
                del plan[k][cut[k]:]
            eng_for = {"soute": "scalar", "souto": "scalar", "sct": "scalar",
                       "srt": "scalar", "sxf": "scalar", "sxg": "scalar",
                       "sx0e": "scalar", "sx0o": "scalar",
                       "ldc": "sync", "lx0e": "sync", "lx0o": "sync",
                       "lb2": "sync", "lx1": "sync", "lrt": "sync",
                       "cc": "gpsimd"}
            for nm, v in csnap.items():
                if v > 0 and not nm.startswith(("pe", "dv", "ac")):
                    eng = eng_for.get(nm, "gpsimd")
                    if nm.startswith("syi"):
                        eng = "scalar"
                    semh = sems[nm]
                    plan[eng].append(lambda e, s=semh, vv=v: e.wait_ge(s, vv))
        blk = ex(nc.Block())

        @blk.tensor
        def _(e):
            for fn in plan["tensor"]:
                fn(e)

        @blk.vector
        def _(e):
            for fn in plan["vector"]:
                fn(e)

        @blk.scalar
        def _(e):
            for fn in plan["scalar"]:
                fn(e)

        @blk.gpsimd
        def _(e):
            for fn in plan["gpsimd"]:
                fn(e)

        @blk.sync
        def _(e):
            for fn in plan["sync"]:
                fn(e)

    nc.compile()
    return nc


_CACHE = {}


def kernel(L0_attn_x, L0_attn_W_Q, L0_attn_W_K, L0_attn_W_V, L0_attn_W_O,
           L0_ffn_router_w, L0_ffn_w1, L0_ffn_b1, L0_ffn_w2, L0_ffn_b2):
    x = np.ascontiguousarray(np.asarray(L0_attn_x, dtype=np.float32))
    wq = np.ascontiguousarray(np.asarray(L0_attn_W_Q, np.float32).reshape(D, D).T)
    wk = np.ascontiguousarray(np.asarray(L0_attn_W_K, np.float32).reshape(D, D).T)
    wv = np.ascontiguousarray(np.asarray(L0_attn_W_V, np.float32).reshape(D, D).T)
    wo = np.ascontiguousarray(
        np.asarray(L0_attn_W_O, np.float32).transpose(0, 2, 1).reshape(D, D))
    rw = np.ascontiguousarray(np.asarray(L0_ffn_router_w, np.float32))
    w1 = np.ascontiguousarray(np.asarray(L0_ffn_w1, np.float32))
    b1 = np.ascontiguousarray(np.asarray(L0_ffn_b1, np.float32))
    w2 = np.ascontiguousarray(np.asarray(L0_ffn_w2, np.float32))
    b2 = np.ascontiguousarray(np.asarray(L0_ffn_b2, np.float32))

    if "nc" not in _CACHE:
        _CACHE["nc"] = build_nc()
    nc = _CACHE["nc"]

    in_maps = []
    sidx = np.arange(S)
    for c in range(8):
        masklt = np.zeros((8, 16), np.float32)
        masklt[:c, :] = 1.0
        onehot = np.zeros((8, 8), np.float32)
        onehot[c, :] = 1.0
        ramp = np.zeros((128, 16), np.float32)
        ramp[sidx % 128, sidx // 128] = c * S + sidx + 1
        in_maps.append({
            "x_in": x[c], "wq_in": wq, "wk_in": wk, "wv_in": wv, "wo_in": wo,
            "rw_in": rw, "w1_in": w1[c], "b1_in": b1[c], "w2_in": w2[c],
            "b2_in": b2[c:c + 1], "masklt_in": masklt, "onehot_in": onehot,
            "ramp_in": ramp,
        })
    res = run_bass_kernel_spmd(nc, in_maps, core_ids=list(range(8)))
    return np.stack([res.results[c]["out"] for c in range(8)], axis=0)


# version 09
# speedup vs baseline: 1.0529x; 1.0529x over previous
import os as _os
import numpy as np
import concourse.bacc as bacc_mod
import concourse.mybir as mybir
from concourse.bass_utils import run_bass_kernel_spmd

F32 = mybir.dt.float32
BF16 = mybir.dt.bfloat16
I16 = mybir.dt.int16
I32 = mybir.dt.int32
OP = mybir.AluOpType
AF = mybir.ActivationFunctionType
AX = mybir.AxisListType.X

S = 2048
D = 1024
NH = 16
F = 4096
T = 16384
SEG = 2080
EPS = 1e-5
BIGE = 65536.0


PHASE_LIMIT = _os.environ.get("KPHASE", "Z")
KDEBUG = _os.environ.get("KDEBUG", "") == "1"


def build_nc():
    nc = bacc_mod.Bacc("TRN2", target_bir_lowering=False)

    x_in = nc.dram_tensor("x_in", [S, D], F32, kind="ExternalInput").ap()
    wq_in = nc.dram_tensor("wq_in", [D, D], F32, kind="ExternalInput").ap()
    wk_in = nc.dram_tensor("wk_in", [D, D], F32, kind="ExternalInput").ap()
    wv_in = nc.dram_tensor("wv_in", [D, D], F32, kind="ExternalInput").ap()
    wo_in = nc.dram_tensor("wo_in", [D, D], F32, kind="ExternalInput").ap()
    rw_in = nc.dram_tensor("rw_in", [D, 8], F32, kind="ExternalInput").ap()
    w1_in = nc.dram_tensor("w1_in", [D, F], F32, kind="ExternalInput").ap()
    b1_in = nc.dram_tensor("b1_in", [F], F32, kind="ExternalInput").ap()
    w2_in = nc.dram_tensor("w2_in", [F, D], F32, kind="ExternalInput").ap()
    b2_in = nc.dram_tensor("b2_in", [1, D], F32, kind="ExternalInput").ap()
    masklt_in = nc.dram_tensor("masklt_in", [8, 16], F32, kind="ExternalInput").ap()
    onehot_in = nc.dram_tensor("onehot_in", [8, 8], F32, kind="ExternalInput").ap()
    ramp_in = nc.dram_tensor("ramp_in", [128, 16], F32, kind="ExternalInput").ap()
    out_t = nc.dram_tensor("out", [S, D], F32, kind="ExternalOutput").ap()

    x1f = nc.dram_tensor("x1f", [S, D], F32)
    xg_i = nc.dram_tensor("xg_i", [S * D], BF16)
    xg_o = nc.dram_tensor("xg_o", [T * D], BF16)
    cnt_i = nc.dram_tensor("cnt_i", [8], F32)
    cnt_o = nc.dram_tensor("cnt_o", [64], F32)
    listtab = nc.dram_tensor("listtab", [8 * SEG, 64], F32)
    seg_t = nc.dram_tensor("seg_t", [SEG * 64], F32)
    rt = nc.dram_tensor("rt", [2, S], F32)
    yg_i = nc.dram_tensor("yg_i", [S * D], BF16)
    yg_o = nc.dram_tensor("yg_o", [T * D], BF16)

    ident = nc.inline_tensor(np.eye(128, dtype=np.float32), name="cident").ap()
    sl16_np = np.zeros((16, 16), np.float32)
    for k in range(16):
        sl16_np[k, k + 1:] = 1.0
    sl16_d = nc.inline_tensor(sl16_np, name="csl16").ap()
    masks_np = np.zeros((128, 4, 512), np.float32)
    for dd in range(4):
        for p in range(128):
            if p + dd * 128 < 512:
                masks_np[p, dd, p + dd * 128:] = 1.0
    masks_d = nc.inline_tensor(masks_np, name="cmasks").ap()
    erowp_d = nc.inline_tensor(
        np.tile(np.arange(8, dtype=np.float32)[None, :] + BIGE, (128, 1)),
        name="cerowp").ap()
    erow_d = nc.inline_tensor(
        np.tile(np.arange(8, dtype=np.float32)[None, :], (128, 1)), name="cerow").ap()
    k512_d = nc.inline_tensor(
        np.array([[0.0, 512.0, 1024.0, 1536.0, 0.0, 0.0, 0.0, 0.0]], np.float32),
        name="ck512").ap()

    from contextlib import ExitStack
    with ExitStack() as stk:
        ex = stk.enter_context

        ident_sb = ex(nc.sbuf_tensor("ident_sb", [128, 128], F32))
        ones_sb = ex(nc.sbuf_tensor("ones_sb", [128, 64], F32))
        onesb_sb = ex(nc.sbuf_tensor("onesb_sb", [2, 128], BF16))
        erow_sb = ex(nc.sbuf_tensor("erow_sb", [128, 8], F32))
        erowp_sb = ex(nc.sbuf_tensor("erowp_sb", [128, 8], F32))
        k512_sb = ex(nc.sbuf_tensor("k512_sb", [1, 8], F32))
        gate_tm = ex(nc.sbuf_tensor("gate_tm", [128, 16], F32))
        gact_tm = ex(nc.sbuf_tensor("gact_tm", [128, 16], F32))
        emask_tm = ex(nc.sbuf_tensor("emask_tm", [128, 16, 8], F32))
        small = ex(nc.sbuf_tensor("small", [128, 32], F32))
        lnv_sb = ex(nc.sbuf_tensor("lnv_sb", [128, 8], F32))
        posm_sb = ex(nc.sbuf_tensor("posm_sb", [128, 128], I16))
        gidx_sb = ex(nc.sbuf_tensor("gidx_sb", [128, 128], I16))
        xidx_sb = ex(nc.sbuf_tensor("xidx_sb", [128, 128], I16))
        ccnt_sb = ex(nc.sbuf_tensor("ccnt_sb", [1, 8], I32))
        ccntf_sb = ex(nc.sbuf_tensor("ccntf_sb", [1, 8], F32))
        b1_sb = ex(nc.sbuf_tensor("b1_sb", [128, 32], F32))
        b2hl_sb = ex(nc.sbuf_tensor("b2hl_sb", [2, D], BF16))
        meta_sb = ex(nc.sbuf_tensor("meta_sb", [8, 24], F32))
        rvec = ex(nc.sbuf_tensor("rvec", [16, 64], F32))
        ramp_sb = ex(nc.sbuf_tensor("ramp_sb", [128, 16], F32))
        cnts_ag = ex(nc.sbuf_tensor("cnts_ag", [8, 8], F32))

        sc_ps = [ex(nc.psum_tensor("sc_ps%d" % i, [128, 512], F32)) for i in range(2)]
        z_ps = [ex(nc.psum_tensor("z_ps%d" % i, [128, 512], F32)) for i in range(2)]
        b_ps = ex(nc.psum_tensor("b_ps", [64, 512], F32))
        ao_ps = [ex(nc.psum_tensor("ao_ps%d" % i, [128, 512], F32)) for i in range(2)]
        ms_ps = ex(nc.psum_tensor("ms_ps", [128, 512], F32))

        sem_names = ["syi%d" % i for i in range(16)] + ["ldc", "lx0e", "lx0o", "lb2", "lx1", "gpw", "gpwt", "gpc",
                     "gsc", "gw1", "gw2", "lrt",
                     "gxg0", "gxg1", "gxg2", "gxg3", "gyg0", "gyg1", "gyg2", "gyg3",
                     "sx0e", "sx0o", "sxf", "sxg", "sct", "srt",
                     "soute", "souto",
                     "pe", "dv", "ac", "cc"]
        sems = {n: ex(nc.semaphore(n)) for n in sem_names}
        cnt = {n: 0 for n in sem_names}
        plan = {k: [] for k in ("tensor", "vector", "scalar", "gpsimd", "sync")}

        def emit(eng, fn):
            plan[eng].append(fn)

        def dma(eng, sem, out_ap, in_ap, **kw):
            cnt[sem] += 16
            semh = sems[sem]
            emit(eng, lambda e, o=out_ap, i=in_ap: e.dma_start(out=o, in_=i, **kw)
                 .then_inc(semh, 16))
            return cnt[sem]

        def wait(eng, sem, val):
            semh = sems[sem]
            emit(eng, lambda e, v=val: e.wait_ge(semh, v))

        ENG_SEM = {"vector": "dv", "scalar": "ac"}

        def opi(eng, sem, fn):
            own = ENG_SEM.get(eng)
            if own is not None and cnt[own] > 0:
                semo = sems[own]
                emit(eng, lambda e, v=cnt[own]: e.wait_ge(semo, v))
            cnt[sem] += 1
            semh = sems[sem]
            emit(eng, lambda e, f=fn: f(e).then_inc(semh, 1))
            return cnt[sem]

        def op0(eng, fn):
            emit(eng, lambda e, f=fn: f(e))

        marks = {}

        def mark(name):
            marks[name] = ({k: len(v) for k, v in plan.items()}, dict(cnt))

        def phase_barrier(engines):
            snap = [(nm, cnt[nm]) for nm in sem_names if cnt[nm] > 0]
            for eng in engines:
                for nm, v in snap:
                    wait(eng, nm, v)

        stAB = ExitStack()
        v_tok = stAB.enter_context(nc.sbuf_tensor("v_tok", [128, 16, NH * 65], BF16))
        q_sb = stAB.enter_context(nc.sbuf_tensor("q_sb", [128, 8, S], BF16))
        k_sb = stAB.enter_context(nc.sbuf_tensor("k_sb", [128, 8, S], BF16))
        wo_sb = stAB.enter_context(nc.sbuf_tensor("wo_sb", [128, 8, D], BF16))
        masks_sb = stAB.enter_context(nc.sbuf_tensor("masks_sb", [128, 4, 512], BF16))
        rw_sb = stAB.enter_context(nc.sbuf_tensor("rw_sb", [128, 8, 8], F32))

        stA = ExitStack()
        wtmp_sb = stA.enter_context(nc.sbuf_tensor("wtmp_sb", [128, 8, D], BF16))
        x0t_sb = stA.enter_context(nc.sbuf_tensor("x0t_sb", [128, 8, S], BF16))
        x0_sb = stA.enter_context(nc.sbuf_tensor("x0_sb", [128, 2, D], F32))
        b2s_sb = stA.enter_context(nc.sbuf_tensor("b2s_sb", [1, 2 * D], F32))
        b1t_sb = stA.enter_context(nc.sbuf_tensor("b1t_sb", [32, 128], F32))
        b2w_sb = stA.enter_context(nc.sbuf_tensor("b2w_sb", [1, 2, D], BF16))

        dma("sync", "ldc", ident_sb[:, :], ident[:, :])
        dma("sync", "ldc", erow_sb[:, :], erow_d[:, :])
        dma("sync", "ldc", erowp_sb[:, :], erowp_d[:, :])
        dma("sync", "ldc", k512_sb[:, :], k512_d[:, :])
        dma("sync", "ldc", rw_sb[:, :, :], rw_in.rearrange("(c p) e -> p c e", p=128))
        dma("sync", "ldc", meta_sb[:, 0:16], masklt_in[:, :])
        dma("sync", "ldc", meta_sb[:, 16:24], onehot_in[:, :])
        dma("sync", "ldc", b1t_sb[:, :], b1_in.rearrange("(a p) -> a p", p=128))
        dma("sync", "ldc", b2s_sb[0:1, 0:D], b2_in[:, :])
        dma("sync", "ldc", rvec[0:16, 48:64], sl16_d[:, :])
        T_LDC = dma("sync", "ldc", ramp_sb[:, :], ramp_in[:, :])
        assert T_LDC == 176
        dma("gpsimd", "gpw", masks_sb[:, :, :], masks_d[:, :, :])
        T_GPW = dma("gpsimd", "gpw", wo_sb[:, :, :], wo_in.rearrange("(c p) e -> p c e", p=128))

        opi("vector", "dv", lambda e: e.memset(ones_sb[:, :], 1.0))
        opi("vector", "dv", lambda e: e.memset(onesb_sb[:, :], 1.0))
        t_vones = opi("vector", "dv", lambda e: e.memset(
            v_tok.ap().rearrange("p a (h c) -> p a h c", c=65)[:, :, :, 64:65], 1.0))
        wait("vector", "ldc", T_LDC)
        opi("vector", "dv", lambda e: e.tensor_copy(b2w_sb[0:1, 0, :], b2s_sb[0:1, 0:D]))
        opi("vector", "dv", lambda e: e.tensor_copy(b2s_sb[0:1, D:2 * D], b2w_sb[0:1, 0, :]))
        opi("vector", "dv", lambda e: e.tensor_sub(
            b2s_sb[0:1, D:2 * D], b2s_sb[0:1, 0:D], b2s_sb[0:1, D:2 * D]))
        t_b2pre = opi("vector", "dv", lambda e: e.tensor_copy(
            b2w_sb[0:1, 1, :], b2s_sb[0:1, D:2 * D]))
        wait("sync", "dv", t_b2pre)
        dma("sync", "lb2", b2hl_sb[0:1, :], b2w_sb[0:1, 0, :])
        t_b2l_dma = dma("sync", "lb2", b2hl_sb[1:2, :], b2w_sb[0:1, 1, :])

        wait("tensor", "ldc", T_LDC)
        tpe_b1 = opi("tensor", "pe", lambda e: e.transpose(
            ms_ps[:, 0:32], b1t_sb[:, :], ident_sb[0:32, 0:32]))
        wait("vector", "pe", tpe_b1)
        tdv_b1 = opi("vector", "dv", lambda e: e.tensor_copy(b1_sb[:, :], ms_ps[:, 0:32]))
        wait("tensor", "dv", tdv_b1)
        t_xtr_done = [0] * 16
        for tb in range(16):
            if tb >= 2:
                wait("sync", "dv", t_xtr_done[tb - 2])
            t_ld = dma("sync", "lx0e" if tb % 2 == 0 else "lx0o",
                       x0_sb[:, tb % 2, :], x_in[tb * 128:(tb + 1) * 128, :])
            wait("tensor", "lx0e" if tb % 2 == 0 else "lx0o", 16 * (tb // 2 + 1))
            for dc in range(8):
                tpe = opi("tensor", "pe", lambda e, s=x0_sb[:, tb % 2, dc * 128:(dc + 1) * 128]:
                          e.transpose(ms_ps[:, 0:128], s, ident_sb[:, :]))
                wait("vector", "pe", tpe)
                tdv = opi("vector", "dv", lambda e, d=x0t_sb[:, dc, tb * 128:(tb + 1) * 128]:
                          e.tensor_copy(d, ms_ps[:, 0:128]))
                wait("tensor", "dv", tdv)
            t_xtr_done[tb] = cnt["dv"]

        dma("gpsimd", "gpwt", wtmp_sb[:, :, :],
            wv_in.rearrange("(c p) a -> p c a", p=128))
        wait("tensor", "gpwt", 16)
        wait("tensor", "dv", t_vones)
        last_evac = {0: 0, 1: 0}
        for tb in range(16):
            for half in range(2):
                ps = sc_ps[half]
                if last_evac[half]:
                    wait("tensor", "dv", last_evac[half])
                for dc in range(8):
                    fn = (lambda e, l=x0t_sb[:, dc, tb * 128:(tb + 1) * 128],
                          r=wtmp_sb[:, dc, half * 512:(half + 1) * 512], p=ps,
                          s=(dc == 0), t=(dc == 7):
                          e.matmul(p[:, :], l, r, start=s, stop=t))
                    if dc < 7:
                        op0("tensor", fn)
                    else:
                        tpe = opi("tensor", "pe", fn)
                dst = v_tok.ap().rearrange("p a (h c) -> p a h c", c=65)[
                    :, tb, half * 8:(half + 1) * 8, 0:64]
                wait("vector", "pe", tpe)
                last_evac[half] = opi("vector", "dv",
                                      lambda e, d=dst, p=ps: e.tensor_copy(d, p[:, :]))
        for wi, (w_dram, dsts) in enumerate(((wq_in, q_sb), (wk_in, k_sb))):
            wait("gpsimd", "dv", cnt["dv"])
            dma("gpsimd", "gpwt", wtmp_sb[:, :, :],
                w_dram.rearrange("(c p) a -> p c a", p=128))
            wait("tensor", "gpwt", 16 * (wi + 2))
            for ac in range(8):
                for tt in range(4):
                    ps = sc_ps[tt % 2]
                    if last_evac[tt % 2]:
                        wait("tensor", "dv", last_evac[tt % 2])
                    for dc in range(8):
                        fn = (lambda e, l=wtmp_sb[:, dc, ac * 128:(ac + 1) * 128],
                              r=x0t_sb[:, dc, tt * 512:(tt + 1) * 512], p=ps,
                              s=(dc == 0), t=(dc == 7):
                              e.matmul(p[:, :], l, r, start=s, stop=t))
                        if dc < 7:
                            op0("tensor", fn)
                        else:
                            tpe = opi("tensor", "pe", fn)
                    wait("vector", "pe", tpe)
                    last_evac[tt % 2] = opi(
                        "vector", "dv", lambda e, d=dsts[:, ac, tt * 512:(tt + 1) * 512],
                        p=ps: e.tensor_copy(d, p[:, :]))
        stA.close()

        mark("B")
        strip = stAB.enter_context(nc.sbuf_tensor("strip", [128, 16, 512], BF16))
        z_sb = stAB.enter_context(nc.sbuf_tensor("z_sb", [128, 8, 512], BF16))
        recip_sb = stAB.enter_context(nc.sbuf_tensor("recip_sb", [1, 512], F32))
        bsb_sb = stAB.enter_context(nc.sbuf_tensor("bsb_sb", [64, 512], F32))
        x1_sb = stAB.enter_context(nc.sbuf_tensor("x1_sb", [128, D], F32))
        x1b_sb = stAB.enter_context(nc.sbuf_tensor("x1b_sb", [128, D], BF16))
        x1t_sb = stAB.enter_context(nc.sbuf_tensor("x1t_sb", [128, 8, 128], F32))
        sq_sb = stAB.enter_context(nc.sbuf_tensor("sq_sb", [128, D], F32))

        wait("vector", "lb2", 32)
        wait("scalar", "lb2", 32)
        wait("tensor", "gpw", T_GPW)
        wait("tensor", "dv", max(last_evac[0], last_evac[1]))
        t_eq_prev = 0
        t_zn_par = {0: 0, 1: 0}
        t_zn_prev = 0
        for j in range(4):
            nch = 4 * j + 4
            for a in range(NH):
                par = (j * 16 + a) % 2
                rowb = 64 * (a % 2)
                ahc = a // 2
                t_exp = [0] * nch
                wait("tensor", "ac", cnt["ac"])
                for i in range(nch):
                    ps = sc_ps[i % 2]
                    if i >= 2:
                        wait("tensor", "ac", t_exp[i - 2])
                    tpe = opi("tensor", "pe", lambda e,
                              l=q_sb[rowb:rowb + 64, ahc, i * 128:(i + 1) * 128],
                              r=k_sb[rowb:rowb + 64, ahc, j * 512:(j + 1) * 512],
                              p=ps: e.matmul(p[:, :], l, r, start=True, stop=True))
                    wait("scalar", "pe", tpe)
                    t_exp[i] = opi("scalar", "ac", lambda e, d=strip[:, i, :], p=ps:
                                   e.activation(d, p[:, :], AF.Exp, bias=0.0, scale=0.125))
                wait("vector", "ac", t_exp[nch - 1])
                for dd in range(4):
                    opi("vector", "dv", lambda e, s=strip[:, 4 * j + dd, :],
                        m=masks_sb[:, dd, :]: e.tensor_tensor(s, s, m, OP.mult))
                t_mask = cnt["dv"]
                zp = z_ps[par]
                wait("tensor", "dv", t_mask)
                if t_zn_par[par]:
                    wait("tensor", "dv", t_zn_par[par])
                for i in range(nch):
                    fn = (lambda e, l=v_tok[:, i, a * 65:(a + 1) * 65],
                          r=strip[:, i, :], p=zp, s=(i == 0), t=(i == nch - 1):
                          e.matmul(p[0:65, :], l, r, start=s, stop=t))
                    if i < nch - 1:
                        op0("tensor", fn)
                    else:
                        tpe_z = opi("tensor", "pe", fn)
                wait("vector", "pe", tpe_z)
                t_rc = opi("vector", "dv", lambda e, p=zp:
                           e.reciprocal(recip_sb[0:1, :], p[64:65, :]))
                if t_zn_prev:
                    wait("tensor", "dv", t_zn_prev)
                wait("tensor", "dv", t_rc)
                tpe_b = opi("tensor", "pe", lambda e: e.matmul(
                    b_ps[:, :], ones_sb[0:1, 0:64], recip_sb[0:1, :],
                    start=True, stop=True))
                wait("vector", "pe", tpe_b)
                opi("vector", "dv", lambda e: e.tensor_copy(bsb_sb[:, :], b_ps[:, :]))
                t_zn = opi("vector", "dv", lambda e, d=z_sb[rowb:rowb + 64, ahc, :],
                           p=zp: e.tensor_tensor(d, p[0:65 - 1, :], bsb_sb[:, :], OP.mult))
                t_zn_par[par] = t_zn
                t_zn_prev = t_zn

            wait("tensor", "gpw", T_GPW)
            wait("tensor", "dv", t_zn_prev)
            for s4 in range(4):
                tb = j * 4 + s4
                for half in range(2):
                    ps = ao_ps[half]
                    for ahc in range(8):
                        fn = (lambda e, l=z_sb[:, ahc, s4 * 128:(s4 + 1) * 128],
                              r=wo_sb[:, ahc, half * 512:(half + 1) * 512], p=ps,
                              s=(ahc == 0), t=(ahc == 7):
                              e.matmul(p[:, :], l, r, start=s, stop=t))
                        if ahc < 7:
                            op0("tensor", fn)
                        else:
                            tpe_ao = opi("tensor", "pe", fn)
                if tb > 0:
                    wait("scalar", "pe", t_rtr_last)
                    wait("scalar", "sxf", 16 * tb)
                sx0 = "sx0e" if tb % 2 == 0 else "sx0o"
                t_x0r = dma("scalar", sx0, x1_sb[:, :], x_in[tb * 128:(tb + 1) * 128, :])
                wait("vector", sx0, 16 * (tb // 2 + 1))
                wait("vector", "pe", tpe_ao)
                for half in range(2):
                    opi("vector", "dv", lambda e, h=half: e.tensor_tensor(
                        x1_sb[:, h * 512:(h + 1) * 512], x1_sb[:, h * 512:(h + 1) * 512],
                        ao_ps[h][:, :], OP.add))
                t_res = cnt["dv"]
                wait("tensor", "dv", t_res)
                opi("vector", "dv", lambda e: e.reduce_sum(
                    lnv_sb[:, 0:1], x1_sb[:, :], axis=AX))
                opi("vector", "dv", lambda e: e.tensor_scalar(
                    lnv_sb[:, 1:2], lnv_sb[:, 0:1], 1.0 / D, None, OP.mult))
                t_sq = opi("vector", "dv", lambda e: e.scalar_tensor_tensor(
                    sq_sb[:, :], x1_sb[:, :], 1.0, x1_sb[:, :], OP.mult, OP.mult,
                    accum_out=lnv_sb[:, 2:3]))
                opi("vector", "dv", lambda e: e.tensor_scalar(
                    lnv_sb[:, 3:4], lnv_sb[:, 2:3], 1.0 / D, EPS, OP.mult, OP.add))
                opi("vector", "dv", lambda e: e.tensor_tensor(
                    lnv_sb[:, 4:5], lnv_sb[:, 1:2], lnv_sb[:, 1:2], OP.mult))
                t_var = opi("vector", "dv", lambda e: e.tensor_sub(
                    lnv_sb[:, 3:4], lnv_sb[:, 3:4], lnv_sb[:, 4:5]))
                wait("scalar", "dv", t_var)
                opi("scalar", "ac", lambda e: e.activation(
                    lnv_sb[:, 5:6], lnv_sb[:, 3:4], AF.Ln, bias=0.0, scale=1.0))
                t_rs = opi("scalar", "ac", lambda e: e.activation(
                    lnv_sb[:, 6:7], lnv_sb[:, 5:6], AF.Exp, bias=0.0, scale=-0.5))
                wait("vector", "ac", t_rs)
                t_x1 = opi("vector", "dv", lambda e: e.tensor_scalar(
                    x1_sb[:, :], x1_sb[:, :], lnv_sb[:, 1:2], lnv_sb[:, 6:7],
                    OP.subtract, OP.mult))
                wait("scalar", "dv", t_x1)
                dma("scalar", "sxf", x1f[tb * 128:(tb + 1) * 128, :], x1_sb[:, :])
                if tb > 0:
                    wait("vector", "sxg", 16 * tb)
                t_x1b = opi("vector", "dv", lambda e: e.tensor_copy(
                    x1b_sb[:, :], x1_sb[:, :]))
                wait("scalar", "dv", t_x1b)
                dma("scalar", "sxg",
                    xg_i.rearrange("(t d) -> t d", d=D)[tb * 128:(tb + 1) * 128, :],
                    x1b_sb[:, :])
                wait("tensor", "dv", t_x1)
                for dc in range(8):
                    tpe_t = opi("tensor", "pe", lambda e,
                                s=x1_sb[:, dc * 128:(dc + 1) * 128]:
                                e.transpose(ms_ps[:, 0:128], s, ident_sb[:, :]))
                    wait("vector", "pe", tpe_t)
                    tdv_t = opi("vector", "dv", lambda e, d=x1t_sb[:, dc, :]:
                                e.tensor_copy(d, ms_ps[:, 0:128]))
                    wait("tensor", "dv", tdv_t)
                for dc in range(8):
                    fn = (lambda e, l=x1t_sb[:, dc, :], r=rw_sb[:, dc, :],
                          s=(dc == 0), t=(dc == 7):
                          e.matmul(ms_ps[:, 128:136], l, r, start=s, stop=t))
                    if dc < 7:
                        op0("tensor", fn)
                    else:
                        tpe_rt = opi("tensor", "pe", fn)
                t_rtr_last = tpe_rt
                wait("vector", "pe", tpe_rt)
                opi("vector", "dv", lambda e: e.reduce_max(
                    small[:, 12:13], ms_ps[:, 128:136], axis=AX, negate=True))
                t_nm = cnt["dv"]
                wait("scalar", "dv", t_nm)
                t_ex = opi("scalar", "ac", lambda e: e.activation(
                    small[:, 24:32], ms_ps[:, 128:136], AF.Exp, bias=small[:, 12:13],
                    scale=1.0, accum_out=small[:, 8:9]))
                wait("vector", "ac", t_ex)
                opi("vector", "dv", lambda e, tb_=tb: e.reciprocal(
                    gate_tm[:, tb_:tb_ + 1], small[:, 8:9]))
                opi("vector", "dv", lambda e: e.tensor_scalar(
                    small[:, 0:8], ms_ps[:, 128:136], small[:, 12:13], None, OP.add))
                opi("vector", "dv", lambda e: e.tensor_scalar(
                    small[:, 0:8], small[:, 0:8], 0.0, -BIGE, OP.is_ge, OP.mult))
                opi("vector", "dv", lambda e: e.tensor_tensor(
                    small[:, 0:8], small[:, 0:8], erowp_sb[:, :], OP.add))
                opi("vector", "dv", lambda e: e.tensor_reduce(
                    small[:, 9:10], small[:, 0:8], AX, OP.min))
                t_eq_prev = opi("vector", "dv", lambda e, tb_=tb: e.tensor_scalar(
                    emask_tm[:, tb_, :], erow_sb[:, :], small[:, 9:10], None,
                    OP.is_equal))

        mark("C")
        stC = ExitStack()
        emt = stC.enter_context(nc.sbuf_tensor("emt", [16, 8, 128], F32))
        posg = stC.enter_context(nc.sbuf_tensor("posg", [16, 8, 128], F32))
        racc = stC.enter_context(nc.sbuf_tensor("racc", [16, 4, 128], F32))
        rta_sb = stC.enter_context(nc.sbuf_tensor("rta_sb", [128, 2, 16], F32))
        vals_sb = stC.enter_context(nc.sbuf_tensor("vals_sb", [128, 16, 64], F32))
        zer_sb = stC.enter_context(nc.sbuf_tensor("zer_sb", [128, 2080], F32))
        lsta_sb = stC.enter_context(nc.sbuf_tensor("lsta_sb", [128, 16, 64], F32))

        wait("gpsimd", "sxg", 256)
        t_agx = opi("gpsimd", "cc", lambda e: e.collective_compute(
            "AllGather", OP.bypass, replica_groups=[list(range(8))],
            ins=[xg_i.ap().opt()], outs=[xg_o.ap().opt()])._wait_ge(sems["sxg"], 256))


        wait("tensor", "dv", t_eq_prev)
        for e8 in range(8):
            tpe = opi("tensor", "pe", lambda e, s=emask_tm[:, :, e8]:
                      e.transpose(ms_ps[0:16, 0:128], s, ident_sb[:, :]))
            wait("vector", "pe", tpe)
            tdv = opi("vector", "dv", lambda e, d=emt[:, e8, :]:
                      e.tensor_copy(d, ms_ps[0:16, 0:128]))
            wait("tensor", "dv", tdv)
        for e8 in range(8):
            opi("vector", "dv", lambda e, e_=e8: e.reduce_sum(
                rvec[0:16, e_:e_ + 1], emt[:, e_, :], axis=AX))
        t_tot = cnt["dv"]
        wait("tensor", "dv", t_tot)
        tpe_c = opi("tensor", "pe", lambda e: e.matmul(
            ms_ps[0:8, 136:137], rvec[0:16, 0:8], ones_sb[0:16, 0:1],
            start=True, stop=True))
        wait("vector", "pe", tpe_c)
        t_cv = opi("vector", "dv", lambda e: e.tensor_copy(
            small[0:8, 10:11], ms_ps[0:8, 136:137]))
        wait("scalar", "dv", t_cv)
        dma("scalar", "sct", cnt_i.rearrange("(a c) -> a c", c=1), small[0:8, 10:11])
        wait("gpsimd", "sct", 16)
        t_agc = opi("gpsimd", "cc", lambda e: e.collective_compute(
            "AllGather", OP.bypass, replica_groups=[list(range(8))],
            ins=[cnt_i.ap().opt()], outs=[cnt_o.ap().opt()])._wait_ge(sems["sct"], 16))
        wait("gpsimd", "cc", t_agc)
        mark("C1")
        t_gpc = dma("gpsimd", "gpc", cnts_ag[:, :], cnt_o.rearrange("(c e) -> c e", e=8))
        wait("gpsimd", "gpc", t_gpc)
        wait("tensor", "gpc", t_gpc)
        op0("tensor", lambda e: e.matmul(
            ms_ps[0:16, 144:152], rvec[0:16, 48:64], rvec[0:16, 0:8],
            start=True, stop=False))
        tpe_o = opi("tensor", "pe", lambda e: e.matmul(
            ms_ps[0:16, 144:152], meta_sb[0:8, 0:16], cnts_ag[:, :],
            start=False, stop=True))
        wait("vector", "pe", tpe_o)
        opi("vector", "dv", lambda e: e.tensor_copy(
            rvec[0:16, 8:16], ms_ps[0:16, 144:152]))
        opi("vector", "dv", lambda e: e.memset(racc[:, 3, :], 0.0))
        for e8 in range(8):
            opi("vector", "dv", lambda e, e_=e8: e.tensor_tensor_scan(
                posg[:, e_, :], emt[:, e_, :], racc[0:16, 3, :],
                rvec[0:16, 8 + e_:9 + e_], OP.add, OP.add))
        opi("vector", "dv", lambda e: e.memset(racc[:, 0, :], 0.0))
        opi("vector", "dv", lambda e: e.memset(racc[:, 1, :], 0.0))
        opi("vector", "dv", lambda e: e.memset(racc[:, 2, :], 0.0))
        t_fit = 0
        for e8 in range(8):
            opi("vector", "dv", lambda e, e_=e8: e.tensor_scalar(
                racc[:, 3, :], posg[:, e_, :], float(e_ * SEG - 1),
                float(e_ * SEG + SEG - 1), OP.add, OP.min))
            opi("vector", "dv", lambda e, e_=e8: e.tensor_tensor(
                racc[:, 3, :], racc[:, 3, :], emt[:, e_, :], OP.mult))
            opi("vector", "dv", lambda e: e.tensor_tensor(
                racc[:, 0, :], racc[:, 0, :], racc[:, 3, :], OP.add))
            opi("vector", "dv", lambda e, e_=e8: e.tensor_scalar(
                racc[:, 3, :], posg[:, e_, :], float(e_ * 2048 - 1),
                float(e_ * 2048 + 2047), OP.add, OP.min))
            opi("vector", "dv", lambda e, e_=e8: e.tensor_tensor(
                racc[:, 3, :], racc[:, 3, :], emt[:, e_, :], OP.mult))
            opi("vector", "dv", lambda e: e.tensor_tensor(
                racc[:, 1, :], racc[:, 1, :], racc[:, 3, :], OP.add))
            opi("vector", "dv", lambda e, e_=e8: e.tensor_scalar(
                racc[:, 3, :], posg[:, e_, :], 2048.0, None, OP.is_le))
            opi("vector", "dv", lambda e, e_=e8: e.tensor_tensor(
                racc[:, 3, :], racc[:, 3, :], emt[:, e_, :], OP.mult))
            t_fit = opi("vector", "dv", lambda e: e.tensor_tensor(
                racc[:, 2, :], racc[:, 2, :], racc[:, 3, :], OP.add))
        wait("scalar", "dv", t_fit)
        dma("scalar", "srt", rt[0:1, :].rearrange("x (p c) -> p x c", c=128),
            racc[:, 0:1, :])
        dma("scalar", "srt", rt[1:2, :].rearrange("x (p c) -> p x c", c=128),
            racc[:, 1:2, :])
        wait("sync", "srt", 32)
        mark("C2")
        dma("sync", "lrt", rta_sb[:, :, :],
            rt.rearrange("x (b g) -> b x g", g=16))
        wait("tensor", "lrt", 16)
        mark("C2a")
        for which, dst_i16 in ((0, posm_sb), (1, gidx_sb)):
            tpe = opi("tensor", "pe", lambda e, s=rta_sb[:, which, :]:
                      e.transpose(ms_ps[0:16, 0:128], s, ident_sb[:, :]))
            wait("vector", "pe", tpe)
            tdv = opi("vector", "dv", lambda e, d=dst_i16[0:16, :]:
                      e.tensor_copy(d, ms_ps[0:16, 0:128]))
            wait("tensor", "dv", tdv)
        mark("C2b")
        tpe_f = opi("tensor", "pe", lambda e: e.transpose(
            ms_ps[:, 384:400], racc[0:16, 2, :], ident_sb[0:16, 0:16]))
        wait("vector", "pe", tpe_f)
        t_ge = opi("vector", "dv", lambda e: e.tensor_tensor(
            gact_tm[:, :], gate_tm[:, :], ms_ps[:, 384:400], OP.mult))
        if KDEBUG:
            dbgs = {
                "dbg_posm": (posm_sb[0:16, :], I16, [16, 128]),
                "dbg_gidx": (gidx_sb[0:16, :], I16, [16, 128]),
                "dbg_xon": (None, None, None),
            }
            del dbgs["dbg_xon"]
            dbg_list = [
                ("dbg_posm", posm_sb[0:16, :], I16, [16, 128]),
                ("dbg_gidx", gidx_sb[0:16, :], I16, [16, 128]),
                ("dbg_racc", racc[:, :, :], F32, [16, 4, 128]),
                ("dbg_posg", posg[:, :, :], F32, [16, 8, 128]),
                ("dbg_emt", emt[:, :, :], F32, [16, 8, 128]),
                ("dbg_rvec", rvec[0:16, 0:16], F32, [16, 16]),
                ("dbg_cnts", cnts_ag[:, :], F32, [8, 8]),
                ("dbg_small", small[:, 8:10], F32, [128, 2]),
                ("dbg_gate", gate_tm[:, :], F32, [128, 16]),
                ("dbg_gact", gact_tm[:, :], F32, [128, 16]),
                ("dbg_emask", emask_tm[:, :, :], F32, [128, 16, 8]),
            ]
            wait("sync", "dv", cnt["dv"])
            wait("sync", "ac", cnt["ac"])
            for nm, ap_, dt_, shp in dbg_list:
                od = nc.dram_tensor(nm, shp, dt_, kind="ExternalOutput").ap()
                dma("sync", "lx1", od[tuple(slice(None) for _ in shp)], ap_)
            wait("sync", "lx1", cnt["lx1"])
        mark("C2c")
        wait("tensor", "dv", t_ge)
        tpe_g = opi("tensor", "pe", lambda e: e.matmul(
            ms_ps[0:8, 408:409], cnts_ag[:, :], ones_sb[0:8, 0:1],
            start=True, stop=True))
        wait("vector", "pe", tpe_g)
        t_gt = opi("vector", "dv", lambda e: e.tensor_copy(
            small[0:8, 13:14], ms_ps[0:8, 408:409]))
        wait("tensor", "dv", t_gt)
        mark("C2d")
        tpe_m = opi("tensor", "pe", lambda e: e.matmul(
            ms_ps[0:1, 400:408], small[0:8, 13:14], meta_sb[0:8, 16:24],
            start=True, stop=True))
        wait("vector", "pe", tpe_m)
        opi("vector", "dv", lambda e: e.tensor_copy(small[0:1, 8:16],
                                                    ms_ps[0:1, 400:408]))
        mark("C2e")
        opi("vector", "dv", lambda e: e.tensor_scalar(
            ccntf_sb[:, :], k512_sb[:, :], -1.0, small[0:1, 8:9], OP.mult, OP.add))
        opi("vector", "dv", lambda e: e.tensor_scalar_max(
            ccntf_sb[:, :], ccntf_sb[:, :], 0.0))
        opi("vector", "dv", lambda e: e.tensor_scalar_min(
            ccntf_sb[:, :], ccntf_sb[:, :], 512.0))
        t_ccnt = opi("vector", "dv", lambda e: e.tensor_copy(ccnt_sb[:, :],
                                                             ccntf_sb[:, :]))
        mark("C3")
        opi("vector", "dv", lambda e: e.memset(zer_sb[:, :], 0.0))
        opi("vector", "dv", lambda e: e.memset(vals_sb[:, :, :], 0.0))
        t_vals = opi("vector", "dv", lambda e: e.tensor_copy(vals_sb[:, :, 0],
                                                             ramp_sb[:, :]))
        wait("gpsimd", "dv", cnt["dv"])
        for qq in range(4):
            dma("gpsimd", "gpc",
                listtab.rearrange("t e -> (t e)").rearrange(
                    "(q p f) -> q p f", q=4, p=128)[qq, :, :],
                zer_sb[:, :])
        for rr in range(1, 8):
            dma("gpsimd", "gpc", posm_sb[16 * rr:16 * (rr + 1), :], posm_sb[0:16, :])
            dma("gpsimd", "gpc", gidx_sb[16 * rr:16 * (rr + 1), :], gidx_sb[0:16, :])
        wait("gpsimd", "gpc", cnt["gpc"])
        for kk in range(4):
            cnt["gsc"] += 16
            semh = sems["gsc"]
            emit("gpsimd", lambda e, k_=kk, sh=semh, w=16 * kk: e.dma_scatter_add(
                listtab.ap(), vals_sb[:, k_ * 4:(k_ + 1) * 4, :],
                posm_sb[:, k_ * 32:(k_ + 1) * 32], 512, 512, 64)
                .then_inc(sh, 16)._wait_ge(sh, w))
        wait("gpsimd", "gsc", 64)
        t_rs = opi("gpsimd", "cc", lambda e: e.collective_compute(
            "ReduceScatter", OP.add, replica_groups=[list(range(8))],
            ins=[listtab.ap().rearrange("t e -> (t e)").opt()],
            outs=[seg_t.ap().opt()])._wait_ge(sems["gsc"], 64))
        wait("gpsimd", "cc", t_rs)
        wait("gpsimd", "lb2", 32)
        t_lst = dma("gpsimd", "gpc", lsta_sb[:, :, :],
                    seg_t.rearrange("(b a e) -> b a e", a=16, e=64)[0:128, :, :])
        wait("gpsimd", "gpc", t_lst)
        wait("tensor", "gpc", t_lst)
        tpe_l = opi("tensor", "pe", lambda e: e.transpose(
            ms_ps[0:16, 0:128], lsta_sb[:, :, 0], ident_sb[:, :]))
        wait("vector", "pe", tpe_l)
        t_xi = opi("vector", "dv", lambda e: e.tensor_scalar(
            xidx_sb[0:16, :], ms_ps[0:16, 0:128], -1.0, None, OP.add))
        wait("gpsimd", "dv", t_xi)
        for rr in range(1, 8):
            dma("gpsimd", "gpc", xidx_sb[16 * rr:16 * (rr + 1), :], xidx_sb[0:16, :])
        t_xirep = cnt["gpc"]
        wait("gpsimd", "gpc", t_xirep)
        stC.close()
        stAB.close()

        mark("D")
        stD = ExitStack()
        w1_sb = stD.enter_context(nc.sbuf_tensor("w1_sb", [128, 8, F], BF16))
        w2_sb = stD.enter_context(nc.sbuf_tensor("w2_sb", [128, 32, D], BF16))
        xet_sb = stD.enter_context(nc.sbuf_tensor("xet_sb", [128, 4, 8, 512], BF16))
        h_sb = stD.enter_context(nc.sbuf_tensor("h_sb", [128, 32, 512], BF16))
        yt_sb = stD.enter_context(nc.sbuf_tensor("yt_sb", [128, 2, D], BF16))

        phase_barrier(["gpsimd", "vector", "scalar", "tensor", "sync"])
        dma("gpsimd", "gw1", w1_sb[:, :, :],
            w1_in.rearrange("(c p) f -> p c f", p=128))
        dma("gpsimd", "gw2", w2_sb[:, :, :],
            w2_in.rearrange("(c p) d -> p c d", p=128))
        t_xz = opi("vector", "dv", lambda e: e.memset(xet_sb[:, :, :, :], 0.0))
        wait("gpsimd", "cc", t_agx)
        wait("gpsimd", "dv", t_xz)
        for kk in range(4):
            gname = "gxg%d" % kk
            cnt[gname] += 16
            semh = sems[gname]

            def _mk_gather(e, k_=kk, sh=semh):
                cv = e.value_load(ccnt_sb[0:1, k_:k_ + 1], min_val=0, max_val=512)
                return e.dma_gather(
                    xet_sb[:, k_, :, :], xg_o.rearrange("(t d) -> t d", d=D),
                    xidx_sb[:, k_ * 32:(k_ + 1) * 32], 512, cv, D,
                    transpose=True).then_inc(sh, 16)
            emit("gpsimd", _mk_gather)

        wait("tensor", "gw1", 16)
        wait("tensor", "lb2", 32)
        y_evac = {0: 0, 1: 0}
        t_yst = [0] * 16
        for kk in range(4):
            wait("tensor", "gxg%d" % kk, 16)
            t_hrelu = [0] * 32
            for fb in range(32):
                ps = sc_ps[fb % 2]
                if fb >= 2:
                    wait("tensor", "ac", t_hrelu[fb - 2])
                for dc in range(8):
                    fn = (lambda e, l=w1_sb[:, dc, fb * 128:(fb + 1) * 128],
                          r=xet_sb[:, kk, dc, :], p=ps, s=(dc == 0), t=(dc == 7):
                          e.matmul(p[:, :], l, r, start=s, stop=t))
                    if dc < 7:
                        op0("tensor", fn)
                    else:
                        tpe_h = opi("tensor", "pe", fn)
                wait("scalar", "pe", tpe_h)
                t_hrelu[fb] = opi("scalar", "ac", lambda e, d=h_sb[:, fb, :],
                                  p=ps, b=b1_sb[:, fb:fb + 1]: e.activation(
                                      d, p[:, :], AF.Relu, bias=b, scale=1.0))
            if kk == 0:
                wait("tensor", "gw2", 16)
            wait("tensor", "ac", t_hrelu[31])
            for sb4 in range(4):
                islot = kk * 4 + sb4
                if islot >= 2:
                    wait("tensor", "syi%d" % (islot - 2), 16)
                if y_evac[islot % 2]:
                    wait("tensor", "dv", y_evac[islot % 2])
                ybanks = [ao_ps[0], ao_ps[1]] if islot % 2 == 0 else [z_ps[0], z_ps[1]]
                for half in range(2):
                    ps = ybanks[half]
                    for fb in range(32):
                        op0("tensor", lambda e, l=h_sb[:, fb, sb4 * 128:(sb4 + 1) * 128],
                            r=w2_sb[:, fb, half * 512:(half + 1) * 512], p=ps,
                            s=(fb == 0), t=False:
                            e.matmul(p[:, :], l, r, start=s, stop=t))
                    tpe_y = opi("tensor", "pe",
                                lambda e, r=b2hl_sb[:, half * 512:(half + 1) * 512],
                                p=ps: e.matmul(p[:, :], onesb_sb[:, :], r,
                                               start=False, stop=True))
                wait("vector", "pe", tpe_y)
                tdv_y = 0
                for half in range(2):
                    tdv_y = opi("vector", "dv", lambda e,
                                d=yt_sb[:, islot % 2, half * 512:(half + 1) * 512],
                                p=ybanks[half]: e.tensor_copy(d, p[:, :]))
                y_evac[islot % 2] = tdv_y
                wait("scalar", "dv", tdv_y)
                dma("scalar", "syi%d" % islot,
                    yg_i.rearrange("(t d) -> t d", d=D)[islot * 128:(islot + 1) * 128, :],
                    yt_sb[:, islot % 2, :])
        stD.close()

        mark("E")
        stE = ExitStack()
        ytok_sb = stE.enter_context(nc.sbuf_tensor("ytok_sb", [128, 16, D], BF16))
        x2_sb = stE.enter_context(nc.sbuf_tensor("x2_sb", [128, D], F32))
        sq2_sb = stE.enter_context(nc.sbuf_tensor("sq2_sb", [128, D], F32))

        for ii in range(16):
            wait("gpsimd", "syi%d" % ii, 16)
        t_gss = opi("gpsimd", "cc", lambda e: e.nop())
        t_agy = opi("gpsimd", "cc", lambda e: e.collective_compute(
            "AllGather", OP.bypass, replica_groups=[list(range(8))],
            ins=[yg_i.ap().opt()], outs=[yg_o.ap().opt()])._wait_ge(sems["cc"], t_gss))
        wait("gpsimd", "cc", t_agy)
        phase_barrier(["gpsimd", "vector", "scalar", "tensor", "sync"])
        for kk in range(4):
            gname = "gyg%d" % kk
            cnt[gname] += 16
            semh = sems[gname]
            emit("gpsimd", lambda e, k_=kk, sh=semh: e.dma_gather(
                ytok_sb[:, k_ * 4:(k_ + 1) * 4, :], yg_o.rearrange("(t d) -> t d", d=D),
                gidx_sb[:, k_ * 32:(k_ + 1) * 32], 512, 512, D).then_inc(sh, 16))

        for tb in range(16):
            if tb > 0:
                wait("sync", "soute" if (tb - 1) % 2 == 0 else "souto",
                     16 * ((tb - 1) // 2 + 1))
            t_x1r = dma("sync", "lx1", x2_sb[:, :], x1f[tb * 128:(tb + 1) * 128, :])
            wait("vector", "lx1", t_x1r)
            wait("vector", "gyg%d" % (tb // 4), 16)
            opi("vector", "dv", lambda e, tb_=tb: e.scalar_tensor_tensor(
                x2_sb[:, :], ytok_sb[:, tb_, :], gact_tm[:, tb_:tb_ + 1], x2_sb[:, :],
                OP.mult, OP.add))
            opi("vector", "dv", lambda e: e.reduce_sum(
                lnv_sb[:, 0:1], x2_sb[:, :], axis=AX))
            opi("vector", "dv", lambda e: e.tensor_scalar(
                lnv_sb[:, 1:2], lnv_sb[:, 0:1], 1.0 / D, None, OP.mult))
            opi("vector", "dv", lambda e: e.scalar_tensor_tensor(
                sq2_sb[:, :], x2_sb[:, :], 1.0, x2_sb[:, :], OP.mult, OP.mult,
                accum_out=lnv_sb[:, 2:3]))
            opi("vector", "dv", lambda e: e.tensor_scalar(
                lnv_sb[:, 3:4], lnv_sb[:, 2:3], 1.0 / D, EPS, OP.mult, OP.add))
            opi("vector", "dv", lambda e: e.tensor_tensor(
                lnv_sb[:, 4:5], lnv_sb[:, 1:2], lnv_sb[:, 1:2], OP.mult))
            t_var2 = opi("vector", "dv", lambda e: e.tensor_sub(
                lnv_sb[:, 3:4], lnv_sb[:, 3:4], lnv_sb[:, 4:5]))
            wait("scalar", "dv", t_var2)
            opi("scalar", "ac", lambda e: e.activation(
                lnv_sb[:, 5:6], lnv_sb[:, 3:4], AF.Ln, bias=0.0, scale=1.0))
            t_rs2 = opi("scalar", "ac", lambda e: e.activation(
                lnv_sb[:, 6:7], lnv_sb[:, 5:6], AF.Exp, bias=0.0, scale=-0.5))
            wait("vector", "ac", t_rs2)
            t_x2 = opi("vector", "dv", lambda e: e.tensor_scalar(
                x2_sb[:, :], x2_sb[:, :], lnv_sb[:, 1:2], lnv_sb[:, 6:7],
                OP.subtract, OP.mult))
            wait("scalar", "dv", t_x2)
            dma("scalar", "soute" if tb % 2 == 0 else "souto",
                out_t[tb * 128:(tb + 1) * 128, :], x2_sb[:, :])
        stE.close()

        wait("scalar", "soute", cnt["soute"])
        wait("scalar", "souto", cnt["souto"])
        for nm in ("gxg0", "gxg1", "gxg2", "gxg3", "gyg0", "gyg1", "gyg2", "gyg3",
                   "gw1", "gw2", "gpc", "gsc", "gpw", "gpwt"):
            wait("gpsimd", nm, cnt[nm])
        wait("gpsimd", "cc", cnt["cc"])
        for nm in ("ldc", "lx0e", "lx0o", "lb2", "lx1", "srt", "sxf", "sxg"):
            wait("sync", nm, cnt[nm])
        wait("scalar", "sx0e", cnt["sx0e"])
        wait("scalar", "sx0o", cnt["sx0o"])
        for ii in range(16):
            wait("scalar", "syi%d" % ii, cnt["syi%d" % ii])
        wait("scalar", "sct", cnt["sct"])

        if PHASE_LIMIT in marks:
            cut, csnap = marks[PHASE_LIMIT]
            for k in plan:
                del plan[k][cut[k]:]
            eng_for = {"soute": "scalar", "souto": "scalar", "sct": "scalar",
                       "srt": "scalar", "sxf": "scalar", "sxg": "scalar",
                       "sx0e": "scalar", "sx0o": "scalar",
                       "ldc": "sync", "lx0e": "sync", "lx0o": "sync",
                       "lb2": "sync", "lx1": "sync", "lrt": "sync",
                       "cc": "gpsimd"}
            for nm, v in csnap.items():
                if v > 0 and not nm.startswith(("pe", "dv", "ac")):
                    eng = eng_for.get(nm, "gpsimd")
                    if nm.startswith("syi"):
                        eng = "scalar"
                    semh = sems[nm]
                    plan[eng].append(lambda e, s=semh, vv=v: e.wait_ge(s, vv))
        blk = ex(nc.Block())

        @blk.tensor
        def _(e):
            for fn in plan["tensor"]:
                fn(e)

        @blk.vector
        def _(e):
            for fn in plan["vector"]:
                fn(e)

        @blk.scalar
        def _(e):
            for fn in plan["scalar"]:
                fn(e)

        @blk.gpsimd
        def _(e):
            for fn in plan["gpsimd"]:
                fn(e)

        @blk.sync
        def _(e):
            for fn in plan["sync"]:
                fn(e)

    nc.compile()
    return nc


_CACHE = {}


def kernel(L0_attn_x, L0_attn_W_Q, L0_attn_W_K, L0_attn_W_V, L0_attn_W_O,
           L0_ffn_router_w, L0_ffn_w1, L0_ffn_b1, L0_ffn_w2, L0_ffn_b2):
    x = np.ascontiguousarray(np.asarray(L0_attn_x, dtype=np.float32))
    wq = np.ascontiguousarray(np.asarray(L0_attn_W_Q, np.float32).reshape(D, D).T)
    wk = np.ascontiguousarray(np.asarray(L0_attn_W_K, np.float32).reshape(D, D).T)
    wv = np.ascontiguousarray(np.asarray(L0_attn_W_V, np.float32).reshape(D, D).T)
    wo = np.ascontiguousarray(
        np.asarray(L0_attn_W_O, np.float32).transpose(0, 2, 1).reshape(D, D))
    rw = np.ascontiguousarray(np.asarray(L0_ffn_router_w, np.float32))
    w1 = np.ascontiguousarray(np.asarray(L0_ffn_w1, np.float32))
    b1 = np.ascontiguousarray(np.asarray(L0_ffn_b1, np.float32))
    w2 = np.ascontiguousarray(np.asarray(L0_ffn_w2, np.float32))
    b2 = np.ascontiguousarray(np.asarray(L0_ffn_b2, np.float32))

    if "nc" not in _CACHE:
        _CACHE["nc"] = build_nc()
    nc = _CACHE["nc"]

    in_maps = []
    sidx = np.arange(S)
    for c in range(8):
        masklt = np.zeros((8, 16), np.float32)
        masklt[:c, :] = 1.0
        onehot = np.zeros((8, 8), np.float32)
        onehot[c, :] = 1.0
        ramp = np.zeros((128, 16), np.float32)
        ramp[sidx % 128, sidx // 128] = c * S + sidx + 1
        in_maps.append({
            "x_in": x[c], "wq_in": wq, "wk_in": wk, "wv_in": wv, "wo_in": wo,
            "rw_in": rw, "w1_in": w1[c], "b1_in": b1[c], "w2_in": w2[c],
            "b2_in": b2[c:c + 1], "masklt_in": masklt, "onehot_in": onehot,
            "ramp_in": ramp,
        })
    res = run_bass_kernel_spmd(nc, in_maps, core_ids=list(range(8)))
    return np.stack([res.results[c]["out"] for c in range(8)], axis=0)


# version 10
# speedup vs baseline: 1.1188x; 1.0626x over previous
import os as _os
import numpy as np
import concourse.bacc as bacc_mod
import concourse.mybir as mybir
from concourse.bass_utils import run_bass_kernel_spmd

F32 = mybir.dt.float32
BF16 = mybir.dt.bfloat16
I16 = mybir.dt.int16
I32 = mybir.dt.int32
OP = mybir.AluOpType
AF = mybir.ActivationFunctionType
AX = mybir.AxisListType.X

S = 2048
D = 1024
NH = 16
F = 4096
T = 16384
SEG = 2080
EPS = 1e-5
BIGE = 65536.0


PHASE_LIMIT = _os.environ.get("KPHASE", "Z")
KDEBUG = _os.environ.get("KDEBUG", "") == "1"


def build_nc():
    nc = bacc_mod.Bacc("TRN2", target_bir_lowering=False)

    x_in = nc.dram_tensor("x_in", [S, D], F32, kind="ExternalInput").ap()
    wq_in = nc.dram_tensor("wq_in", [D, D], F32, kind="ExternalInput").ap()
    wk_in = nc.dram_tensor("wk_in", [D, D], F32, kind="ExternalInput").ap()
    wv_in = nc.dram_tensor("wv_in", [D, D], F32, kind="ExternalInput").ap()
    wo_in = nc.dram_tensor("wo_in", [D, D], F32, kind="ExternalInput").ap()
    rw_in = nc.dram_tensor("rw_in", [D, 8], F32, kind="ExternalInput").ap()
    w1_in = nc.dram_tensor("w1_in", [D, F], F32, kind="ExternalInput").ap()
    b1_in = nc.dram_tensor("b1_in", [F], F32, kind="ExternalInput").ap()
    w2_in = nc.dram_tensor("w2_in", [F, D], F32, kind="ExternalInput").ap()
    b2_in = nc.dram_tensor("b2_in", [1, D], F32, kind="ExternalInput").ap()
    masklt_in = nc.dram_tensor("masklt_in", [8, 16], F32, kind="ExternalInput").ap()
    onehot_in = nc.dram_tensor("onehot_in", [8, 8], F32, kind="ExternalInput").ap()
    ramp_in = nc.dram_tensor("ramp_in", [128, 16], F32, kind="ExternalInput").ap()
    out_t = nc.dram_tensor("out", [S, D], F32, kind="ExternalOutput").ap()

    x1f = nc.dram_tensor("x1f", [S, D], F32)
    xg_i = nc.dram_tensor("xg_i", [S * D], BF16)
    xg_o = nc.dram_tensor("xg_o", [T * D], BF16)
    cnt_i = nc.dram_tensor("cnt_i", [8], F32)
    cnt_o = nc.dram_tensor("cnt_o", [64], F32)
    listtab = nc.dram_tensor("listtab", [8 * SEG, 64], F32)
    seg_t = nc.dram_tensor("seg_t", [SEG * 64], F32)
    rt = nc.dram_tensor("rt", [2, S], F32)
    yg_i = nc.dram_tensor("yg_i", [S * D], BF16)
    yg_o = nc.dram_tensor("yg_o", [T * D], BF16)

    ident = nc.inline_tensor(np.eye(128, dtype=np.float32), name="cident").ap()
    sl16_np = np.zeros((16, 16), np.float32)
    for k in range(16):
        sl16_np[k, k + 1:] = 1.0
    sl16_d = nc.inline_tensor(sl16_np, name="csl16").ap()
    masks_np = np.zeros((128, 4, 512), np.float32)
    for dd in range(4):
        for p in range(128):
            if p + dd * 128 < 512:
                masks_np[p, dd, p + dd * 128:] = 1.0
    masks_d = nc.inline_tensor(masks_np, name="cmasks").ap()
    erowp_d = nc.inline_tensor(
        np.tile(np.arange(8, dtype=np.float32)[None, :] + BIGE, (128, 1)),
        name="cerowp").ap()
    erow_d = nc.inline_tensor(
        np.tile(np.arange(8, dtype=np.float32)[None, :], (128, 1)), name="cerow").ap()
    k512_d = nc.inline_tensor(
        np.array([[0.0, 512.0, 1024.0, 1536.0, 0.0, 0.0, 0.0, 0.0]], np.float32),
        name="ck512").ap()

    from contextlib import ExitStack
    with ExitStack() as stk:
        ex = stk.enter_context

        ident_sb = ex(nc.sbuf_tensor("ident_sb", [128, 128], F32))
        ones_sb = ex(nc.sbuf_tensor("ones_sb", [128, 64], F32))
        onesb_sb = ex(nc.sbuf_tensor("onesb_sb", [2, 128], BF16))
        erow_sb = ex(nc.sbuf_tensor("erow_sb", [128, 8], F32))
        erowp_sb = ex(nc.sbuf_tensor("erowp_sb", [128, 8], F32))
        k512_sb = ex(nc.sbuf_tensor("k512_sb", [1, 8], F32))
        gate_tm = ex(nc.sbuf_tensor("gate_tm", [128, 16], F32))
        gact_tm = ex(nc.sbuf_tensor("gact_tm", [128, 16], F32))
        emask_tm = ex(nc.sbuf_tensor("emask_tm", [128, 16, 8], F32))
        small = ex(nc.sbuf_tensor("small", [128, 32], F32))
        lnv_sb = ex(nc.sbuf_tensor("lnv_sb", [128, 8], F32))
        posm_sb = ex(nc.sbuf_tensor("posm_sb", [128, 128], I16))
        gidx_sb = ex(nc.sbuf_tensor("gidx_sb", [128, 128], I16))
        xidx_sb = ex(nc.sbuf_tensor("xidx_sb", [128, 128], I16))
        ccnt_sb = ex(nc.sbuf_tensor("ccnt_sb", [1, 8], I32))
        ccntf_sb = ex(nc.sbuf_tensor("ccntf_sb", [1, 8], F32))
        b1_sb = ex(nc.sbuf_tensor("b1_sb", [128, 32], F32))
        b2hl_sb = ex(nc.sbuf_tensor("b2hl_sb", [2, D], BF16))
        meta_sb = ex(nc.sbuf_tensor("meta_sb", [8, 24], F32))
        rvec = ex(nc.sbuf_tensor("rvec", [16, 64], F32))
        ramp_sb = ex(nc.sbuf_tensor("ramp_sb", [128, 16], F32))
        cnts_ag = ex(nc.sbuf_tensor("cnts_ag", [8, 8], F32))

        sc_ps = [ex(nc.psum_tensor("sc_ps%d" % i, [128, 512], F32)) for i in range(2)]
        z_ps = [ex(nc.psum_tensor("z_ps%d" % i, [128, 512], F32)) for i in range(2)]
        b_ps = ex(nc.psum_tensor("b_ps", [64, 512], F32))
        ao_ps = [ex(nc.psum_tensor("ao_ps%d" % i, [128, 512], F32)) for i in range(2)]
        ms_ps = ex(nc.psum_tensor("ms_ps", [128, 512], F32))

        sem_names = ["syi%d" % i for i in range(16)] + ["ldc", "lx0e", "lx0o", "lb2", "lx1", "gpw", "gpwt", "gpc",
                     "gsc", "gw1", "gw2", "lrt",
                     "gxg0", "gxg1", "gxg2", "gxg3", "gyg0", "gyg1", "gyg2", "gyg3",
                     "sx0e", "sx0o", "sxf", "sxg", "sct", "srt",
                     "soute", "souto",
                     "pe", "dv", "ac", "cc"]
        sems = {n: ex(nc.semaphore(n)) for n in sem_names}
        cnt = {n: 0 for n in sem_names}
        plan = {k: [] for k in ("tensor", "vector", "scalar", "gpsimd", "sync")}

        def emit(eng, fn):
            plan[eng].append(fn)

        def dma(eng, sem, out_ap, in_ap, **kw):
            cnt[sem] += 16
            semh = sems[sem]
            emit(eng, lambda e, o=out_ap, i=in_ap: e.dma_start(out=o, in_=i, **kw)
                 .then_inc(semh, 16))
            return cnt[sem]

        def wait(eng, sem, val):
            semh = sems[sem]
            emit(eng, lambda e, v=val: e.wait_ge(semh, v))

        ENG_SEM = {"vector": "dv", "scalar": "ac"}

        def opi(eng, sem, fn):
            own = ENG_SEM.get(eng)
            if own is not None and cnt[own] > 0:
                semo = sems[own]
                emit(eng, lambda e, v=cnt[own]: e.wait_ge(semo, v))
            cnt[sem] += 1
            semh = sems[sem]
            emit(eng, lambda e, f=fn: f(e).then_inc(semh, 1))
            return cnt[sem]

        def op0(eng, fn):
            emit(eng, lambda e, f=fn: f(e))

        marks = {}

        def mark(name):
            marks[name] = ({k: len(v) for k, v in plan.items()}, dict(cnt))

        def phase_barrier(engines):
            snap = [(nm, cnt[nm]) for nm in sem_names if cnt[nm] > 0]
            for eng in engines:
                for nm, v in snap:
                    wait(eng, nm, v)

        stAB = ExitStack()
        v_tok = stAB.enter_context(nc.sbuf_tensor("v_tok", [128, 16, NH * 65], BF16))
        q_sb = stAB.enter_context(nc.sbuf_tensor("q_sb", [128, 8, S], BF16))
        k_sb = stAB.enter_context(nc.sbuf_tensor("k_sb", [128, 8, S], BF16))
        wo_sb = stAB.enter_context(nc.sbuf_tensor("wo_sb", [128, 8, D], BF16))
        masks_sb = stAB.enter_context(nc.sbuf_tensor("masks_sb", [128, 4, 512], BF16))
        rw_sb = stAB.enter_context(nc.sbuf_tensor("rw_sb", [128, 8, 8], F32))

        stA = ExitStack()
        wtmp_sb = stA.enter_context(nc.sbuf_tensor("wtmp_sb", [128, 8, D], BF16))
        x0t_sb = stA.enter_context(nc.sbuf_tensor("x0t_sb", [128, 8, S], BF16))
        x0_sb = stA.enter_context(nc.sbuf_tensor("x0_sb", [128, 2, D], F32))
        b2s_sb = stA.enter_context(nc.sbuf_tensor("b2s_sb", [1, 2 * D], F32))
        b1t_sb = stA.enter_context(nc.sbuf_tensor("b1t_sb", [32, 128], F32))
        b2w_sb = stA.enter_context(nc.sbuf_tensor("b2w_sb", [1, 2, D], BF16))

        dma("sync", "ldc", ident_sb[:, :], ident[:, :])
        dma("sync", "ldc", erow_sb[:, :], erow_d[:, :])
        dma("sync", "ldc", erowp_sb[:, :], erowp_d[:, :])
        dma("sync", "ldc", k512_sb[:, :], k512_d[:, :])
        dma("sync", "ldc", rw_sb[:, :, :], rw_in.rearrange("(c p) e -> p c e", p=128))
        dma("sync", "ldc", meta_sb[:, 0:16], masklt_in[:, :])
        dma("sync", "ldc", meta_sb[:, 16:24], onehot_in[:, :])
        dma("sync", "ldc", b1t_sb[:, :], b1_in.rearrange("(a p) -> a p", p=128))
        dma("sync", "ldc", b2s_sb[0:1, 0:D], b2_in[:, :])
        dma("sync", "ldc", rvec[0:16, 48:64], sl16_d[:, :])
        T_LDC = dma("sync", "ldc", ramp_sb[:, :], ramp_in[:, :])
        assert T_LDC == 176
        dma("gpsimd", "gpw", masks_sb[:, :, :], masks_d[:, :, :])
        T_GPW = dma("gpsimd", "gpw", wo_sb[:, :, :], wo_in.rearrange("(c p) e -> p c e", p=128))

        opi("vector", "dv", lambda e: e.memset(ones_sb[:, :], 1.0))
        opi("vector", "dv", lambda e: e.memset(onesb_sb[:, :], 1.0))
        t_vones = opi("vector", "dv", lambda e: e.memset(
            v_tok.ap().rearrange("p a (h c) -> p a h c", c=65)[:, :, :, 64:65], 1.0))
        wait("vector", "ldc", T_LDC)
        opi("vector", "dv", lambda e: e.tensor_copy(b2w_sb[0:1, 0, :], b2s_sb[0:1, 0:D]))
        opi("vector", "dv", lambda e: e.tensor_copy(b2s_sb[0:1, D:2 * D], b2w_sb[0:1, 0, :]))
        opi("vector", "dv", lambda e: e.tensor_sub(
            b2s_sb[0:1, D:2 * D], b2s_sb[0:1, 0:D], b2s_sb[0:1, D:2 * D]))
        t_b2pre = opi("vector", "dv", lambda e: e.tensor_copy(
            b2w_sb[0:1, 1, :], b2s_sb[0:1, D:2 * D]))
        wait("sync", "dv", t_b2pre)
        dma("sync", "lb2", b2hl_sb[0:1, :], b2w_sb[0:1, 0, :])
        t_b2l_dma = dma("sync", "lb2", b2hl_sb[1:2, :], b2w_sb[0:1, 1, :])

        wait("tensor", "ldc", T_LDC)
        tpe_b1 = opi("tensor", "pe", lambda e: e.transpose(
            ms_ps[:, 0:32], b1t_sb[:, :], ident_sb[0:32, 0:32]))
        wait("vector", "pe", tpe_b1)
        tdv_b1 = opi("vector", "dv", lambda e: e.tensor_copy(b1_sb[:, :], ms_ps[:, 0:32]))
        wait("tensor", "dv", tdv_b1)
        t_xtr_done = [0] * 16
        for tb in range(16):
            if tb >= 2:
                wait("sync", "dv", t_xtr_done[tb - 2])
            t_ld = dma("sync", "lx0e" if tb % 2 == 0 else "lx0o",
                       x0_sb[:, tb % 2, :], x_in[tb * 128:(tb + 1) * 128, :])
            wait("tensor", "lx0e" if tb % 2 == 0 else "lx0o", 16 * (tb // 2 + 1))
            for dc in range(8):
                tpe = opi("tensor", "pe", lambda e, s=x0_sb[:, tb % 2, dc * 128:(dc + 1) * 128]:
                          e.transpose(ms_ps[:, 0:128], s, ident_sb[:, :]))
                wait("vector", "pe", tpe)
                tdv = opi("vector", "dv", lambda e, d=x0t_sb[:, dc, tb * 128:(tb + 1) * 128]:
                          e.tensor_copy(d, ms_ps[:, 0:128]))
                wait("tensor", "dv", tdv)
            t_xtr_done[tb] = cnt["dv"]

        dma("gpsimd", "gpwt", wtmp_sb[:, :, :],
            wv_in.rearrange("(c p) a -> p c a", p=128))
        wait("tensor", "gpwt", 16)
        wait("tensor", "dv", t_vones)
        last_evac = {0: 0, 1: 0}
        for tb in range(16):
            for half in range(2):
                ps = sc_ps[half]
                if last_evac[half]:
                    wait("tensor", "dv", last_evac[half])
                for dc in range(8):
                    fn = (lambda e, l=x0t_sb[:, dc, tb * 128:(tb + 1) * 128],
                          r=wtmp_sb[:, dc, half * 512:(half + 1) * 512], p=ps,
                          s=(dc == 0), t=(dc == 7):
                          e.matmul(p[:, :], l, r, start=s, stop=t))
                    if dc < 7:
                        op0("tensor", fn)
                    else:
                        tpe = opi("tensor", "pe", fn)
                dst = v_tok.ap().rearrange("p a (h c) -> p a h c", c=65)[
                    :, tb, half * 8:(half + 1) * 8, 0:64]
                wait("vector", "pe", tpe)
                last_evac[half] = opi("vector", "dv",
                                      lambda e, d=dst, p=ps: e.tensor_copy(d, p[:, :]))
        for wi, (w_dram, dsts) in enumerate(((wq_in, q_sb), (wk_in, k_sb))):
            wait("gpsimd", "dv", cnt["dv"])
            dma("gpsimd", "gpwt", wtmp_sb[:, :, :],
                w_dram.rearrange("(c p) a -> p c a", p=128))
            wait("tensor", "gpwt", 16 * (wi + 2))
            for ac in range(8):
                for tt in range(4):
                    ps = sc_ps[tt % 2]
                    if last_evac[tt % 2]:
                        wait("tensor", "dv", last_evac[tt % 2])
                    for dc in range(8):
                        fn = (lambda e, l=wtmp_sb[:, dc, ac * 128:(ac + 1) * 128],
                              r=x0t_sb[:, dc, tt * 512:(tt + 1) * 512], p=ps,
                              s=(dc == 0), t=(dc == 7):
                              e.matmul(p[:, :], l, r, start=s, stop=t))
                        if dc < 7:
                            op0("tensor", fn)
                        else:
                            tpe = opi("tensor", "pe", fn)
                    wait("vector", "pe", tpe)
                    last_evac[tt % 2] = opi(
                        "vector", "dv", lambda e, d=dsts[:, ac, tt * 512:(tt + 1) * 512],
                        p=ps: e.tensor_copy(d, p[:, :]))
        stA.close()

        mark("B")
        strip = stAB.enter_context(nc.sbuf_tensor("strip", [128, 16, 512], BF16))
        z_sb = stAB.enter_context(nc.sbuf_tensor("z_sb", [128, 8, 512], BF16))
        recip_sb = stAB.enter_context(nc.sbuf_tensor("recip_sb", [1, 512], F32))
        bsb_sb = stAB.enter_context(nc.sbuf_tensor("bsb_sb", [64, 512], F32))
        x1_sb = stAB.enter_context(nc.sbuf_tensor("x1_sb", [128, D], F32))
        x1b_sb = stAB.enter_context(nc.sbuf_tensor("x1b_sb", [128, D], BF16))
        x1t_sb = stAB.enter_context(nc.sbuf_tensor("x1t_sb", [128, 8, 128], F32))
        sq_sb = stAB.enter_context(nc.sbuf_tensor("sq_sb", [128, D], F32))

        wait("vector", "lb2", 32)
        wait("scalar", "lb2", 32)
        wait("tensor", "gpw", T_GPW)
        wait("tensor", "dv", max(last_evac[0], last_evac[1]))
        t_eq_prev = 0
        t_zn_par = {0: 0, 1: 0}
        t_zn_prev = 0
        for j in range(4):
            nch = 4 * j + 4
            for a in range(NH):
                par = (j * 16 + a) % 2
                rowb = 64 * (a % 2)
                ahc = a // 2
                t_exp = [0] * nch
                t_msk = [0] * nch
                wait("tensor", "ac", cnt["ac"])
                for i in range(nch):
                    ps = sc_ps[i % 2]
                    if i >= 2:
                        wait("tensor", "ac", t_exp[i - 2])
                    tpe = opi("tensor", "pe", lambda e,
                              l=q_sb[rowb:rowb + 64, ahc, i * 128:(i + 1) * 128],
                              r=k_sb[rowb:rowb + 64, ahc, j * 512:(j + 1) * 512],
                              p=ps: e.matmul(p[:, :], l, r, start=True, stop=True))
                    wait("scalar", "pe", tpe)
                    t_exp[i] = opi("scalar", "ac", lambda e, d=strip[:, i, :], p=ps:
                                   e.activation(d, p[:, :], AF.Exp, bias=0.0, scale=0.125))
                    if i >= 4 * j:
                        wait("vector", "ac", t_exp[i])
                        t_msk[i] = opi("vector", "dv", lambda e, s=strip[:, i, :],
                                       m=masks_sb[:, i - 4 * j, :]:
                                       e.tensor_tensor(s, s, m, OP.mult))
                zp = z_ps[par]
                if t_zn_par[par]:
                    wait("tensor", "dv", t_zn_par[par])
                for i in range(nch):
                    if t_msk[i]:
                        wait("tensor", "dv", t_msk[i])
                    else:
                        wait("tensor", "ac", t_exp[i])
                    fn = (lambda e, l=v_tok[:, i, a * 65:(a + 1) * 65],
                          r=strip[:, i, :], p=zp, s=(i == 0), t=(i == nch - 1):
                          e.matmul(p[0:65, :], l, r, start=s, stop=t))
                    if i < nch - 1:
                        op0("tensor", fn)
                    else:
                        tpe_z = opi("tensor", "pe", fn)
                wait("vector", "pe", tpe_z)
                t_rc = opi("vector", "dv", lambda e, p=zp:
                           e.reciprocal(recip_sb[0:1, :], p[64:65, :]))
                if t_zn_prev:
                    wait("tensor", "dv", t_zn_prev)
                wait("tensor", "dv", t_rc)
                tpe_b = opi("tensor", "pe", lambda e: e.matmul(
                    b_ps[:, :], ones_sb[0:1, 0:64], recip_sb[0:1, :],
                    start=True, stop=True))
                wait("vector", "pe", tpe_b)
                opi("vector", "dv", lambda e: e.tensor_copy(bsb_sb[:, :], b_ps[:, :]))
                t_zn = opi("vector", "dv", lambda e, d=z_sb[rowb:rowb + 64, ahc, :],
                           p=zp: e.tensor_tensor(d, p[0:65 - 1, :], bsb_sb[:, :], OP.mult))
                t_zn_par[par] = t_zn
                t_zn_prev = t_zn

            wait("tensor", "gpw", T_GPW)
            wait("tensor", "dv", t_zn_prev)
            for s4 in range(4):
                tb = j * 4 + s4
                for half in range(2):
                    ps = ao_ps[half]
                    for ahc in range(8):
                        fn = (lambda e, l=z_sb[:, ahc, s4 * 128:(s4 + 1) * 128],
                              r=wo_sb[:, ahc, half * 512:(half + 1) * 512], p=ps,
                              s=(ahc == 0), t=(ahc == 7):
                              e.matmul(p[:, :], l, r, start=s, stop=t))
                        if ahc < 7:
                            op0("tensor", fn)
                        else:
                            tpe_ao = opi("tensor", "pe", fn)
                if tb > 0:
                    wait("scalar", "pe", t_rtr_last)
                    wait("scalar", "sxf", 16 * tb)
                sx0 = "sx0e" if tb % 2 == 0 else "sx0o"
                t_x0r = dma("scalar", sx0, x1_sb[:, :], x_in[tb * 128:(tb + 1) * 128, :])
                wait("vector", sx0, 16 * (tb // 2 + 1))
                wait("vector", "pe", tpe_ao)
                for half in range(2):
                    opi("vector", "dv", lambda e, h=half: e.tensor_tensor(
                        x1_sb[:, h * 512:(h + 1) * 512], x1_sb[:, h * 512:(h + 1) * 512],
                        ao_ps[h][:, :], OP.add))
                t_res = cnt["dv"]
                wait("tensor", "dv", t_res)
                opi("vector", "dv", lambda e: e.reduce_sum(
                    lnv_sb[:, 0:1], x1_sb[:, :], axis=AX))
                opi("vector", "dv", lambda e: e.tensor_scalar(
                    lnv_sb[:, 1:2], lnv_sb[:, 0:1], 1.0 / D, None, OP.mult))
                t_sq = opi("vector", "dv", lambda e: e.scalar_tensor_tensor(
                    sq_sb[:, :], x1_sb[:, :], 1.0, x1_sb[:, :], OP.mult, OP.mult,
                    accum_out=lnv_sb[:, 2:3]))
                opi("vector", "dv", lambda e: e.tensor_scalar(
                    lnv_sb[:, 3:4], lnv_sb[:, 2:3], 1.0 / D, EPS, OP.mult, OP.add))
                opi("vector", "dv", lambda e: e.tensor_tensor(
                    lnv_sb[:, 4:5], lnv_sb[:, 1:2], lnv_sb[:, 1:2], OP.mult))
                t_var = opi("vector", "dv", lambda e: e.tensor_sub(
                    lnv_sb[:, 3:4], lnv_sb[:, 3:4], lnv_sb[:, 4:5]))
                wait("scalar", "dv", t_var)
                opi("scalar", "ac", lambda e: e.activation(
                    lnv_sb[:, 5:6], lnv_sb[:, 3:4], AF.Ln, bias=0.0, scale=1.0))
                t_rs = opi("scalar", "ac", lambda e: e.activation(
                    lnv_sb[:, 6:7], lnv_sb[:, 5:6], AF.Exp, bias=0.0, scale=-0.5))
                wait("vector", "ac", t_rs)
                t_x1 = opi("vector", "dv", lambda e: e.tensor_scalar(
                    x1_sb[:, :], x1_sb[:, :], lnv_sb[:, 1:2], lnv_sb[:, 6:7],
                    OP.subtract, OP.mult))
                wait("scalar", "dv", t_x1)
                dma("scalar", "sxf", x1f[tb * 128:(tb + 1) * 128, :], x1_sb[:, :])
                if tb > 0:
                    wait("vector", "sxg", 16 * tb)
                t_x1b = opi("vector", "dv", lambda e: e.tensor_copy(
                    x1b_sb[:, :], x1_sb[:, :]))
                wait("scalar", "dv", t_x1b)
                dma("scalar", "sxg",
                    xg_i.rearrange("(t d) -> t d", d=D)[tb * 128:(tb + 1) * 128, :],
                    x1b_sb[:, :])
                wait("tensor", "dv", t_x1)
                for dc in range(8):
                    tpe_t = opi("tensor", "pe", lambda e,
                                s=x1_sb[:, dc * 128:(dc + 1) * 128]:
                                e.transpose(ms_ps[:, 0:128], s, ident_sb[:, :]))
                    wait("vector", "pe", tpe_t)
                    tdv_t = opi("vector", "dv", lambda e, d=x1t_sb[:, dc, :]:
                                e.tensor_copy(d, ms_ps[:, 0:128]))
                    wait("tensor", "dv", tdv_t)
                for dc in range(8):
                    fn = (lambda e, l=x1t_sb[:, dc, :], r=rw_sb[:, dc, :],
                          s=(dc == 0), t=(dc == 7):
                          e.matmul(ms_ps[:, 128:136], l, r, start=s, stop=t))
                    if dc < 7:
                        op0("tensor", fn)
                    else:
                        tpe_rt = opi("tensor", "pe", fn)
                t_rtr_last = tpe_rt
                wait("vector", "pe", tpe_rt)
                opi("vector", "dv", lambda e: e.reduce_max(
                    small[:, 12:13], ms_ps[:, 128:136], axis=AX, negate=True))
                t_nm = cnt["dv"]
                wait("scalar", "dv", t_nm)
                t_ex = opi("scalar", "ac", lambda e: e.activation(
                    small[:, 24:32], ms_ps[:, 128:136], AF.Exp, bias=small[:, 12:13],
                    scale=1.0, accum_out=small[:, 8:9]))
                wait("vector", "ac", t_ex)
                opi("vector", "dv", lambda e, tb_=tb: e.reciprocal(
                    gate_tm[:, tb_:tb_ + 1], small[:, 8:9]))
                opi("vector", "dv", lambda e: e.tensor_scalar(
                    small[:, 0:8], ms_ps[:, 128:136], small[:, 12:13], None, OP.add))
                opi("vector", "dv", lambda e: e.tensor_scalar(
                    small[:, 0:8], small[:, 0:8], 0.0, -BIGE, OP.is_ge, OP.mult))
                opi("vector", "dv", lambda e: e.tensor_tensor(
                    small[:, 0:8], small[:, 0:8], erowp_sb[:, :], OP.add))
                opi("vector", "dv", lambda e: e.tensor_reduce(
                    small[:, 9:10], small[:, 0:8], AX, OP.min))
                t_eq_prev = opi("vector", "dv", lambda e, tb_=tb: e.tensor_scalar(
                    emask_tm[:, tb_, :], erow_sb[:, :], small[:, 9:10], None,
                    OP.is_equal))

        mark("C")
        stC = ExitStack()
        emt = stC.enter_context(nc.sbuf_tensor("emt", [16, 8, 128], F32))
        posg = stC.enter_context(nc.sbuf_tensor("posg", [16, 8, 128], F32))
        racc = stC.enter_context(nc.sbuf_tensor("racc", [16, 4, 128], F32))
        rta_sb = stC.enter_context(nc.sbuf_tensor("rta_sb", [128, 2, 16], F32))
        vals_sb = stC.enter_context(nc.sbuf_tensor("vals_sb", [128, 16, 64], F32))
        zer_sb = stC.enter_context(nc.sbuf_tensor("zer_sb", [128, 2080], F32))
        lsta_sb = stC.enter_context(nc.sbuf_tensor("lsta_sb", [128, 16, 64], F32))

        wait("gpsimd", "sxg", 256)
        t_agx = opi("gpsimd", "cc", lambda e: e.collective_compute(
            "AllGather", OP.bypass, replica_groups=[list(range(8))],
            ins=[xg_i.ap().opt()], outs=[xg_o.ap().opt()])._wait_ge(sems["sxg"], 256))


        wait("tensor", "dv", t_eq_prev)
        for e8 in range(8):
            tpe = opi("tensor", "pe", lambda e, s=emask_tm[:, :, e8]:
                      e.transpose(ms_ps[0:16, 0:128], s, ident_sb[:, :]))
            wait("vector", "pe", tpe)
            tdv = opi("vector", "dv", lambda e, d=emt[:, e8, :]:
                      e.tensor_copy(d, ms_ps[0:16, 0:128]))
            wait("tensor", "dv", tdv)
        for e8 in range(8):
            opi("vector", "dv", lambda e, e_=e8: e.reduce_sum(
                rvec[0:16, e_:e_ + 1], emt[:, e_, :], axis=AX))
        t_tot = cnt["dv"]
        wait("tensor", "dv", t_tot)
        tpe_c = opi("tensor", "pe", lambda e: e.matmul(
            ms_ps[0:8, 136:137], rvec[0:16, 0:8], ones_sb[0:16, 0:1],
            start=True, stop=True))
        wait("vector", "pe", tpe_c)
        t_cv = opi("vector", "dv", lambda e: e.tensor_copy(
            small[0:8, 10:11], ms_ps[0:8, 136:137]))
        wait("scalar", "dv", t_cv)
        dma("scalar", "sct", cnt_i.rearrange("(a c) -> a c", c=1), small[0:8, 10:11])
        wait("gpsimd", "sct", 16)
        t_agc = opi("gpsimd", "cc", lambda e: e.collective_compute(
            "AllGather", OP.bypass, replica_groups=[list(range(8))],
            ins=[cnt_i.ap().opt()], outs=[cnt_o.ap().opt()])._wait_ge(sems["sct"], 16))
        wait("gpsimd", "cc", t_agc)
        mark("C1")
        t_gpc = dma("gpsimd", "gpc", cnts_ag[:, :], cnt_o.rearrange("(c e) -> c e", e=8))
        wait("gpsimd", "gpc", t_gpc)
        wait("tensor", "gpc", t_gpc)
        op0("tensor", lambda e: e.matmul(
            ms_ps[0:16, 144:152], rvec[0:16, 48:64], rvec[0:16, 0:8],
            start=True, stop=False))
        tpe_o = opi("tensor", "pe", lambda e: e.matmul(
            ms_ps[0:16, 144:152], meta_sb[0:8, 0:16], cnts_ag[:, :],
            start=False, stop=True))
        wait("vector", "pe", tpe_o)
        opi("vector", "dv", lambda e: e.tensor_copy(
            rvec[0:16, 8:16], ms_ps[0:16, 144:152]))
        opi("vector", "dv", lambda e: e.memset(racc[:, 3, :], 0.0))
        for e8 in range(8):
            opi("vector", "dv", lambda e, e_=e8: e.tensor_tensor_scan(
                posg[:, e_, :], emt[:, e_, :], racc[0:16, 3, :],
                rvec[0:16, 8 + e_:9 + e_], OP.add, OP.add))
        opi("vector", "dv", lambda e: e.memset(racc[:, 0, :], 0.0))
        opi("vector", "dv", lambda e: e.memset(racc[:, 1, :], 0.0))
        opi("vector", "dv", lambda e: e.memset(racc[:, 2, :], 0.0))
        t_fit = 0
        for e8 in range(8):
            opi("vector", "dv", lambda e, e_=e8: e.tensor_scalar(
                racc[:, 3, :], posg[:, e_, :], float(e_ * SEG - 1),
                float(e_ * SEG + SEG - 1), OP.add, OP.min))
            opi("vector", "dv", lambda e, e_=e8: e.tensor_tensor(
                racc[:, 3, :], racc[:, 3, :], emt[:, e_, :], OP.mult))
            opi("vector", "dv", lambda e: e.tensor_tensor(
                racc[:, 0, :], racc[:, 0, :], racc[:, 3, :], OP.add))
            opi("vector", "dv", lambda e, e_=e8: e.tensor_scalar(
                racc[:, 3, :], posg[:, e_, :], float(e_ * 2048 - 1),
                float(e_ * 2048 + 2047), OP.add, OP.min))
            opi("vector", "dv", lambda e, e_=e8: e.tensor_tensor(
                racc[:, 3, :], racc[:, 3, :], emt[:, e_, :], OP.mult))
            opi("vector", "dv", lambda e: e.tensor_tensor(
                racc[:, 1, :], racc[:, 1, :], racc[:, 3, :], OP.add))
            opi("vector", "dv", lambda e, e_=e8: e.tensor_scalar(
                racc[:, 3, :], posg[:, e_, :], 2048.0, None, OP.is_le))
            opi("vector", "dv", lambda e, e_=e8: e.tensor_tensor(
                racc[:, 3, :], racc[:, 3, :], emt[:, e_, :], OP.mult))
            t_fit = opi("vector", "dv", lambda e: e.tensor_tensor(
                racc[:, 2, :], racc[:, 2, :], racc[:, 3, :], OP.add))
        wait("scalar", "dv", t_fit)
        dma("scalar", "srt", rt[0:1, :].rearrange("x (p c) -> p x c", c=128),
            racc[:, 0:1, :])
        dma("scalar", "srt", rt[1:2, :].rearrange("x (p c) -> p x c", c=128),
            racc[:, 1:2, :])
        wait("sync", "srt", 32)
        mark("C2")
        dma("sync", "lrt", rta_sb[:, :, :],
            rt.rearrange("x (b g) -> b x g", g=16))
        wait("tensor", "lrt", 16)
        mark("C2a")
        for which, dst_i16 in ((0, posm_sb), (1, gidx_sb)):
            tpe = opi("tensor", "pe", lambda e, s=rta_sb[:, which, :]:
                      e.transpose(ms_ps[0:16, 0:128], s, ident_sb[:, :]))
            wait("vector", "pe", tpe)
            tdv = opi("vector", "dv", lambda e, d=dst_i16[0:16, :]:
                      e.tensor_copy(d, ms_ps[0:16, 0:128]))
            wait("tensor", "dv", tdv)
        mark("C2b")
        tpe_f = opi("tensor", "pe", lambda e: e.transpose(
            ms_ps[:, 384:400], racc[0:16, 2, :], ident_sb[0:16, 0:16]))
        wait("vector", "pe", tpe_f)
        t_ge = opi("vector", "dv", lambda e: e.tensor_tensor(
            gact_tm[:, :], gate_tm[:, :], ms_ps[:, 384:400], OP.mult))
        if KDEBUG:
            dbgs = {
                "dbg_posm": (posm_sb[0:16, :], I16, [16, 128]),
                "dbg_gidx": (gidx_sb[0:16, :], I16, [16, 128]),
                "dbg_xon": (None, None, None),
            }
            del dbgs["dbg_xon"]
            dbg_list = [
                ("dbg_posm", posm_sb[0:16, :], I16, [16, 128]),
                ("dbg_gidx", gidx_sb[0:16, :], I16, [16, 128]),
                ("dbg_racc", racc[:, :, :], F32, [16, 4, 128]),
                ("dbg_posg", posg[:, :, :], F32, [16, 8, 128]),
                ("dbg_emt", emt[:, :, :], F32, [16, 8, 128]),
                ("dbg_rvec", rvec[0:16, 0:16], F32, [16, 16]),
                ("dbg_cnts", cnts_ag[:, :], F32, [8, 8]),
                ("dbg_small", small[:, 8:10], F32, [128, 2]),
                ("dbg_gate", gate_tm[:, :], F32, [128, 16]),
                ("dbg_gact", gact_tm[:, :], F32, [128, 16]),
                ("dbg_emask", emask_tm[:, :, :], F32, [128, 16, 8]),
            ]
            wait("sync", "dv", cnt["dv"])
            wait("sync", "ac", cnt["ac"])
            for nm, ap_, dt_, shp in dbg_list:
                od = nc.dram_tensor(nm, shp, dt_, kind="ExternalOutput").ap()
                dma("sync", "lx1", od[tuple(slice(None) for _ in shp)], ap_)
            wait("sync", "lx1", cnt["lx1"])
        mark("C2c")
        wait("tensor", "dv", t_ge)
        tpe_g = opi("tensor", "pe", lambda e: e.matmul(
            ms_ps[0:8, 408:409], cnts_ag[:, :], ones_sb[0:8, 0:1],
            start=True, stop=True))
        wait("vector", "pe", tpe_g)
        t_gt = opi("vector", "dv", lambda e: e.tensor_copy(
            small[0:8, 13:14], ms_ps[0:8, 408:409]))
        wait("tensor", "dv", t_gt)
        mark("C2d")
        tpe_m = opi("tensor", "pe", lambda e: e.matmul(
            ms_ps[0:1, 400:408], small[0:8, 13:14], meta_sb[0:8, 16:24],
            start=True, stop=True))
        wait("vector", "pe", tpe_m)
        opi("vector", "dv", lambda e: e.tensor_copy(small[0:1, 8:16],
                                                    ms_ps[0:1, 400:408]))
        mark("C2e")
        opi("vector", "dv", lambda e: e.tensor_scalar(
            ccntf_sb[:, :], k512_sb[:, :], -1.0, small[0:1, 8:9], OP.mult, OP.add))
        opi("vector", "dv", lambda e: e.tensor_scalar_max(
            ccntf_sb[:, :], ccntf_sb[:, :], 0.0))
        opi("vector", "dv", lambda e: e.tensor_scalar_min(
            ccntf_sb[:, :], ccntf_sb[:, :], 512.0))
        t_ccnt = opi("vector", "dv", lambda e: e.tensor_copy(ccnt_sb[:, :],
                                                             ccntf_sb[:, :]))
        mark("C3")
        opi("vector", "dv", lambda e: e.memset(zer_sb[:, :], 0.0))
        opi("vector", "dv", lambda e: e.memset(vals_sb[:, :, :], 0.0))
        t_vals = opi("vector", "dv", lambda e: e.tensor_copy(vals_sb[:, :, 0],
                                                             ramp_sb[:, :]))
        wait("gpsimd", "dv", cnt["dv"])
        for qq in range(4):
            dma("gpsimd", "gpc",
                listtab.rearrange("t e -> (t e)").rearrange(
                    "(q p f) -> q p f", q=4, p=128)[qq, :, :],
                zer_sb[:, :])
        for rr in range(1, 8):
            dma("gpsimd", "gpc", posm_sb[16 * rr:16 * (rr + 1), :], posm_sb[0:16, :])
            dma("gpsimd", "gpc", gidx_sb[16 * rr:16 * (rr + 1), :], gidx_sb[0:16, :])
        wait("gpsimd", "gpc", cnt["gpc"])
        for kk in range(4):
            cnt["gsc"] += 16
            semh = sems["gsc"]
            emit("gpsimd", lambda e, k_=kk, sh=semh, w=16 * kk: e.dma_scatter_add(
                listtab.ap(), vals_sb[:, k_ * 4:(k_ + 1) * 4, :],
                posm_sb[:, k_ * 32:(k_ + 1) * 32], 512, 512, 64)
                .then_inc(sh, 16)._wait_ge(sh, w))
        wait("gpsimd", "gsc", 64)
        t_rs = opi("gpsimd", "cc", lambda e: e.collective_compute(
            "ReduceScatter", OP.add, replica_groups=[list(range(8))],
            ins=[listtab.ap().rearrange("t e -> (t e)").opt()],
            outs=[seg_t.ap().opt()])._wait_ge(sems["gsc"], 64))
        wait("gpsimd", "cc", t_rs)
        wait("gpsimd", "lb2", 32)
        t_lst = dma("gpsimd", "gpc", lsta_sb[:, :, :],
                    seg_t.rearrange("(b a e) -> b a e", a=16, e=64)[0:128, :, :])
        wait("gpsimd", "gpc", t_lst)
        wait("tensor", "gpc", t_lst)
        tpe_l = opi("tensor", "pe", lambda e: e.transpose(
            ms_ps[0:16, 0:128], lsta_sb[:, :, 0], ident_sb[:, :]))
        wait("vector", "pe", tpe_l)
        t_xi = opi("vector", "dv", lambda e: e.tensor_scalar(
            xidx_sb[0:16, :], ms_ps[0:16, 0:128], -1.0, None, OP.add))
        wait("gpsimd", "dv", t_xi)
        for rr in range(1, 8):
            dma("gpsimd", "gpc", xidx_sb[16 * rr:16 * (rr + 1), :], xidx_sb[0:16, :])
        t_xirep = cnt["gpc"]
        wait("gpsimd", "gpc", t_xirep)
        stC.close()
        stAB.close()

        mark("D")
        stD = ExitStack()
        w1_sb = stD.enter_context(nc.sbuf_tensor("w1_sb", [128, 8, F], BF16))
        w2_sb = stD.enter_context(nc.sbuf_tensor("w2_sb", [128, 32, D], BF16))
        xet_sb = stD.enter_context(nc.sbuf_tensor("xet_sb", [128, 4, 8, 512], BF16))
        h_sb = stD.enter_context(nc.sbuf_tensor("h_sb", [128, 32, 512], BF16))
        yt_sb = stD.enter_context(nc.sbuf_tensor("yt_sb", [128, 2, D], BF16))

        phase_barrier(["gpsimd", "vector", "scalar", "tensor", "sync"])
        dma("gpsimd", "gw1", w1_sb[:, :, :],
            w1_in.rearrange("(c p) f -> p c f", p=128))
        dma("gpsimd", "gw2", w2_sb[:, :, :],
            w2_in.rearrange("(c p) d -> p c d", p=128))
        t_xz = opi("vector", "dv", lambda e: e.memset(xet_sb[:, :, :, :], 0.0))
        wait("gpsimd", "cc", t_agx)
        wait("gpsimd", "dv", t_xz)
        for kk in range(4):
            gname = "gxg%d" % kk
            cnt[gname] += 16
            semh = sems[gname]

            def _mk_gather(e, k_=kk, sh=semh):
                cv = e.value_load(ccnt_sb[0:1, k_:k_ + 1], min_val=0, max_val=512)
                return e.dma_gather(
                    xet_sb[:, k_, :, :], xg_o.rearrange("(t d) -> t d", d=D),
                    xidx_sb[:, k_ * 32:(k_ + 1) * 32], 512, cv, D,
                    transpose=True).then_inc(sh, 16)
            emit("gpsimd", _mk_gather)

        wait("tensor", "gw1", 16)
        wait("tensor", "lb2", 32)
        y_evac = {0: 0, 1: 0}
        t_yst = [0] * 16
        for kk in range(4):
            wait("tensor", "gxg%d" % kk, 16)
            t_hrelu = [0] * 32
            for fb in range(32):
                ps = sc_ps[fb % 2]
                if fb >= 2:
                    wait("tensor", "ac", t_hrelu[fb - 2])
                for dc in range(8):
                    fn = (lambda e, l=w1_sb[:, dc, fb * 128:(fb + 1) * 128],
                          r=xet_sb[:, kk, dc, :], p=ps, s=(dc == 0), t=(dc == 7):
                          e.matmul(p[:, :], l, r, start=s, stop=t))
                    if dc < 7:
                        op0("tensor", fn)
                    else:
                        tpe_h = opi("tensor", "pe", fn)
                wait("scalar", "pe", tpe_h)
                t_hrelu[fb] = opi("scalar", "ac", lambda e, d=h_sb[:, fb, :],
                                  p=ps, b=b1_sb[:, fb:fb + 1]: e.activation(
                                      d, p[:, :], AF.Relu, bias=b, scale=1.0))
            if kk == 0:
                wait("tensor", "gw2", 16)
            wait("tensor", "ac", t_hrelu[31])
            for sb4 in range(4):
                islot = kk * 4 + sb4
                if islot >= 2:
                    wait("tensor", "syi%d" % (islot - 2), 16)
                if y_evac[islot % 2]:
                    wait("tensor", "dv", y_evac[islot % 2])
                ybanks = [ao_ps[0], ao_ps[1]] if islot % 2 == 0 else [z_ps[0], z_ps[1]]
                for half in range(2):
                    ps = ybanks[half]
                    for fb in range(32):
                        op0("tensor", lambda e, l=h_sb[:, fb, sb4 * 128:(sb4 + 1) * 128],
                            r=w2_sb[:, fb, half * 512:(half + 1) * 512], p=ps,
                            s=(fb == 0), t=False:
                            e.matmul(p[:, :], l, r, start=s, stop=t))
                    tpe_y = opi("tensor", "pe",
                                lambda e, r=b2hl_sb[:, half * 512:(half + 1) * 512],
                                p=ps: e.matmul(p[:, :], onesb_sb[:, :], r,
                                               start=False, stop=True))
                wait("vector", "pe", tpe_y)
                tdv_y = 0
                for half in range(2):
                    tdv_y = opi("vector", "dv", lambda e,
                                d=yt_sb[:, islot % 2, half * 512:(half + 1) * 512],
                                p=ybanks[half]: e.tensor_copy(d, p[:, :]))
                y_evac[islot % 2] = tdv_y
                wait("scalar", "dv", tdv_y)
                dma("scalar", "syi%d" % islot,
                    yg_i.rearrange("(t d) -> t d", d=D)[islot * 128:(islot + 1) * 128, :],
                    yt_sb[:, islot % 2, :])
        stD.close()

        mark("E")
        stE = ExitStack()
        ytok_sb = stE.enter_context(nc.sbuf_tensor("ytok_sb", [128, 16, D], BF16))
        x2_sb = stE.enter_context(nc.sbuf_tensor("x2_sb", [128, D], F32))
        sq2_sb = stE.enter_context(nc.sbuf_tensor("sq2_sb", [128, D], F32))

        for ii in range(16):
            wait("gpsimd", "syi%d" % ii, 16)
        t_gss = opi("gpsimd", "cc", lambda e: e.nop())
        t_agy = opi("gpsimd", "cc", lambda e: e.collective_compute(
            "AllGather", OP.bypass, replica_groups=[list(range(8))],
            ins=[yg_i.ap().opt()], outs=[yg_o.ap().opt()])._wait_ge(sems["cc"], t_gss))
        wait("gpsimd", "cc", t_agy)
        phase_barrier(["gpsimd", "vector", "scalar", "tensor", "sync"])
        for kk in range(4):
            gname = "gyg%d" % kk
            cnt[gname] += 16
            semh = sems[gname]
            emit("gpsimd", lambda e, k_=kk, sh=semh: e.dma_gather(
                ytok_sb[:, k_ * 4:(k_ + 1) * 4, :], yg_o.rearrange("(t d) -> t d", d=D),
                gidx_sb[:, k_ * 32:(k_ + 1) * 32], 512, 512, D).then_inc(sh, 16))

        for tb in range(16):
            if tb > 0:
                wait("sync", "soute" if (tb - 1) % 2 == 0 else "souto",
                     16 * ((tb - 1) // 2 + 1))
            t_x1r = dma("sync", "lx1", x2_sb[:, :], x1f[tb * 128:(tb + 1) * 128, :])
            wait("vector", "lx1", t_x1r)
            wait("vector", "gyg%d" % (tb // 4), 16)
            opi("vector", "dv", lambda e, tb_=tb: e.scalar_tensor_tensor(
                x2_sb[:, :], ytok_sb[:, tb_, :], gact_tm[:, tb_:tb_ + 1], x2_sb[:, :],
                OP.mult, OP.add))
            opi("vector", "dv", lambda e: e.reduce_sum(
                lnv_sb[:, 0:1], x2_sb[:, :], axis=AX))
            opi("vector", "dv", lambda e: e.tensor_scalar(
                lnv_sb[:, 1:2], lnv_sb[:, 0:1], 1.0 / D, None, OP.mult))
            opi("vector", "dv", lambda e: e.scalar_tensor_tensor(
                sq2_sb[:, :], x2_sb[:, :], 1.0, x2_sb[:, :], OP.mult, OP.mult,
                accum_out=lnv_sb[:, 2:3]))
            opi("vector", "dv", lambda e: e.tensor_scalar(
                lnv_sb[:, 3:4], lnv_sb[:, 2:3], 1.0 / D, EPS, OP.mult, OP.add))
            opi("vector", "dv", lambda e: e.tensor_tensor(
                lnv_sb[:, 4:5], lnv_sb[:, 1:2], lnv_sb[:, 1:2], OP.mult))
            t_var2 = opi("vector", "dv", lambda e: e.tensor_sub(
                lnv_sb[:, 3:4], lnv_sb[:, 3:4], lnv_sb[:, 4:5]))
            wait("scalar", "dv", t_var2)
            opi("scalar", "ac", lambda e: e.activation(
                lnv_sb[:, 5:6], lnv_sb[:, 3:4], AF.Ln, bias=0.0, scale=1.0))
            t_rs2 = opi("scalar", "ac", lambda e: e.activation(
                lnv_sb[:, 6:7], lnv_sb[:, 5:6], AF.Exp, bias=0.0, scale=-0.5))
            wait("vector", "ac", t_rs2)
            t_x2 = opi("vector", "dv", lambda e: e.tensor_scalar(
                x2_sb[:, :], x2_sb[:, :], lnv_sb[:, 1:2], lnv_sb[:, 6:7],
                OP.subtract, OP.mult))
            wait("scalar", "dv", t_x2)
            dma("scalar", "soute" if tb % 2 == 0 else "souto",
                out_t[tb * 128:(tb + 1) * 128, :], x2_sb[:, :])
        stE.close()

        wait("scalar", "soute", cnt["soute"])
        wait("scalar", "souto", cnt["souto"])
        for nm in ("gxg0", "gxg1", "gxg2", "gxg3", "gyg0", "gyg1", "gyg2", "gyg3",
                   "gw1", "gw2", "gpc", "gsc", "gpw", "gpwt"):
            wait("gpsimd", nm, cnt[nm])
        wait("gpsimd", "cc", cnt["cc"])
        for nm in ("ldc", "lx0e", "lx0o", "lb2", "lx1", "srt", "sxf", "sxg"):
            wait("sync", nm, cnt[nm])
        wait("scalar", "sx0e", cnt["sx0e"])
        wait("scalar", "sx0o", cnt["sx0o"])
        for ii in range(16):
            wait("scalar", "syi%d" % ii, cnt["syi%d" % ii])
        wait("scalar", "sct", cnt["sct"])

        if PHASE_LIMIT in marks:
            cut, csnap = marks[PHASE_LIMIT]
            for k in plan:
                del plan[k][cut[k]:]
            eng_for = {"soute": "scalar", "souto": "scalar", "sct": "scalar",
                       "srt": "scalar", "sxf": "scalar", "sxg": "scalar",
                       "sx0e": "scalar", "sx0o": "scalar",
                       "ldc": "sync", "lx0e": "sync", "lx0o": "sync",
                       "lb2": "sync", "lx1": "sync", "lrt": "sync",
                       "cc": "gpsimd"}
            for nm, v in csnap.items():
                if v > 0 and not nm.startswith(("pe", "dv", "ac")):
                    eng = eng_for.get(nm, "gpsimd")
                    if nm.startswith("syi"):
                        eng = "scalar"
                    semh = sems[nm]
                    plan[eng].append(lambda e, s=semh, vv=v: e.wait_ge(s, vv))
        blk = ex(nc.Block())

        @blk.tensor
        def _(e):
            for fn in plan["tensor"]:
                fn(e)

        @blk.vector
        def _(e):
            for fn in plan["vector"]:
                fn(e)

        @blk.scalar
        def _(e):
            for fn in plan["scalar"]:
                fn(e)

        @blk.gpsimd
        def _(e):
            for fn in plan["gpsimd"]:
                fn(e)

        @blk.sync
        def _(e):
            for fn in plan["sync"]:
                fn(e)

    nc.compile()
    return nc


_CACHE = {}


def kernel(L0_attn_x, L0_attn_W_Q, L0_attn_W_K, L0_attn_W_V, L0_attn_W_O,
           L0_ffn_router_w, L0_ffn_w1, L0_ffn_b1, L0_ffn_w2, L0_ffn_b2):
    x = np.ascontiguousarray(np.asarray(L0_attn_x, dtype=np.float32))
    wq = np.ascontiguousarray(np.asarray(L0_attn_W_Q, np.float32).reshape(D, D).T)
    wk = np.ascontiguousarray(np.asarray(L0_attn_W_K, np.float32).reshape(D, D).T)
    wv = np.ascontiguousarray(np.asarray(L0_attn_W_V, np.float32).reshape(D, D).T)
    wo = np.ascontiguousarray(
        np.asarray(L0_attn_W_O, np.float32).transpose(0, 2, 1).reshape(D, D))
    rw = np.ascontiguousarray(np.asarray(L0_ffn_router_w, np.float32))
    w1 = np.ascontiguousarray(np.asarray(L0_ffn_w1, np.float32))
    b1 = np.ascontiguousarray(np.asarray(L0_ffn_b1, np.float32))
    w2 = np.ascontiguousarray(np.asarray(L0_ffn_w2, np.float32))
    b2 = np.ascontiguousarray(np.asarray(L0_ffn_b2, np.float32))

    if "nc" not in _CACHE:
        _CACHE["nc"] = build_nc()
    nc = _CACHE["nc"]

    in_maps = []
    sidx = np.arange(S)
    for c in range(8):
        masklt = np.zeros((8, 16), np.float32)
        masklt[:c, :] = 1.0
        onehot = np.zeros((8, 8), np.float32)
        onehot[c, :] = 1.0
        ramp = np.zeros((128, 16), np.float32)
        ramp[sidx % 128, sidx // 128] = c * S + sidx + 1
        in_maps.append({
            "x_in": x[c], "wq_in": wq, "wk_in": wk, "wv_in": wv, "wo_in": wo,
            "rw_in": rw, "w1_in": w1[c], "b1_in": b1[c], "w2_in": w2[c],
            "b2_in": b2[c:c + 1], "masklt_in": masklt, "onehot_in": onehot,
            "ramp_in": ramp,
        })
    res = run_bass_kernel_spmd(nc, in_maps, core_ids=list(range(8)))
    return np.stack([res.results[c]["out"] for c in range(8)], axis=0)


# version 11
# speedup vs baseline: 1.2075x; 1.0792x over previous
import os as _os
import numpy as np
import concourse.bacc as bacc_mod
import concourse.mybir as mybir
from concourse.bass_utils import run_bass_kernel_spmd

F32 = mybir.dt.float32
BF16 = mybir.dt.bfloat16
I16 = mybir.dt.int16
I32 = mybir.dt.int32
OP = mybir.AluOpType
AF = mybir.ActivationFunctionType
AX = mybir.AxisListType.X

S = 2048
D = 1024
NH = 16
F = 4096
T = 16384
SEG = 2080
EPS = 1e-5
BIGE = 65536.0


PHASE_LIMIT = _os.environ.get("KPHASE", "Z")
KDEBUG = _os.environ.get("KDEBUG", "") == "1"


def build_nc():
    nc = bacc_mod.Bacc("TRN2", target_bir_lowering=False)

    x_in = nc.dram_tensor("x_in", [S, D], F32, kind="ExternalInput").ap()
    wq_in = nc.dram_tensor("wq_in", [D, D], F32, kind="ExternalInput").ap()
    wk_in = nc.dram_tensor("wk_in", [D, D], F32, kind="ExternalInput").ap()
    wv_in = nc.dram_tensor("wv_in", [D, D], F32, kind="ExternalInput").ap()
    wo_in = nc.dram_tensor("wo_in", [D, D], F32, kind="ExternalInput").ap()
    rw_in = nc.dram_tensor("rw_in", [D, 8], F32, kind="ExternalInput").ap()
    w1_in = nc.dram_tensor("w1_in", [D, F], F32, kind="ExternalInput").ap()
    b1_in = nc.dram_tensor("b1_in", [F], F32, kind="ExternalInput").ap()
    w2_in = nc.dram_tensor("w2_in", [F, D], F32, kind="ExternalInput").ap()
    b2_in = nc.dram_tensor("b2_in", [1, D], F32, kind="ExternalInput").ap()
    masklt_in = nc.dram_tensor("masklt_in", [8, 16], F32, kind="ExternalInput").ap()
    onehot_in = nc.dram_tensor("onehot_in", [8, 8], F32, kind="ExternalInput").ap()
    ramp_in = nc.dram_tensor("ramp_in", [128, 16], F32, kind="ExternalInput").ap()
    out_t = nc.dram_tensor("out", [S, D], F32, kind="ExternalOutput").ap()

    x1f = nc.dram_tensor("x1f", [S, D], F32)
    xg_i = nc.dram_tensor("xg_i", [S * D], BF16)
    xg_o = nc.dram_tensor("xg_o", [T * D], BF16)
    cnt_i = nc.dram_tensor("cnt_i", [8], F32)
    cnt_o = nc.dram_tensor("cnt_o", [64], F32)
    listtab = nc.dram_tensor("listtab", [8 * SEG, 64], F32)
    seg_t = nc.dram_tensor("seg_t", [SEG * 64], F32)
    rt = nc.dram_tensor("rt", [2, S], F32)
    yg_i = nc.dram_tensor("yg_i", [S * D], BF16)
    yg_o = nc.dram_tensor("yg_o", [T * D], BF16)

    ident = nc.inline_tensor(np.eye(128, dtype=np.float32), name="cident").ap()
    sl16_np = np.zeros((16, 16), np.float32)
    for k in range(16):
        sl16_np[k, k + 1:] = 1.0
    sl16_d = nc.inline_tensor(sl16_np, name="csl16").ap()
    masks_np = np.zeros((128, 4, 512), np.float32)
    for dd in range(4):
        for p in range(128):
            if p + dd * 128 < 512:
                masks_np[p, dd, p + dd * 128:] = 1.0
    masks_d = nc.inline_tensor(masks_np, name="cmasks").ap()
    erowp_d = nc.inline_tensor(
        np.tile(np.arange(8, dtype=np.float32)[None, :] + BIGE, (128, 1)),
        name="cerowp").ap()
    erow_d = nc.inline_tensor(
        np.tile(np.arange(8, dtype=np.float32)[None, :], (128, 1)), name="cerow").ap()
    k512_d = nc.inline_tensor(
        np.array([[0.0, 512.0, 1024.0, 1536.0, 0.0, 0.0, 0.0, 0.0]], np.float32),
        name="ck512").ap()

    from contextlib import ExitStack
    with ExitStack() as stk:
        ex = stk.enter_context

        ident_sb = ex(nc.sbuf_tensor("ident_sb", [128, 128], F32))
        ones_sb = ex(nc.sbuf_tensor("ones_sb", [128, 64], F32))
        onesb_sb = ex(nc.sbuf_tensor("onesb_sb", [2, 128], BF16))
        erow_sb = ex(nc.sbuf_tensor("erow_sb", [128, 8], F32))
        erowp_sb = ex(nc.sbuf_tensor("erowp_sb", [128, 8], F32))
        k512_sb = ex(nc.sbuf_tensor("k512_sb", [1, 8], F32))
        gate_tm = ex(nc.sbuf_tensor("gate_tm", [128, 16], F32))
        gact_tm = ex(nc.sbuf_tensor("gact_tm", [128, 16], F32))
        emask_tm = ex(nc.sbuf_tensor("emask_tm", [128, 16, 8], F32))
        small = ex(nc.sbuf_tensor("small", [128, 32], F32))
        lnv_sb = ex(nc.sbuf_tensor("lnv_sb", [128, 8], F32))
        posm_sb = ex(nc.sbuf_tensor("posm_sb", [128, 128], I16))
        gidx_sb = ex(nc.sbuf_tensor("gidx_sb", [128, 128], I16))
        xidx_sb = ex(nc.sbuf_tensor("xidx_sb", [128, 128], I16))
        ccnt_sb = ex(nc.sbuf_tensor("ccnt_sb", [1, 8], I32))
        ccntf_sb = ex(nc.sbuf_tensor("ccntf_sb", [1, 8], F32))
        b1_sb = ex(nc.sbuf_tensor("b1_sb", [128, 32], F32))
        b2hl_sb = ex(nc.sbuf_tensor("b2hl_sb", [2, D], BF16))
        meta_sb = ex(nc.sbuf_tensor("meta_sb", [8, 24], F32))
        rvec = ex(nc.sbuf_tensor("rvec", [16, 64], F32))
        ramp_sb = ex(nc.sbuf_tensor("ramp_sb", [128, 16], F32))
        cnts_ag = ex(nc.sbuf_tensor("cnts_ag", [8, 8], F32))

        sc_ps = [ex(nc.psum_tensor("sc_ps%d" % i, [128, 512], F32)) for i in range(2)]
        z_ps = [ex(nc.psum_tensor("z_ps%d" % i, [128, 512], F32)) for i in range(2)]
        b_ps = ex(nc.psum_tensor("b_ps", [64, 512], F32))
        ao_ps = [ex(nc.psum_tensor("ao_ps%d" % i, [128, 512], F32)) for i in range(2)]
        ms_ps = ex(nc.psum_tensor("ms_ps", [128, 512], F32))

        sem_names = ["syi%d" % i for i in range(16)] + ["ldc", "lx0e", "lx0o", "lb2", "lx1", "gpw", "gpwt", "gpc",
                     "gsc", "gw1", "gw2", "lrt",
                     "gxg0", "gxg1", "gxg2", "gxg3", "gyg0", "gyg1", "gyg2", "gyg3",
                     "sx0e", "sx0o", "sxf", "sxg", "sct", "srt",
                     "soute", "souto",
                     "pe", "dv", "ac", "cc"]
        sems = {n: ex(nc.semaphore(n)) for n in sem_names}
        cnt = {n: 0 for n in sem_names}
        plan = {k: [] for k in ("tensor", "vector", "scalar", "gpsimd", "sync")}

        def emit(eng, fn):
            plan[eng].append(fn)

        def dma(eng, sem, out_ap, in_ap, **kw):
            cnt[sem] += 16
            semh = sems[sem]
            emit(eng, lambda e, o=out_ap, i=in_ap: e.dma_start(out=o, in_=i, **kw)
                 .then_inc(semh, 16))
            return cnt[sem]

        def wait(eng, sem, val):
            semh = sems[sem]
            emit(eng, lambda e, v=val: e.wait_ge(semh, v))

        ENG_SEM = {"vector": "dv", "scalar": "ac"}

        def opi(eng, sem, fn):
            own = ENG_SEM.get(eng)
            if own is not None and cnt[own] > 0:
                semo = sems[own]
                emit(eng, lambda e, v=cnt[own]: e.wait_ge(semo, v))
            cnt[sem] += 1
            semh = sems[sem]
            emit(eng, lambda e, f=fn: f(e).then_inc(semh, 1))
            return cnt[sem]

        def op0(eng, fn):
            emit(eng, lambda e, f=fn: f(e))

        marks = {}

        def mark(name):
            marks[name] = ({k: len(v) for k, v in plan.items()}, dict(cnt))

        def phase_barrier(engines):
            snap = [(nm, cnt[nm]) for nm in sem_names if cnt[nm] > 0]
            for eng in engines:
                for nm, v in snap:
                    wait(eng, nm, v)

        stAB = ExitStack()
        v_tok = stAB.enter_context(nc.sbuf_tensor("v_tok", [128, 16, NH * 65], BF16))
        q_sb = stAB.enter_context(nc.sbuf_tensor("q_sb", [128, 8, S], BF16))
        k_sb = stAB.enter_context(nc.sbuf_tensor("k_sb", [128, 8, S], BF16))
        wo_sb = stAB.enter_context(nc.sbuf_tensor("wo_sb", [128, 8, D], BF16))
        masks_sb = stAB.enter_context(nc.sbuf_tensor("masks_sb", [128, 4, 512], BF16))
        rw_sb = stAB.enter_context(nc.sbuf_tensor("rw_sb", [128, 8, 8], F32))

        stA = ExitStack()
        wtmp_sb = stA.enter_context(nc.sbuf_tensor("wtmp_sb", [128, 8, D], BF16))
        x0t_sb = stA.enter_context(nc.sbuf_tensor("x0t_sb", [128, 8, S], BF16))
        x0_sb = stA.enter_context(nc.sbuf_tensor("x0_sb", [128, 2, D], F32))
        b2s_sb = stA.enter_context(nc.sbuf_tensor("b2s_sb", [1, 2 * D], F32))
        b1t_sb = stA.enter_context(nc.sbuf_tensor("b1t_sb", [32, 128], F32))
        b2w_sb = stA.enter_context(nc.sbuf_tensor("b2w_sb", [1, 2, D], BF16))

        dma("sync", "ldc", ident_sb[:, :], ident[:, :])
        dma("sync", "ldc", erow_sb[:, :], erow_d[:, :])
        dma("sync", "ldc", erowp_sb[:, :], erowp_d[:, :])
        dma("sync", "ldc", k512_sb[:, :], k512_d[:, :])
        dma("sync", "ldc", rw_sb[:, :, :], rw_in.rearrange("(c p) e -> p c e", p=128))
        dma("sync", "ldc", meta_sb[:, 0:16], masklt_in[:, :])
        dma("sync", "ldc", meta_sb[:, 16:24], onehot_in[:, :])
        dma("sync", "ldc", b1t_sb[:, :], b1_in.rearrange("(a p) -> a p", p=128))
        dma("sync", "ldc", b2s_sb[0:1, 0:D], b2_in[:, :])
        dma("sync", "ldc", rvec[0:16, 48:64], sl16_d[:, :])
        T_LDC = dma("sync", "ldc", ramp_sb[:, :], ramp_in[:, :])
        assert T_LDC == 176
        dma("gpsimd", "gpw", masks_sb[:, :, :], masks_d[:, :, :])
        T_GPW = dma("gpsimd", "gpw", wo_sb[:, :, :], wo_in.rearrange("(c p) e -> p c e", p=128))

        opi("vector", "dv", lambda e: e.memset(ones_sb[:, :], 1.0))
        opi("vector", "dv", lambda e: e.memset(onesb_sb[:, :], 1.0))
        t_vones = opi("vector", "dv", lambda e: e.memset(
            v_tok.ap().rearrange("p a (h c) -> p a h c", c=65)[:, :, :, 64:65], 1.0))
        wait("vector", "ldc", T_LDC)
        opi("vector", "dv", lambda e: e.tensor_copy(b2w_sb[0:1, 0, :], b2s_sb[0:1, 0:D]))
        opi("vector", "dv", lambda e: e.tensor_copy(b2s_sb[0:1, D:2 * D], b2w_sb[0:1, 0, :]))
        opi("vector", "dv", lambda e: e.tensor_sub(
            b2s_sb[0:1, D:2 * D], b2s_sb[0:1, 0:D], b2s_sb[0:1, D:2 * D]))
        t_b2pre = opi("vector", "dv", lambda e: e.tensor_copy(
            b2w_sb[0:1, 1, :], b2s_sb[0:1, D:2 * D]))
        wait("sync", "dv", t_b2pre)
        dma("sync", "lb2", b2hl_sb[0:1, :], b2w_sb[0:1, 0, :])
        t_b2l_dma = dma("sync", "lb2", b2hl_sb[1:2, :], b2w_sb[0:1, 1, :])

        wait("tensor", "ldc", T_LDC)
        tpe_b1 = opi("tensor", "pe", lambda e: e.transpose(
            ms_ps[:, 0:32], b1t_sb[:, :], ident_sb[0:32, 0:32]))
        wait("vector", "pe", tpe_b1)
        tdv_b1 = opi("vector", "dv", lambda e: e.tensor_copy(b1_sb[:, :], ms_ps[:, 0:32]))
        wait("tensor", "dv", tdv_b1)
        t_xtr_done = [0] * 16
        for tb in range(16):
            if tb >= 2:
                wait("sync", "dv", t_xtr_done[tb - 2])
            t_ld = dma("sync", "lx0e" if tb % 2 == 0 else "lx0o",
                       x0_sb[:, tb % 2, :], x_in[tb * 128:(tb + 1) * 128, :])
            wait("tensor", "lx0e" if tb % 2 == 0 else "lx0o", 16 * (tb // 2 + 1))
            for dc in range(8):
                tpe = opi("tensor", "pe", lambda e, s=x0_sb[:, tb % 2, dc * 128:(dc + 1) * 128]:
                          e.transpose(ms_ps[:, 0:128], s, ident_sb[:, :]))
                wait("vector", "pe", tpe)
                tdv = opi("vector", "dv", lambda e, d=x0t_sb[:, dc, tb * 128:(tb + 1) * 128]:
                          e.tensor_copy(d, ms_ps[:, 0:128]))
                wait("tensor", "dv", tdv)
            t_xtr_done[tb] = cnt["dv"]

        dma("gpsimd", "gpwt", wtmp_sb[:, :, :],
            wv_in.rearrange("(c p) a -> p c a", p=128))
        wait("tensor", "gpwt", 16)
        wait("tensor", "dv", t_vones)
        last_evac = {0: 0, 1: 0}
        for tb in range(16):
            for half in range(2):
                ps = sc_ps[half]
                if last_evac[half]:
                    wait("tensor", "dv", last_evac[half])
                for dc in range(8):
                    fn = (lambda e, l=x0t_sb[:, dc, tb * 128:(tb + 1) * 128],
                          r=wtmp_sb[:, dc, half * 512:(half + 1) * 512], p=ps,
                          s=(dc == 0), t=(dc == 7):
                          e.matmul(p[:, :], l, r, start=s, stop=t))
                    if dc < 7:
                        op0("tensor", fn)
                    else:
                        tpe = opi("tensor", "pe", fn)
                dst = v_tok.ap().rearrange("p a (h c) -> p a h c", c=65)[
                    :, tb, half * 8:(half + 1) * 8, 0:64]
                wait("vector", "pe", tpe)
                last_evac[half] = opi("vector", "dv",
                                      lambda e, d=dst, p=ps: e.tensor_copy(d, p[:, :]))
        for wi, (w_dram, dsts) in enumerate(((wq_in, q_sb), (wk_in, k_sb))):
            wait("gpsimd", "dv", cnt["dv"])
            dma("gpsimd", "gpwt", wtmp_sb[:, :, :],
                w_dram.rearrange("(c p) a -> p c a", p=128))
            wait("tensor", "gpwt", 16 * (wi + 2))
            for ac in range(8):
                for tt in range(4):
                    ps = sc_ps[tt % 2]
                    if last_evac[tt % 2]:
                        wait("tensor", "dv", last_evac[tt % 2])
                    for dc in range(8):
                        fn = (lambda e, l=wtmp_sb[:, dc, ac * 128:(ac + 1) * 128],
                              r=x0t_sb[:, dc, tt * 512:(tt + 1) * 512], p=ps,
                              s=(dc == 0), t=(dc == 7):
                              e.matmul(p[:, :], l, r, start=s, stop=t))
                        if dc < 7:
                            op0("tensor", fn)
                        else:
                            tpe = opi("tensor", "pe", fn)
                    wait("vector", "pe", tpe)
                    last_evac[tt % 2] = opi(
                        "vector", "dv", lambda e, d=dsts[:, ac, tt * 512:(tt + 1) * 512],
                        p=ps: e.tensor_copy(d, p[:, :]))
        stA.close()

        mark("B")
        strip = stAB.enter_context(nc.sbuf_tensor("strip", [128, 16, 512], BF16))
        z_sb = stAB.enter_context(nc.sbuf_tensor("z_sb", [128, 8, 512], BF16))
        recip_sb = stAB.enter_context(nc.sbuf_tensor("recip_sb", [1, 1024], F32))
        bsb_sb = stAB.enter_context(nc.sbuf_tensor("bsb_sb", [64, 512], F32))
        x1_sb = stAB.enter_context(nc.sbuf_tensor("x1_sb", [128, D], F32))
        x1b_sb = stAB.enter_context(nc.sbuf_tensor("x1b_sb", [128, D], BF16))
        x1t_sb = stAB.enter_context(nc.sbuf_tensor("x1t_sb", [128, 8, 128], F32))
        sq_sb = stAB.enter_context(nc.sbuf_tensor("sq_sb", [128, D], F32))

        wait("vector", "lb2", 32)
        wait("scalar", "lb2", 32)
        wait("tensor", "gpw", T_GPW)
        wait("tensor", "dv", max(last_evac[0], last_evac[1]))
        t_eq_prev = 0
        t_zn_par = {0: 0, 1: 0}
        t_zn_prev = 0
        banks4 = [sc_ps[0], sc_ps[1], ao_ps[0], ao_ps[1]]
        last_sc_exp = {0: 0, 1: 0, 2: 0, 3: 0}
        ao_guard = {2: 0, 3: 0}
        pend = [None]
        t_zn_last = [0]

        def emit_tail():
            t_rc_p, zp_p, rowb_p, ahc_p, par_p = pend[0]
            pend[0] = None
            wait("tensor", "dv", max(t_rc_p, t_zn_last[0]))
            tpe_b = opi("tensor", "pe", lambda e,
                        rp=recip_sb[0:1, par_p * 512:(par_p + 1) * 512]: e.matmul(
                            b_ps[:, :], ones_sb[0:1, 0:64], rp, start=True, stop=True))
            wait("vector", "pe", tpe_b)
            opi("vector", "dv", lambda e: e.tensor_copy(bsb_sb[:, :], b_ps[:, :]))
            t_zn = opi("vector", "dv", lambda e, d=z_sb[rowb_p:rowb_p + 64, ahc_p, :],
                       p=zp_p: e.tensor_tensor(d, p[0:64, :], bsb_sb[:, :], OP.mult))
            t_zn_par[par_p] = t_zn
            t_zn_last[0] = t_zn
            return t_zn

        for j in range(4):
            nch = 4 * j + 4
            for a in range(NH):
                par = (j * 16 + a) % 2
                rowb = 64 * (a % 2)
                ahc = a // 2
                t_exp = [0] * nch
                t_msk = [0] * nch
                for i in range(nch):
                    b4 = i % 4
                    ps = banks4[b4]
                    if b4 >= 2 and ao_guard[b4]:
                        wait("tensor", "dv", ao_guard[b4])
                        ao_guard[b4] = 0
                    if last_sc_exp[b4]:
                        wait("tensor", "ac", last_sc_exp[b4])
                    tpe = opi("tensor", "pe", lambda e,
                              l=q_sb[rowb:rowb + 64, ahc, i * 128:(i + 1) * 128],
                              r=k_sb[rowb:rowb + 64, ahc, j * 512:(j + 1) * 512],
                              p=ps: e.matmul(p[:, :], l, r, start=True, stop=True))
                    wait("scalar", "pe", tpe)
                    t_exp[i] = opi("scalar", "ac", lambda e, d=strip[:, i, :], p=ps:
                                   e.activation(d, p[:, :], AF.Exp, bias=0.0, scale=0.125))
                    last_sc_exp[b4] = t_exp[i]
                    if i >= 4 * j:
                        wait("vector", "ac", t_exp[i])
                        t_msk[i] = opi("vector", "dv", lambda e, s=strip[:, i, :],
                                       m=masks_sb[:, i - 4 * j, :]:
                                       e.tensor_tensor(s, s, m, OP.mult))
                zp = z_ps[par]
                if t_zn_par[par]:
                    wait("tensor", "dv", t_zn_par[par])
                for i in range(nch):
                    if t_msk[i]:
                        wait("tensor", "dv", t_msk[i])
                    else:
                        wait("tensor", "ac", t_exp[i])
                    fn = (lambda e, l=v_tok[:, i, a * 65:(a + 1) * 65],
                          r=strip[:, i, :], p=zp, s=(i == 0), t=(i == nch - 1):
                          e.matmul(p[0:65, :], l, r, start=s, stop=t))
                    if i < nch - 1:
                        op0("tensor", fn)
                    else:
                        tpe_z = opi("tensor", "pe", fn)
                wait("vector", "pe", tpe_z)
                t_rc = opi("vector", "dv", lambda e, p=zp, pr=par:
                           e.reciprocal(recip_sb[0:1, pr * 512:(pr + 1) * 512],
                                        p[64:65, :]))
                if pend[0] is not None:
                    t_zn_prev = emit_tail()
                pend[0] = (t_rc, zp, rowb, ahc, par)
            t_zn_prev = emit_tail()

            wait("tensor", "gpw", T_GPW)
            wait("tensor", "dv", t_zn_prev)
            for s4 in range(4):
                tb = j * 4 + s4
                for half in range(2):
                    ps = ao_ps[half]
                    for ahc in range(8):
                        fn = (lambda e, l=z_sb[:, ahc, s4 * 128:(s4 + 1) * 128],
                              r=wo_sb[:, ahc, half * 512:(half + 1) * 512], p=ps,
                              s=(ahc == 0), t=(ahc == 7):
                              e.matmul(p[:, :], l, r, start=s, stop=t))
                        if ahc < 7:
                            op0("tensor", fn)
                        else:
                            tpe_ao = opi("tensor", "pe", fn)
                if tb > 0:
                    wait("scalar", "pe", t_rtr_last)
                    wait("scalar", "sxf", 16 * tb)
                sx0 = "sx0e" if tb % 2 == 0 else "sx0o"
                t_x0r = dma("scalar", sx0, x1_sb[:, :], x_in[tb * 128:(tb + 1) * 128, :])
                wait("vector", sx0, 16 * (tb // 2 + 1))
                wait("vector", "pe", tpe_ao)
                for half in range(2):
                    opi("vector", "dv", lambda e, h=half: e.tensor_tensor(
                        x1_sb[:, h * 512:(h + 1) * 512], x1_sb[:, h * 512:(h + 1) * 512],
                        ao_ps[h][:, :], OP.add))
                t_res = cnt["dv"]
                wait("tensor", "dv", t_res)
                opi("vector", "dv", lambda e: e.reduce_sum(
                    lnv_sb[:, 0:1], x1_sb[:, :], axis=AX))
                opi("vector", "dv", lambda e: e.tensor_scalar(
                    lnv_sb[:, 1:2], lnv_sb[:, 0:1], 1.0 / D, None, OP.mult))
                t_sq = opi("vector", "dv", lambda e: e.scalar_tensor_tensor(
                    sq_sb[:, :], x1_sb[:, :], 1.0, x1_sb[:, :], OP.mult, OP.mult,
                    accum_out=lnv_sb[:, 2:3]))
                opi("vector", "dv", lambda e: e.tensor_scalar(
                    lnv_sb[:, 3:4], lnv_sb[:, 2:3], 1.0 / D, EPS, OP.mult, OP.add))
                opi("vector", "dv", lambda e: e.tensor_tensor(
                    lnv_sb[:, 4:5], lnv_sb[:, 1:2], lnv_sb[:, 1:2], OP.mult))
                t_var = opi("vector", "dv", lambda e: e.tensor_sub(
                    lnv_sb[:, 3:4], lnv_sb[:, 3:4], lnv_sb[:, 4:5]))
                wait("scalar", "dv", t_var)
                opi("scalar", "ac", lambda e: e.activation(
                    lnv_sb[:, 5:6], lnv_sb[:, 3:4], AF.Ln, bias=0.0, scale=1.0))
                t_rs = opi("scalar", "ac", lambda e: e.activation(
                    lnv_sb[:, 6:7], lnv_sb[:, 5:6], AF.Exp, bias=0.0, scale=-0.5))
                wait("vector", "ac", t_rs)
                t_x1 = opi("vector", "dv", lambda e: e.tensor_scalar(
                    x1_sb[:, :], x1_sb[:, :], lnv_sb[:, 1:2], lnv_sb[:, 6:7],
                    OP.subtract, OP.mult))
                wait("scalar", "dv", t_x1)
                dma("scalar", "sxf", x1f[tb * 128:(tb + 1) * 128, :], x1_sb[:, :])
                if tb > 0:
                    wait("vector", "sxg", 16 * tb)
                t_x1b = opi("vector", "dv", lambda e: e.tensor_copy(
                    x1b_sb[:, :], x1_sb[:, :]))
                wait("scalar", "dv", t_x1b)
                dma("scalar", "sxg",
                    xg_i.rearrange("(t d) -> t d", d=D)[tb * 128:(tb + 1) * 128, :],
                    x1b_sb[:, :])
                wait("tensor", "dv", t_x1)
                for dc in range(8):
                    tpe_t = opi("tensor", "pe", lambda e,
                                s=x1_sb[:, dc * 128:(dc + 1) * 128]:
                                e.transpose(ms_ps[:, 0:128], s, ident_sb[:, :]))
                    wait("vector", "pe", tpe_t)
                    tdv_t = opi("vector", "dv", lambda e, d=x1t_sb[:, dc, :]:
                                e.tensor_copy(d, ms_ps[:, 0:128]))
                    wait("tensor", "dv", tdv_t)
                for dc in range(8):
                    fn = (lambda e, l=x1t_sb[:, dc, :], r=rw_sb[:, dc, :],
                          s=(dc == 0), t=(dc == 7):
                          e.matmul(ms_ps[:, 128:136], l, r, start=s, stop=t))
                    if dc < 7:
                        op0("tensor", fn)
                    else:
                        tpe_rt = opi("tensor", "pe", fn)
                t_rtr_last = tpe_rt
                wait("vector", "pe", tpe_rt)
                opi("vector", "dv", lambda e: e.reduce_max(
                    small[:, 12:13], ms_ps[:, 128:136], axis=AX, negate=True))
                t_nm = cnt["dv"]
                wait("scalar", "dv", t_nm)
                t_ex = opi("scalar", "ac", lambda e: e.activation(
                    small[:, 24:32], ms_ps[:, 128:136], AF.Exp, bias=small[:, 12:13],
                    scale=1.0, accum_out=small[:, 8:9]))
                wait("vector", "ac", t_ex)
                opi("vector", "dv", lambda e, tb_=tb: e.reciprocal(
                    gate_tm[:, tb_:tb_ + 1], small[:, 8:9]))
                opi("vector", "dv", lambda e: e.tensor_scalar(
                    small[:, 0:8], ms_ps[:, 128:136], small[:, 12:13], None, OP.add))
                opi("vector", "dv", lambda e: e.tensor_scalar(
                    small[:, 0:8], small[:, 0:8], 0.0, -BIGE, OP.is_ge, OP.mult))
                opi("vector", "dv", lambda e: e.tensor_tensor(
                    small[:, 0:8], small[:, 0:8], erowp_sb[:, :], OP.add))
                opi("vector", "dv", lambda e: e.tensor_reduce(
                    small[:, 9:10], small[:, 0:8], AX, OP.min))
                t_eq_prev = opi("vector", "dv", lambda e, tb_=tb: e.tensor_scalar(
                    emask_tm[:, tb_, :], erow_sb[:, :], small[:, 9:10], None,
                    OP.is_equal))
            ao_guard[2] = t_res
            ao_guard[3] = t_res

        mark("C")
        stC = ExitStack()
        emt = stC.enter_context(nc.sbuf_tensor("emt", [16, 8, 128], F32))
        posg = stC.enter_context(nc.sbuf_tensor("posg", [16, 8, 128], F32))
        racc = stC.enter_context(nc.sbuf_tensor("racc", [16, 4, 128], F32))
        rta_sb = stC.enter_context(nc.sbuf_tensor("rta_sb", [128, 2, 16], F32))
        vals_sb = stC.enter_context(nc.sbuf_tensor("vals_sb", [128, 16, 64], F32))
        zer_sb = stC.enter_context(nc.sbuf_tensor("zer_sb", [128, 2080], F32))
        lsta_sb = stC.enter_context(nc.sbuf_tensor("lsta_sb", [128, 16, 64], F32))

        wait("gpsimd", "sxg", 256)
        t_agx = opi("gpsimd", "cc", lambda e: e.collective_compute(
            "AllGather", OP.bypass, replica_groups=[list(range(8))],
            ins=[xg_i.ap().opt()], outs=[xg_o.ap().opt()])._wait_ge(sems["sxg"], 256))


        wait("tensor", "dv", t_eq_prev)
        for e8 in range(8):
            tpe = opi("tensor", "pe", lambda e, s=emask_tm[:, :, e8]:
                      e.transpose(ms_ps[0:16, 0:128], s, ident_sb[:, :]))
            wait("vector", "pe", tpe)
            tdv = opi("vector", "dv", lambda e, d=emt[:, e8, :]:
                      e.tensor_copy(d, ms_ps[0:16, 0:128]))
            wait("tensor", "dv", tdv)
        for e8 in range(8):
            opi("vector", "dv", lambda e, e_=e8: e.reduce_sum(
                rvec[0:16, e_:e_ + 1], emt[:, e_, :], axis=AX))
        t_tot = cnt["dv"]
        wait("tensor", "dv", t_tot)
        tpe_c = opi("tensor", "pe", lambda e: e.matmul(
            ms_ps[0:8, 136:137], rvec[0:16, 0:8], ones_sb[0:16, 0:1],
            start=True, stop=True))
        wait("vector", "pe", tpe_c)
        t_cv = opi("vector", "dv", lambda e: e.tensor_copy(
            small[0:8, 10:11], ms_ps[0:8, 136:137]))
        wait("scalar", "dv", t_cv)
        dma("scalar", "sct", cnt_i.rearrange("(a c) -> a c", c=1), small[0:8, 10:11])
        wait("gpsimd", "sct", 16)
        t_agc = opi("gpsimd", "cc", lambda e: e.collective_compute(
            "AllGather", OP.bypass, replica_groups=[list(range(8))],
            ins=[cnt_i.ap().opt()], outs=[cnt_o.ap().opt()])._wait_ge(sems["sct"], 16))
        wait("gpsimd", "cc", t_agc)
        mark("C1")
        t_gpc = dma("gpsimd", "gpc", cnts_ag[:, :], cnt_o.rearrange("(c e) -> c e", e=8))
        wait("gpsimd", "gpc", t_gpc)
        wait("tensor", "gpc", t_gpc)
        op0("tensor", lambda e: e.matmul(
            ms_ps[0:16, 144:152], rvec[0:16, 48:64], rvec[0:16, 0:8],
            start=True, stop=False))
        tpe_o = opi("tensor", "pe", lambda e: e.matmul(
            ms_ps[0:16, 144:152], meta_sb[0:8, 0:16], cnts_ag[:, :],
            start=False, stop=True))
        wait("vector", "pe", tpe_o)
        opi("vector", "dv", lambda e: e.tensor_copy(
            rvec[0:16, 8:16], ms_ps[0:16, 144:152]))
        opi("vector", "dv", lambda e: e.memset(racc[:, 3, :], 0.0))
        for e8 in range(8):
            opi("vector", "dv", lambda e, e_=e8: e.tensor_tensor_scan(
                posg[:, e_, :], emt[:, e_, :], racc[0:16, 3, :],
                rvec[0:16, 8 + e_:9 + e_], OP.add, OP.add))
        opi("vector", "dv", lambda e: e.memset(racc[:, 0, :], 0.0))
        opi("vector", "dv", lambda e: e.memset(racc[:, 1, :], 0.0))
        opi("vector", "dv", lambda e: e.memset(racc[:, 2, :], 0.0))
        t_fit = 0
        for e8 in range(8):
            opi("vector", "dv", lambda e, e_=e8: e.tensor_scalar(
                racc[:, 3, :], posg[:, e_, :], float(e_ * SEG - 1),
                float(e_ * SEG + SEG - 1), OP.add, OP.min))
            opi("vector", "dv", lambda e, e_=e8: e.tensor_tensor(
                racc[:, 3, :], racc[:, 3, :], emt[:, e_, :], OP.mult))
            opi("vector", "dv", lambda e: e.tensor_tensor(
                racc[:, 0, :], racc[:, 0, :], racc[:, 3, :], OP.add))
            opi("vector", "dv", lambda e, e_=e8: e.tensor_scalar(
                racc[:, 3, :], posg[:, e_, :], float(e_ * 2048 - 1),
                float(e_ * 2048 + 2047), OP.add, OP.min))
            opi("vector", "dv", lambda e, e_=e8: e.tensor_tensor(
                racc[:, 3, :], racc[:, 3, :], emt[:, e_, :], OP.mult))
            opi("vector", "dv", lambda e: e.tensor_tensor(
                racc[:, 1, :], racc[:, 1, :], racc[:, 3, :], OP.add))
            opi("vector", "dv", lambda e, e_=e8: e.tensor_scalar(
                racc[:, 3, :], posg[:, e_, :], 2048.0, None, OP.is_le))
            opi("vector", "dv", lambda e, e_=e8: e.tensor_tensor(
                racc[:, 3, :], racc[:, 3, :], emt[:, e_, :], OP.mult))
            t_fit = opi("vector", "dv", lambda e: e.tensor_tensor(
                racc[:, 2, :], racc[:, 2, :], racc[:, 3, :], OP.add))
        wait("scalar", "dv", t_fit)
        dma("scalar", "srt", rt[0:1, :].rearrange("x (p c) -> p x c", c=128),
            racc[:, 0:1, :])
        dma("scalar", "srt", rt[1:2, :].rearrange("x (p c) -> p x c", c=128),
            racc[:, 1:2, :])
        wait("sync", "srt", 32)
        mark("C2")
        dma("sync", "lrt", rta_sb[:, :, :],
            rt.rearrange("x (b g) -> b x g", g=16))
        wait("tensor", "lrt", 16)
        mark("C2a")
        for which, dst_i16 in ((0, posm_sb), (1, gidx_sb)):
            tpe = opi("tensor", "pe", lambda e, s=rta_sb[:, which, :]:
                      e.transpose(ms_ps[0:16, 0:128], s, ident_sb[:, :]))
            wait("vector", "pe", tpe)
            tdv = opi("vector", "dv", lambda e, d=dst_i16[0:16, :]:
                      e.tensor_copy(d, ms_ps[0:16, 0:128]))
            wait("tensor", "dv", tdv)
        mark("C2b")
        tpe_f = opi("tensor", "pe", lambda e: e.transpose(
            ms_ps[:, 384:400], racc[0:16, 2, :], ident_sb[0:16, 0:16]))
        wait("vector", "pe", tpe_f)
        t_ge = opi("vector", "dv", lambda e: e.tensor_tensor(
            gact_tm[:, :], gate_tm[:, :], ms_ps[:, 384:400], OP.mult))
        if KDEBUG:
            dbgs = {
                "dbg_posm": (posm_sb[0:16, :], I16, [16, 128]),
                "dbg_gidx": (gidx_sb[0:16, :], I16, [16, 128]),
                "dbg_xon": (None, None, None),
            }
            del dbgs["dbg_xon"]
            dbg_list = [
                ("dbg_posm", posm_sb[0:16, :], I16, [16, 128]),
                ("dbg_gidx", gidx_sb[0:16, :], I16, [16, 128]),
                ("dbg_racc", racc[:, :, :], F32, [16, 4, 128]),
                ("dbg_posg", posg[:, :, :], F32, [16, 8, 128]),
                ("dbg_emt", emt[:, :, :], F32, [16, 8, 128]),
                ("dbg_rvec", rvec[0:16, 0:16], F32, [16, 16]),
                ("dbg_cnts", cnts_ag[:, :], F32, [8, 8]),
                ("dbg_small", small[:, 8:10], F32, [128, 2]),
                ("dbg_gate", gate_tm[:, :], F32, [128, 16]),
                ("dbg_gact", gact_tm[:, :], F32, [128, 16]),
                ("dbg_emask", emask_tm[:, :, :], F32, [128, 16, 8]),
            ]
            wait("sync", "dv", cnt["dv"])
            wait("sync", "ac", cnt["ac"])
            for nm, ap_, dt_, shp in dbg_list:
                od = nc.dram_tensor(nm, shp, dt_, kind="ExternalOutput").ap()
                dma("sync", "lx1", od[tuple(slice(None) for _ in shp)], ap_)
            wait("sync", "lx1", cnt["lx1"])
        mark("C2c")
        wait("tensor", "dv", t_ge)
        tpe_g = opi("tensor", "pe", lambda e: e.matmul(
            ms_ps[0:8, 408:409], cnts_ag[:, :], ones_sb[0:8, 0:1],
            start=True, stop=True))
        wait("vector", "pe", tpe_g)
        t_gt = opi("vector", "dv", lambda e: e.tensor_copy(
            small[0:8, 13:14], ms_ps[0:8, 408:409]))
        wait("tensor", "dv", t_gt)
        mark("C2d")
        tpe_m = opi("tensor", "pe", lambda e: e.matmul(
            ms_ps[0:1, 400:408], small[0:8, 13:14], meta_sb[0:8, 16:24],
            start=True, stop=True))
        wait("vector", "pe", tpe_m)
        opi("vector", "dv", lambda e: e.tensor_copy(small[0:1, 8:16],
                                                    ms_ps[0:1, 400:408]))
        mark("C2e")
        opi("vector", "dv", lambda e: e.tensor_scalar(
            ccntf_sb[:, :], k512_sb[:, :], -1.0, small[0:1, 8:9], OP.mult, OP.add))
        opi("vector", "dv", lambda e: e.tensor_scalar_max(
            ccntf_sb[:, :], ccntf_sb[:, :], 0.0))
        opi("vector", "dv", lambda e: e.tensor_scalar_min(
            ccntf_sb[:, :], ccntf_sb[:, :], 512.0))
        t_ccnt = opi("vector", "dv", lambda e: e.tensor_copy(ccnt_sb[:, :],
                                                             ccntf_sb[:, :]))
        mark("C3")
        opi("vector", "dv", lambda e: e.memset(zer_sb[:, :], 0.0))
        opi("vector", "dv", lambda e: e.memset(vals_sb[:, :, :], 0.0))
        t_vals = opi("vector", "dv", lambda e: e.tensor_copy(vals_sb[:, :, 0],
                                                             ramp_sb[:, :]))
        wait("gpsimd", "dv", cnt["dv"])
        for qq in range(4):
            dma("gpsimd", "gpc",
                listtab.rearrange("t e -> (t e)").rearrange(
                    "(q p f) -> q p f", q=4, p=128)[qq, :, :],
                zer_sb[:, :])
        for rr in range(1, 8):
            dma("gpsimd", "gpc", posm_sb[16 * rr:16 * (rr + 1), :], posm_sb[0:16, :])
            dma("gpsimd", "gpc", gidx_sb[16 * rr:16 * (rr + 1), :], gidx_sb[0:16, :])
        wait("gpsimd", "gpc", cnt["gpc"])
        for kk in range(4):
            cnt["gsc"] += 16
            semh = sems["gsc"]
            emit("gpsimd", lambda e, k_=kk, sh=semh, w=16 * kk: e.dma_scatter_add(
                listtab.ap(), vals_sb[:, k_ * 4:(k_ + 1) * 4, :],
                posm_sb[:, k_ * 32:(k_ + 1) * 32], 512, 512, 64)
                .then_inc(sh, 16)._wait_ge(sh, w))
        wait("gpsimd", "gsc", 64)
        t_rs = opi("gpsimd", "cc", lambda e: e.collective_compute(
            "ReduceScatter", OP.add, replica_groups=[list(range(8))],
            ins=[listtab.ap().rearrange("t e -> (t e)").opt()],
            outs=[seg_t.ap().opt()])._wait_ge(sems["gsc"], 64))
        wait("gpsimd", "cc", t_rs)
        wait("gpsimd", "lb2", 32)
        t_lst = dma("gpsimd", "gpc", lsta_sb[:, :, :],
                    seg_t.rearrange("(b a e) -> b a e", a=16, e=64)[0:128, :, :])
        wait("gpsimd", "gpc", t_lst)
        wait("tensor", "gpc", t_lst)
        tpe_l = opi("tensor", "pe", lambda e: e.transpose(
            ms_ps[0:16, 0:128], lsta_sb[:, :, 0], ident_sb[:, :]))
        wait("vector", "pe", tpe_l)
        t_xi = opi("vector", "dv", lambda e: e.tensor_scalar(
            xidx_sb[0:16, :], ms_ps[0:16, 0:128], -1.0, None, OP.add))
        wait("gpsimd", "dv", t_xi)
        for rr in range(1, 8):
            dma("gpsimd", "gpc", xidx_sb[16 * rr:16 * (rr + 1), :], xidx_sb[0:16, :])
        t_xirep = cnt["gpc"]
        wait("gpsimd", "gpc", t_xirep)
        stC.close()
        stAB.close()

        mark("D")
        stD = ExitStack()
        w1_sb = stD.enter_context(nc.sbuf_tensor("w1_sb", [128, 8, F], BF16))
        w2_sb = stD.enter_context(nc.sbuf_tensor("w2_sb", [128, 32, D], BF16))
        xet_sb = stD.enter_context(nc.sbuf_tensor("xet_sb", [128, 4, 8, 512], BF16))
        h_sb = stD.enter_context(nc.sbuf_tensor("h_sb", [128, 32, 512], BF16))
        yt_sb = stD.enter_context(nc.sbuf_tensor("yt_sb", [128, 2, D], BF16))

        phase_barrier(["gpsimd", "vector", "scalar", "tensor", "sync"])
        dma("gpsimd", "gw1", w1_sb[:, :, :],
            w1_in.rearrange("(c p) f -> p c f", p=128))
        dma("gpsimd", "gw2", w2_sb[:, :, :],
            w2_in.rearrange("(c p) d -> p c d", p=128))
        t_xz = opi("vector", "dv", lambda e: e.memset(xet_sb[:, :, :, :], 0.0))
        wait("gpsimd", "cc", t_agx)
        wait("gpsimd", "dv", t_xz)
        for kk in range(4):
            gname = "gxg%d" % kk
            cnt[gname] += 16
            semh = sems[gname]

            def _mk_gather(e, k_=kk, sh=semh):
                cv = e.value_load(ccnt_sb[0:1, k_:k_ + 1], min_val=0, max_val=512)
                return e.dma_gather(
                    xet_sb[:, k_, :, :], xg_o.rearrange("(t d) -> t d", d=D),
                    xidx_sb[:, k_ * 32:(k_ + 1) * 32], 512, cv, D,
                    transpose=True).then_inc(sh, 16)
            emit("gpsimd", _mk_gather)

        wait("tensor", "gw1", 16)
        wait("tensor", "lb2", 32)
        y_evac = {0: 0, 1: 0}
        t_yst = [0] * 16
        for kk in range(4):
            wait("tensor", "gxg%d" % kk, 16)
            t_hrelu = [0] * 32
            for fb in range(32):
                ps = sc_ps[fb % 2]
                if fb >= 2:
                    wait("tensor", "ac", t_hrelu[fb - 2])
                for dc in range(8):
                    fn = (lambda e, l=w1_sb[:, dc, fb * 128:(fb + 1) * 128],
                          r=xet_sb[:, kk, dc, :], p=ps, s=(dc == 0), t=(dc == 7):
                          e.matmul(p[:, :], l, r, start=s, stop=t))
                    if dc < 7:
                        op0("tensor", fn)
                    else:
                        tpe_h = opi("tensor", "pe", fn)
                wait("scalar", "pe", tpe_h)
                t_hrelu[fb] = opi("scalar", "ac", lambda e, d=h_sb[:, fb, :],
                                  p=ps, b=b1_sb[:, fb:fb + 1]: e.activation(
                                      d, p[:, :], AF.Relu, bias=b, scale=1.0))
            if kk == 0:
                wait("tensor", "gw2", 16)
            wait("tensor", "ac", t_hrelu[31])
            for sb4 in range(4):
                islot = kk * 4 + sb4
                if islot >= 2:
                    wait("tensor", "syi%d" % (islot - 2), 16)
                if y_evac[islot % 2]:
                    wait("tensor", "dv", y_evac[islot % 2])
                ybanks = [ao_ps[0], ao_ps[1]] if islot % 2 == 0 else [z_ps[0], z_ps[1]]
                for half in range(2):
                    ps = ybanks[half]
                    for fb in range(32):
                        op0("tensor", lambda e, l=h_sb[:, fb, sb4 * 128:(sb4 + 1) * 128],
                            r=w2_sb[:, fb, half * 512:(half + 1) * 512], p=ps,
                            s=(fb == 0), t=False:
                            e.matmul(p[:, :], l, r, start=s, stop=t))
                    tpe_y = opi("tensor", "pe",
                                lambda e, r=b2hl_sb[:, half * 512:(half + 1) * 512],
                                p=ps: e.matmul(p[:, :], onesb_sb[:, :], r,
                                               start=False, stop=True))
                wait("vector", "pe", tpe_y)
                tdv_y = 0
                for half in range(2):
                    tdv_y = opi("vector", "dv", lambda e,
                                d=yt_sb[:, islot % 2, half * 512:(half + 1) * 512],
                                p=ybanks[half]: e.tensor_copy(d, p[:, :]))
                y_evac[islot % 2] = tdv_y
                wait("scalar", "dv", tdv_y)
                dma("scalar", "syi%d" % islot,
                    yg_i.rearrange("(t d) -> t d", d=D)[islot * 128:(islot + 1) * 128, :],
                    yt_sb[:, islot % 2, :])
        stD.close()

        mark("E")
        stE = ExitStack()
        ytok_sb = stE.enter_context(nc.sbuf_tensor("ytok_sb", [128, 16, D], BF16))
        x2_sb = stE.enter_context(nc.sbuf_tensor("x2_sb", [128, D], F32))
        sq2_sb = stE.enter_context(nc.sbuf_tensor("sq2_sb", [128, D], F32))

        for ii in range(16):
            wait("gpsimd", "syi%d" % ii, 16)
        t_gss = opi("gpsimd", "cc", lambda e: e.nop())
        t_agy = opi("gpsimd", "cc", lambda e: e.collective_compute(
            "AllGather", OP.bypass, replica_groups=[list(range(8))],
            ins=[yg_i.ap().opt()], outs=[yg_o.ap().opt()])._wait_ge(sems["cc"], t_gss))
        wait("gpsimd", "cc", t_agy)
        phase_barrier(["gpsimd", "vector", "scalar", "tensor", "sync"])
        for kk in range(4):
            gname = "gyg%d" % kk
            cnt[gname] += 16
            semh = sems[gname]
            emit("gpsimd", lambda e, k_=kk, sh=semh: e.dma_gather(
                ytok_sb[:, k_ * 4:(k_ + 1) * 4, :], yg_o.rearrange("(t d) -> t d", d=D),
                gidx_sb[:, k_ * 32:(k_ + 1) * 32], 512, 512, D).then_inc(sh, 16))

        for tb in range(16):
            if tb > 0:
                wait("sync", "soute" if (tb - 1) % 2 == 0 else "souto",
                     16 * ((tb - 1) // 2 + 1))
            t_x1r = dma("sync", "lx1", x2_sb[:, :], x1f[tb * 128:(tb + 1) * 128, :])
            wait("vector", "lx1", t_x1r)
            wait("vector", "gyg%d" % (tb // 4), 16)
            opi("vector", "dv", lambda e, tb_=tb: e.scalar_tensor_tensor(
                x2_sb[:, :], ytok_sb[:, tb_, :], gact_tm[:, tb_:tb_ + 1], x2_sb[:, :],
                OP.mult, OP.add))
            opi("vector", "dv", lambda e: e.reduce_sum(
                lnv_sb[:, 0:1], x2_sb[:, :], axis=AX))
            opi("vector", "dv", lambda e: e.tensor_scalar(
                lnv_sb[:, 1:2], lnv_sb[:, 0:1], 1.0 / D, None, OP.mult))
            opi("vector", "dv", lambda e: e.scalar_tensor_tensor(
                sq2_sb[:, :], x2_sb[:, :], 1.0, x2_sb[:, :], OP.mult, OP.mult,
                accum_out=lnv_sb[:, 2:3]))
            opi("vector", "dv", lambda e: e.tensor_scalar(
                lnv_sb[:, 3:4], lnv_sb[:, 2:3], 1.0 / D, EPS, OP.mult, OP.add))
            opi("vector", "dv", lambda e: e.tensor_tensor(
                lnv_sb[:, 4:5], lnv_sb[:, 1:2], lnv_sb[:, 1:2], OP.mult))
            t_var2 = opi("vector", "dv", lambda e: e.tensor_sub(
                lnv_sb[:, 3:4], lnv_sb[:, 3:4], lnv_sb[:, 4:5]))
            wait("scalar", "dv", t_var2)
            opi("scalar", "ac", lambda e: e.activation(
                lnv_sb[:, 5:6], lnv_sb[:, 3:4], AF.Ln, bias=0.0, scale=1.0))
            t_rs2 = opi("scalar", "ac", lambda e: e.activation(
                lnv_sb[:, 6:7], lnv_sb[:, 5:6], AF.Exp, bias=0.0, scale=-0.5))
            wait("vector", "ac", t_rs2)
            t_x2 = opi("vector", "dv", lambda e: e.tensor_scalar(
                x2_sb[:, :], x2_sb[:, :], lnv_sb[:, 1:2], lnv_sb[:, 6:7],
                OP.subtract, OP.mult))
            wait("scalar", "dv", t_x2)
            dma("scalar", "soute" if tb % 2 == 0 else "souto",
                out_t[tb * 128:(tb + 1) * 128, :], x2_sb[:, :])
        stE.close()

        wait("scalar", "soute", cnt["soute"])
        wait("scalar", "souto", cnt["souto"])
        for nm in ("gxg0", "gxg1", "gxg2", "gxg3", "gyg0", "gyg1", "gyg2", "gyg3",
                   "gw1", "gw2", "gpc", "gsc", "gpw", "gpwt"):
            wait("gpsimd", nm, cnt[nm])
        wait("gpsimd", "cc", cnt["cc"])
        for nm in ("ldc", "lx0e", "lx0o", "lb2", "lx1", "srt", "sxf", "sxg"):
            wait("sync", nm, cnt[nm])
        wait("scalar", "sx0e", cnt["sx0e"])
        wait("scalar", "sx0o", cnt["sx0o"])
        for ii in range(16):
            wait("scalar", "syi%d" % ii, cnt["syi%d" % ii])
        wait("scalar", "sct", cnt["sct"])

        if PHASE_LIMIT in marks:
            cut, csnap = marks[PHASE_LIMIT]
            for k in plan:
                del plan[k][cut[k]:]
            eng_for = {"soute": "scalar", "souto": "scalar", "sct": "scalar",
                       "srt": "scalar", "sxf": "scalar", "sxg": "scalar",
                       "sx0e": "scalar", "sx0o": "scalar",
                       "ldc": "sync", "lx0e": "sync", "lx0o": "sync",
                       "lb2": "sync", "lx1": "sync", "lrt": "sync",
                       "cc": "gpsimd"}
            for nm, v in csnap.items():
                if v > 0 and not nm.startswith(("pe", "dv", "ac")):
                    eng = eng_for.get(nm, "gpsimd")
                    if nm.startswith("syi"):
                        eng = "scalar"
                    semh = sems[nm]
                    plan[eng].append(lambda e, s=semh, vv=v: e.wait_ge(s, vv))
        blk = ex(nc.Block())

        @blk.tensor
        def _(e):
            for fn in plan["tensor"]:
                fn(e)

        @blk.vector
        def _(e):
            for fn in plan["vector"]:
                fn(e)

        @blk.scalar
        def _(e):
            for fn in plan["scalar"]:
                fn(e)

        @blk.gpsimd
        def _(e):
            for fn in plan["gpsimd"]:
                fn(e)

        @blk.sync
        def _(e):
            for fn in plan["sync"]:
                fn(e)

    nc.compile()
    return nc


_CACHE = {}


def kernel(L0_attn_x, L0_attn_W_Q, L0_attn_W_K, L0_attn_W_V, L0_attn_W_O,
           L0_ffn_router_w, L0_ffn_w1, L0_ffn_b1, L0_ffn_w2, L0_ffn_b2):
    x = np.ascontiguousarray(np.asarray(L0_attn_x, dtype=np.float32))
    wq = np.ascontiguousarray(np.asarray(L0_attn_W_Q, np.float32).reshape(D, D).T)
    wk = np.ascontiguousarray(np.asarray(L0_attn_W_K, np.float32).reshape(D, D).T)
    wv = np.ascontiguousarray(np.asarray(L0_attn_W_V, np.float32).reshape(D, D).T)
    wo = np.ascontiguousarray(
        np.asarray(L0_attn_W_O, np.float32).transpose(0, 2, 1).reshape(D, D))
    rw = np.ascontiguousarray(np.asarray(L0_ffn_router_w, np.float32))
    w1 = np.ascontiguousarray(np.asarray(L0_ffn_w1, np.float32))
    b1 = np.ascontiguousarray(np.asarray(L0_ffn_b1, np.float32))
    w2 = np.ascontiguousarray(np.asarray(L0_ffn_w2, np.float32))
    b2 = np.ascontiguousarray(np.asarray(L0_ffn_b2, np.float32))

    if "nc" not in _CACHE:
        _CACHE["nc"] = build_nc()
    nc = _CACHE["nc"]

    in_maps = []
    sidx = np.arange(S)
    for c in range(8):
        masklt = np.zeros((8, 16), np.float32)
        masklt[:c, :] = 1.0
        onehot = np.zeros((8, 8), np.float32)
        onehot[c, :] = 1.0
        ramp = np.zeros((128, 16), np.float32)
        ramp[sidx % 128, sidx // 128] = c * S + sidx + 1
        in_maps.append({
            "x_in": x[c], "wq_in": wq, "wk_in": wk, "wv_in": wv, "wo_in": wo,
            "rw_in": rw, "w1_in": w1[c], "b1_in": b1[c], "w2_in": w2[c],
            "b2_in": b2[c:c + 1], "masklt_in": masklt, "onehot_in": onehot,
            "ramp_in": ramp,
        })
    res = run_bass_kernel_spmd(nc, in_maps, core_ids=list(range(8)))
    return np.stack([res.results[c]["out"] for c in range(8)], axis=0)


# version 12
# speedup vs baseline: 1.2443x; 1.0305x over previous
import os as _os
import numpy as np
import concourse.bacc as bacc_mod
import concourse.mybir as mybir
from concourse.bass_utils import run_bass_kernel_spmd

F32 = mybir.dt.float32
BF16 = mybir.dt.bfloat16
I16 = mybir.dt.int16
I32 = mybir.dt.int32
OP = mybir.AluOpType
AF = mybir.ActivationFunctionType
AX = mybir.AxisListType.X

S = 2048
D = 1024
NH = 16
F = 4096
T = 16384
SEG = 2080
EPS = 1e-5
BIGE = 65536.0


PHASE_LIMIT = _os.environ.get("KPHASE", "Z")
KDEBUG = _os.environ.get("KDEBUG", "") == "1"


def build_nc():
    nc = bacc_mod.Bacc("TRN2", target_bir_lowering=False)

    x_in = nc.dram_tensor("x_in", [S, D], F32, kind="ExternalInput").ap()
    wq_in = nc.dram_tensor("wq_in", [D, D], F32, kind="ExternalInput").ap()
    wk_in = nc.dram_tensor("wk_in", [D, D], F32, kind="ExternalInput").ap()
    wv_in = nc.dram_tensor("wv_in", [D, D], F32, kind="ExternalInput").ap()
    wo_in = nc.dram_tensor("wo_in", [D, D], F32, kind="ExternalInput").ap()
    rw_in = nc.dram_tensor("rw_in", [D, 8], F32, kind="ExternalInput").ap()
    w1_in = nc.dram_tensor("w1_in", [D, F], F32, kind="ExternalInput").ap()
    b1_in = nc.dram_tensor("b1_in", [F], F32, kind="ExternalInput").ap()
    w2_in = nc.dram_tensor("w2_in", [F, D], F32, kind="ExternalInput").ap()
    b2_in = nc.dram_tensor("b2_in", [1, D], F32, kind="ExternalInput").ap()
    masklt_in = nc.dram_tensor("masklt_in", [8, 16], F32, kind="ExternalInput").ap()
    onehot_in = nc.dram_tensor("onehot_in", [8, 8], F32, kind="ExternalInput").ap()
    ramp_in = nc.dram_tensor("ramp_in", [128, 16], F32, kind="ExternalInput").ap()
    out_t = nc.dram_tensor("out", [S, D], F32, kind="ExternalOutput").ap()

    x1f = nc.dram_tensor("x1f", [S, D], F32)
    xg_i = nc.dram_tensor("xg_i", [S * D], BF16)
    xg_o = nc.dram_tensor("xg_o", [T * D], BF16)
    cnt_i = nc.dram_tensor("cnt_i", [8], F32)
    cnt_o = nc.dram_tensor("cnt_o", [64], F32)
    listtab = nc.dram_tensor("listtab", [8 * SEG, 64], F32)
    seg_t = nc.dram_tensor("seg_t", [SEG * 64], F32)
    rt = nc.dram_tensor("rt", [2, S], F32)
    yg_i = nc.dram_tensor("yg_i", [S * D], BF16)
    yg_o = nc.dram_tensor("yg_o", [T * D], BF16)

    ident = nc.inline_tensor(np.eye(128, dtype=np.float32), name="cident").ap()
    sl16_np = np.zeros((16, 16), np.float32)
    for k in range(16):
        sl16_np[k, k + 1:] = 1.0
    sl16_d = nc.inline_tensor(sl16_np, name="csl16").ap()
    masks_np = np.zeros((128, 4, 512), np.float32)
    for dd in range(4):
        for p in range(128):
            if p + dd * 128 < 512:
                masks_np[p, dd, p + dd * 128:] = 1.0
    masks_d = nc.inline_tensor(masks_np, name="cmasks").ap()
    erowp_d = nc.inline_tensor(
        np.tile(np.arange(8, dtype=np.float32)[None, :] + BIGE, (128, 1)),
        name="cerowp").ap()
    erow_d = nc.inline_tensor(
        np.tile(np.arange(8, dtype=np.float32)[None, :], (128, 1)), name="cerow").ap()
    k512_d = nc.inline_tensor(
        np.array([[0.0, 512.0, 1024.0, 1536.0, 0.0, 0.0, 0.0, 0.0]], np.float32),
        name="ck512").ap()

    from contextlib import ExitStack
    with ExitStack() as stk:
        ex = stk.enter_context

        ident_sb = ex(nc.sbuf_tensor("ident_sb", [128, 128], F32))
        ones_sb = ex(nc.sbuf_tensor("ones_sb", [128, 64], F32))
        onesb_sb = ex(nc.sbuf_tensor("onesb_sb", [2, 128], BF16))
        erow_sb = ex(nc.sbuf_tensor("erow_sb", [128, 8], F32))
        erowp_sb = ex(nc.sbuf_tensor("erowp_sb", [128, 8], F32))
        k512_sb = ex(nc.sbuf_tensor("k512_sb", [1, 8], F32))
        gate_tm = ex(nc.sbuf_tensor("gate_tm", [128, 16], F32))
        gact_tm = ex(nc.sbuf_tensor("gact_tm", [128, 16], F32))
        emask_tm = ex(nc.sbuf_tensor("emask_tm", [128, 16, 8], F32))
        small = ex(nc.sbuf_tensor("small", [128, 32], F32))
        lnv_sb = ex(nc.sbuf_tensor("lnv_sb", [128, 8], F32))
        posm_sb = ex(nc.sbuf_tensor("posm_sb", [128, 128], I16))
        gidx_sb = ex(nc.sbuf_tensor("gidx_sb", [128, 128], I16))
        xidx_sb = ex(nc.sbuf_tensor("xidx_sb", [128, 128], I16))
        ccnt_sb = ex(nc.sbuf_tensor("ccnt_sb", [1, 8], I32))
        ccntf_sb = ex(nc.sbuf_tensor("ccntf_sb", [1, 8], F32))
        b1_sb = ex(nc.sbuf_tensor("b1_sb", [128, 32], F32))
        b2hl_sb = ex(nc.sbuf_tensor("b2hl_sb", [2, D], BF16))
        meta_sb = ex(nc.sbuf_tensor("meta_sb", [8, 24], F32))
        rvec = ex(nc.sbuf_tensor("rvec", [16, 64], F32))
        ramp_sb = ex(nc.sbuf_tensor("ramp_sb", [128, 16], F32))
        cnts_ag = ex(nc.sbuf_tensor("cnts_ag", [8, 8], F32))

        sc_ps = [ex(nc.psum_tensor("sc_ps%d" % i, [128, 512], F32)) for i in range(2)]
        z_ps = [ex(nc.psum_tensor("z_ps%d" % i, [128, 512], F32)) for i in range(2)]
        b_ps = ex(nc.psum_tensor("b_ps", [64, 512], F32))
        ao_ps = [ex(nc.psum_tensor("ao_ps%d" % i, [128, 512], F32)) for i in range(2)]
        ms_ps = ex(nc.psum_tensor("ms_ps", [128, 512], F32))

        sem_names = ["syi%d" % i for i in range(16)] + ["ldc", "lx0e", "lx0o", "lb2", "lx1", "gpw", "gpwt", "gpc",
                     "gsc", "gw1", "gw2", "lrt",
                     "gxg0", "gxg1", "gxg2", "gxg3", "gyg0", "gyg1", "gyg2", "gyg3",
                     "sx0e", "sx0o", "sxf", "sxg", "sct", "srt",
                     "soute", "souto",
                     "pe", "dv", "ac", "cc"]
        sems = {n: ex(nc.semaphore(n)) for n in sem_names}
        cnt = {n: 0 for n in sem_names}
        plan = {k: [] for k in ("tensor", "vector", "scalar", "gpsimd", "sync")}

        def emit(eng, fn):
            plan[eng].append(fn)

        def dma(eng, sem, out_ap, in_ap, **kw):
            cnt[sem] += 16
            semh = sems[sem]
            emit(eng, lambda e, o=out_ap, i=in_ap: e.dma_start(out=o, in_=i, **kw)
                 .then_inc(semh, 16))
            return cnt[sem]

        def wait(eng, sem, val):
            semh = sems[sem]
            emit(eng, lambda e, v=val: e.wait_ge(semh, v))

        ENG_SEM = {"vector": "dv", "scalar": "ac"}

        def opi(eng, sem, fn):
            own = ENG_SEM.get(eng)
            if own is not None and cnt[own] > 0:
                semo = sems[own]
                emit(eng, lambda e, v=cnt[own]: e.wait_ge(semo, v))
            cnt[sem] += 1
            semh = sems[sem]
            emit(eng, lambda e, f=fn: f(e).then_inc(semh, 1))
            return cnt[sem]

        def op0(eng, fn):
            emit(eng, lambda e, f=fn: f(e))

        marks = {}

        def mark(name):
            marks[name] = ({k: len(v) for k, v in plan.items()}, dict(cnt))

        def phase_barrier(engines):
            snap = [(nm, cnt[nm]) for nm in sem_names if cnt[nm] > 0]
            for eng in engines:
                for nm, v in snap:
                    wait(eng, nm, v)

        stAB = ExitStack()
        v_tok = stAB.enter_context(nc.sbuf_tensor("v_tok", [128, 16, NH * 65], BF16))
        q_sb = stAB.enter_context(nc.sbuf_tensor("q_sb", [128, 8, S], BF16))
        k_sb = stAB.enter_context(nc.sbuf_tensor("k_sb", [128, 8, S], BF16))
        wo_sb = stAB.enter_context(nc.sbuf_tensor("wo_sb", [128, 8, D], BF16))
        masks_sb = stAB.enter_context(nc.sbuf_tensor("masks_sb", [128, 4, 512], BF16))
        rw_sb = stAB.enter_context(nc.sbuf_tensor("rw_sb", [128, 8, 8], F32))

        stA = ExitStack()
        wtmp_sb = stA.enter_context(nc.sbuf_tensor("wtmp_sb", [128, 8, D], BF16))
        x0t_sb = stA.enter_context(nc.sbuf_tensor("x0t_sb", [128, 8, S], BF16))
        x0_sb = stA.enter_context(nc.sbuf_tensor("x0_sb", [128, 2, D], F32))
        b2s_sb = stA.enter_context(nc.sbuf_tensor("b2s_sb", [1, 2 * D], F32))
        b1t_sb = stA.enter_context(nc.sbuf_tensor("b1t_sb", [32, 128], F32))
        b2w_sb = stA.enter_context(nc.sbuf_tensor("b2w_sb", [1, 2, D], BF16))

        dma("sync", "ldc", ident_sb[:, :], ident[:, :])
        dma("sync", "ldc", erow_sb[:, :], erow_d[:, :])
        dma("sync", "ldc", erowp_sb[:, :], erowp_d[:, :])
        dma("sync", "ldc", k512_sb[:, :], k512_d[:, :])
        dma("sync", "ldc", rw_sb[:, :, :], rw_in.rearrange("(c p) e -> p c e", p=128))
        dma("sync", "ldc", meta_sb[:, 0:16], masklt_in[:, :])
        dma("sync", "ldc", meta_sb[:, 16:24], onehot_in[:, :])
        dma("sync", "ldc", b1t_sb[:, :], b1_in.rearrange("(a p) -> a p", p=128))
        dma("sync", "ldc", b2s_sb[0:1, 0:D], b2_in[:, :])
        dma("sync", "ldc", rvec[0:16, 48:64], sl16_d[:, :])
        T_LDC = dma("sync", "ldc", ramp_sb[:, :], ramp_in[:, :])
        assert T_LDC == 176
        dma("gpsimd", "gpw", masks_sb[:, :, :], masks_d[:, :, :])
        T_GPW = dma("gpsimd", "gpw", wo_sb[:, :, :], wo_in.rearrange("(c p) e -> p c e", p=128))

        opi("vector", "dv", lambda e: e.memset(ones_sb[:, :], 1.0))
        opi("vector", "dv", lambda e: e.memset(onesb_sb[:, :], 1.0))
        t_vones = opi("vector", "dv", lambda e: e.memset(
            v_tok.ap().rearrange("p a (h c) -> p a h c", c=65)[:, :, :, 64:65], 1.0))
        wait("vector", "ldc", T_LDC)
        opi("vector", "dv", lambda e: e.tensor_copy(b2w_sb[0:1, 0, :], b2s_sb[0:1, 0:D]))
        opi("vector", "dv", lambda e: e.tensor_copy(b2s_sb[0:1, D:2 * D], b2w_sb[0:1, 0, :]))
        opi("vector", "dv", lambda e: e.tensor_sub(
            b2s_sb[0:1, D:2 * D], b2s_sb[0:1, 0:D], b2s_sb[0:1, D:2 * D]))
        t_b2pre = opi("vector", "dv", lambda e: e.tensor_copy(
            b2w_sb[0:1, 1, :], b2s_sb[0:1, D:2 * D]))
        wait("sync", "dv", t_b2pre)
        dma("sync", "lb2", b2hl_sb[0:1, :], b2w_sb[0:1, 0, :])
        t_b2l_dma = dma("sync", "lb2", b2hl_sb[1:2, :], b2w_sb[0:1, 1, :])

        wait("tensor", "ldc", T_LDC)
        tpe_b1 = opi("tensor", "pe", lambda e: e.transpose(
            ms_ps[:, 0:32], b1t_sb[:, :], ident_sb[0:32, 0:32]))
        wait("vector", "pe", tpe_b1)
        tdv_b1 = opi("vector", "dv", lambda e: e.tensor_copy(b1_sb[:, :], ms_ps[:, 0:32]))
        wait("tensor", "dv", tdv_b1)
        t_xtr_done = [0] * 16
        for tb in range(16):
            if tb >= 2:
                wait("sync", "dv", t_xtr_done[tb - 2])
            t_ld = dma("sync", "lx0e" if tb % 2 == 0 else "lx0o",
                       x0_sb[:, tb % 2, :], x_in[tb * 128:(tb + 1) * 128, :])
            wait("tensor", "lx0e" if tb % 2 == 0 else "lx0o", 16 * (tb // 2 + 1))
            for dc in range(8):
                tpe = opi("tensor", "pe", lambda e, s=x0_sb[:, tb % 2, dc * 128:(dc + 1) * 128]:
                          e.transpose(ms_ps[:, 0:128], s, ident_sb[:, :]))
                wait("vector", "pe", tpe)
                tdv = opi("vector", "dv", lambda e, d=x0t_sb[:, dc, tb * 128:(tb + 1) * 128]:
                          e.tensor_copy(d, ms_ps[:, 0:128]))
                wait("tensor", "dv", tdv)
            t_xtr_done[tb] = cnt["dv"]

        dma("gpsimd", "gpwt", wtmp_sb[:, :, :],
            wv_in.rearrange("(c p) a -> p c a", p=128))
        wait("tensor", "gpwt", 16)
        wait("tensor", "dv", t_vones)
        last_evac = {0: 0, 1: 0}
        for tb in range(16):
            for half in range(2):
                ps = sc_ps[half]
                if last_evac[half]:
                    wait("tensor", "dv", last_evac[half])
                for dc in range(8):
                    fn = (lambda e, l=x0t_sb[:, dc, tb * 128:(tb + 1) * 128],
                          r=wtmp_sb[:, dc, half * 512:(half + 1) * 512], p=ps,
                          s=(dc == 0), t=(dc == 7):
                          e.matmul(p[:, :], l, r, start=s, stop=t))
                    if dc < 7:
                        op0("tensor", fn)
                    else:
                        tpe = opi("tensor", "pe", fn)
                dst = v_tok.ap().rearrange("p a (h c) -> p a h c", c=65)[
                    :, tb, half * 8:(half + 1) * 8, 0:64]
                wait("vector", "pe", tpe)
                last_evac[half] = opi("vector", "dv",
                                      lambda e, d=dst, p=ps: e.tensor_copy(d, p[:, :]))
        for wi, (w_dram, dsts) in enumerate(((wq_in, q_sb), (wk_in, k_sb))):
            wait("gpsimd", "dv", cnt["dv"])
            dma("gpsimd", "gpwt", wtmp_sb[:, :, :],
                w_dram.rearrange("(c p) a -> p c a", p=128))
            wait("tensor", "gpwt", 16 * (wi + 2))
            for ac in range(8):
                for tt in range(4):
                    ps = sc_ps[tt % 2]
                    if last_evac[tt % 2]:
                        wait("tensor", "dv", last_evac[tt % 2])
                    for dc in range(8):
                        fn = (lambda e, l=wtmp_sb[:, dc, ac * 128:(ac + 1) * 128],
                              r=x0t_sb[:, dc, tt * 512:(tt + 1) * 512], p=ps,
                              s=(dc == 0), t=(dc == 7):
                              e.matmul(p[:, :], l, r, start=s, stop=t))
                        if dc < 7:
                            op0("tensor", fn)
                        else:
                            tpe = opi("tensor", "pe", fn)
                    wait("vector", "pe", tpe)
                    last_evac[tt % 2] = opi(
                        "vector", "dv", lambda e, d=dsts[:, ac, tt * 512:(tt + 1) * 512],
                        p=ps: e.tensor_copy(d, p[:, :]))
        stA.close()

        mark("B")
        strip = stAB.enter_context(nc.sbuf_tensor("strip", [128, 16, 512], BF16))
        z_sb = stAB.enter_context(nc.sbuf_tensor("z_sb", [128, 8, 512], BF16))
        recip_sb = stAB.enter_context(nc.sbuf_tensor("recip_sb", [1, 1024], F32))
        bsb_sb = stAB.enter_context(nc.sbuf_tensor("bsb_sb", [64, 512], F32))
        x1_sb = stAB.enter_context(nc.sbuf_tensor("x1_sb", [128, D], F32))
        x1b_sb = stAB.enter_context(nc.sbuf_tensor("x1b_sb", [128, D], BF16))
        x1t_sb = stAB.enter_context(nc.sbuf_tensor("x1t_sb", [128, 8, 128], F32))
        sq_sb = stAB.enter_context(nc.sbuf_tensor("sq_sb", [128, D], F32))

        wait("vector", "lb2", 32)
        wait("scalar", "lb2", 32)
        wait("tensor", "gpw", T_GPW)
        wait("tensor", "dv", max(last_evac[0], last_evac[1]))
        t_eq_prev = 0
        t_zn_par = {0: 0, 1: 0}
        t_zn_prev = 0
        banks4 = [sc_ps[0], sc_ps[1], ao_ps[0], ao_ps[1]]
        last_sc_exp = {0: 0, 1: 0, 2: 0, 3: 0}
        ao_guard = {2: 0, 3: 0}
        pend = [None]
        t_zn_last = [0]
        tr_banks = [ms_ps, z_ps[0]]
        t_trcp = {0: 0, 1: 0}

        def emit_tail():
            t_rc_p, zp_p, rowb_p, ahc_p, par_p = pend[0]
            pend[0] = None
            wait("tensor", "dv", max(t_rc_p, t_zn_last[0]))
            tpe_b = opi("tensor", "pe", lambda e,
                        rp=recip_sb[0:1, par_p * 512:(par_p + 1) * 512]: e.matmul(
                            b_ps[:, :], ones_sb[0:1, 0:64], rp, start=True, stop=True))
            wait("vector", "pe", tpe_b)
            opi("vector", "dv", lambda e: e.tensor_copy(bsb_sb[:, :], b_ps[:, :]))
            t_zn = opi("vector", "dv", lambda e, d=z_sb[rowb_p:rowb_p + 64, ahc_p, :],
                       p=zp_p: e.tensor_tensor(d, p[0:64, :], bsb_sb[:, :], OP.mult))
            t_zn_par[par_p] = t_zn
            t_zn_last[0] = t_zn
            return t_zn

        for j in range(4):
            nch = 4 * j + 4
            for a in range(NH):
                par = (j * 16 + a) % 2
                rowb = 64 * (a % 2)
                ahc = a // 2
                t_exp = [0] * nch
                t_msk = [0] * nch
                for i in range(nch):
                    b4 = i % 4
                    ps = banks4[b4]
                    if b4 >= 2 and ao_guard[b4]:
                        wait("tensor", "dv", ao_guard[b4])
                        ao_guard[b4] = 0
                    if last_sc_exp[b4]:
                        wait("tensor", "ac", last_sc_exp[b4])
                    tpe = opi("tensor", "pe", lambda e,
                              l=q_sb[rowb:rowb + 64, ahc, i * 128:(i + 1) * 128],
                              r=k_sb[rowb:rowb + 64, ahc, j * 512:(j + 1) * 512],
                              p=ps: e.matmul(p[:, :], l, r, start=True, stop=True))
                    wait("scalar", "pe", tpe)
                    t_exp[i] = opi("scalar", "ac", lambda e, d=strip[:, i, :], p=ps:
                                   e.activation(d, p[:, :], AF.Exp, bias=0.0, scale=0.125))
                    last_sc_exp[b4] = t_exp[i]
                    if i >= 4 * j:
                        wait("vector", "ac", t_exp[i])
                        t_msk[i] = opi("vector", "dv", lambda e, s=strip[:, i, :],
                                       m=masks_sb[:, i - 4 * j, :]:
                                       e.tensor_tensor(s, s, m, OP.mult))
                zp = z_ps[par]
                if t_zn_par[par]:
                    wait("tensor", "dv", t_zn_par[par])
                for i in range(nch):
                    if t_msk[i]:
                        wait("tensor", "dv", t_msk[i])
                    else:
                        wait("tensor", "ac", t_exp[i])
                    fn = (lambda e, l=v_tok[:, i, a * 65:(a + 1) * 65],
                          r=strip[:, i, :], p=zp, s=(i == 0), t=(i == nch - 1):
                          e.matmul(p[0:65, :], l, r, start=s, stop=t))
                    if i < nch - 1:
                        op0("tensor", fn)
                    else:
                        tpe_z = opi("tensor", "pe", fn)
                wait("vector", "pe", tpe_z)
                t_rc = opi("vector", "dv", lambda e, p=zp, pr=par:
                           e.reciprocal(recip_sb[0:1, pr * 512:(pr + 1) * 512],
                                        p[64:65, :]))
                if pend[0] is not None:
                    t_zn_prev = emit_tail()
                pend[0] = (t_rc, zp, rowb, ahc, par)
            t_zn_prev = emit_tail()

            wait("tensor", "gpw", T_GPW)
            wait("tensor", "dv", t_zn_prev)
            for s4 in range(4):
                tb = j * 4 + s4
                for half in range(2):
                    ps = ao_ps[half]
                    for ahc in range(8):
                        fn = (lambda e, l=z_sb[:, ahc, s4 * 128:(s4 + 1) * 128],
                              r=wo_sb[:, ahc, half * 512:(half + 1) * 512], p=ps,
                              s=(ahc == 0), t=(ahc == 7):
                              e.matmul(p[:, :], l, r, start=s, stop=t))
                        if ahc < 7:
                            op0("tensor", fn)
                        else:
                            tpe_ao = opi("tensor", "pe", fn)
                if tb > 0:
                    wait("scalar", "pe", t_rtr_last)
                    wait("scalar", "sxf", 16 * tb)
                sx0 = "sx0e" if tb % 2 == 0 else "sx0o"
                t_x0r = dma("scalar", sx0, x1_sb[:, :], x_in[tb * 128:(tb + 1) * 128, :])
                wait("vector", sx0, 16 * (tb // 2 + 1))
                wait("vector", "pe", tpe_ao)
                for half in range(2):
                    opi("vector", "dv", lambda e, h=half: e.tensor_tensor(
                        x1_sb[:, h * 512:(h + 1) * 512], x1_sb[:, h * 512:(h + 1) * 512],
                        ao_ps[h][:, :], OP.add))
                t_res = cnt["dv"]
                wait("tensor", "dv", t_res)
                opi("vector", "dv", lambda e: e.reduce_sum(
                    lnv_sb[:, 0:1], x1_sb[:, :], axis=AX))
                opi("vector", "dv", lambda e: e.tensor_scalar(
                    lnv_sb[:, 1:2], lnv_sb[:, 0:1], 1.0 / D, None, OP.mult))
                t_sq = opi("vector", "dv", lambda e: e.scalar_tensor_tensor(
                    sq_sb[:, :], x1_sb[:, :], 1.0, x1_sb[:, :], OP.mult, OP.mult,
                    accum_out=lnv_sb[:, 2:3]))
                opi("vector", "dv", lambda e: e.tensor_scalar(
                    lnv_sb[:, 3:4], lnv_sb[:, 2:3], 1.0 / D, EPS, OP.mult, OP.add))
                opi("vector", "dv", lambda e: e.tensor_tensor(
                    lnv_sb[:, 4:5], lnv_sb[:, 1:2], lnv_sb[:, 1:2], OP.mult))
                t_var = opi("vector", "dv", lambda e: e.tensor_sub(
                    lnv_sb[:, 3:4], lnv_sb[:, 3:4], lnv_sb[:, 4:5]))
                wait("scalar", "dv", t_var)
                opi("scalar", "ac", lambda e: e.activation(
                    lnv_sb[:, 5:6], lnv_sb[:, 3:4], AF.Ln, bias=0.0, scale=1.0))
                t_rs = opi("scalar", "ac", lambda e: e.activation(
                    lnv_sb[:, 6:7], lnv_sb[:, 5:6], AF.Exp, bias=0.0, scale=-0.5))
                wait("vector", "ac", t_rs)
                t_x1 = opi("vector", "dv", lambda e: e.tensor_scalar(
                    x1_sb[:, :], x1_sb[:, :], lnv_sb[:, 1:2], lnv_sb[:, 6:7],
                    OP.subtract, OP.mult))
                wait("scalar", "dv", t_x1)
                dma("scalar", "sxf", x1f[tb * 128:(tb + 1) * 128, :], x1_sb[:, :])
                if tb > 0:
                    wait("vector", "sxg", 16 * tb)
                t_x1b = opi("vector", "dv", lambda e: e.tensor_copy(
                    x1b_sb[:, :], x1_sb[:, :]))
                wait("scalar", "dv", t_x1b)
                dma("scalar", "sxg",
                    xg_i.rearrange("(t d) -> t d", d=D)[tb * 128:(tb + 1) * 128, :],
                    x1b_sb[:, :])
                wait("tensor", "dv", t_x1)
                for dc in range(8):
                    pb = tr_banks[dc % 2]
                    if t_trcp[dc % 2]:
                        wait("tensor", "dv", t_trcp[dc % 2])
                    tpe_t = opi("tensor", "pe", lambda e, p=pb,
                                s=x1_sb[:, dc * 128:(dc + 1) * 128]:
                                e.transpose(p[:, 0:128], s, ident_sb[:, :]))
                    wait("vector", "pe", tpe_t)
                    t_trcp[dc % 2] = opi("vector", "dv", lambda e, p=pb,
                                         d=x1t_sb[:, dc, :]:
                                         e.tensor_copy(d, p[:, 0:128]))
                wait("tensor", "dv", t_trcp[0])
                for dc in range(8):
                    fn = (lambda e, l=x1t_sb[:, dc, :], r=rw_sb[:, dc, :],
                          s=(dc == 0), t=(dc == 7):
                          e.matmul(ms_ps[:, 128:136], l, r, start=s, stop=t))
                    if dc < 7:
                        op0("tensor", fn)
                    else:
                        tpe_rt = opi("tensor", "pe", fn)
                t_rtr_last = tpe_rt
                wait("vector", "pe", tpe_rt)
                opi("vector", "dv", lambda e: e.reduce_max(
                    small[:, 12:13], ms_ps[:, 128:136], axis=AX, negate=True))
                t_nm = cnt["dv"]
                wait("scalar", "dv", t_nm)
                t_ex = opi("scalar", "ac", lambda e: e.activation(
                    small[:, 24:32], ms_ps[:, 128:136], AF.Exp, bias=small[:, 12:13],
                    scale=1.0, accum_out=small[:, 8:9]))
                wait("vector", "ac", t_ex)
                opi("vector", "dv", lambda e, tb_=tb: e.reciprocal(
                    gate_tm[:, tb_:tb_ + 1], small[:, 8:9]))
                opi("vector", "dv", lambda e: e.tensor_scalar(
                    small[:, 0:8], ms_ps[:, 128:136], small[:, 12:13], None, OP.add))
                opi("vector", "dv", lambda e: e.tensor_scalar(
                    small[:, 0:8], small[:, 0:8], 0.0, -BIGE, OP.is_ge, OP.mult))
                opi("vector", "dv", lambda e: e.tensor_tensor(
                    small[:, 0:8], small[:, 0:8], erowp_sb[:, :], OP.add))
                opi("vector", "dv", lambda e: e.tensor_reduce(
                    small[:, 9:10], small[:, 0:8], AX, OP.min))
                t_eq_prev = opi("vector", "dv", lambda e, tb_=tb: e.tensor_scalar(
                    emask_tm[:, tb_, :], erow_sb[:, :], small[:, 9:10], None,
                    OP.is_equal))
            ao_guard[2] = t_res
            ao_guard[3] = t_res
            t_zn_par[0] = max(t_zn_par[0], t_trcp[1])

        mark("C")
        stC = ExitStack()
        emt = stC.enter_context(nc.sbuf_tensor("emt", [16, 8, 128], F32))
        posg = stC.enter_context(nc.sbuf_tensor("posg", [16, 8, 128], F32))
        racc = stC.enter_context(nc.sbuf_tensor("racc", [16, 4, 128], F32))
        rta_sb = stC.enter_context(nc.sbuf_tensor("rta_sb", [128, 2, 16], F32))
        vals_sb = stC.enter_context(nc.sbuf_tensor("vals_sb", [128, 16, 64], F32))
        zer_sb = stC.enter_context(nc.sbuf_tensor("zer_sb", [128, 2080], F32))
        lsta_sb = stC.enter_context(nc.sbuf_tensor("lsta_sb", [128, 16, 64], F32))

        wait("gpsimd", "sxg", 256)
        t_agx = opi("gpsimd", "cc", lambda e: e.collective_compute(
            "AllGather", OP.bypass, replica_groups=[list(range(8))],
            ins=[xg_i.ap().opt()], outs=[xg_o.ap().opt()])._wait_ge(sems["sxg"], 256))


        wait("tensor", "dv", t_eq_prev)
        for e8 in range(8):
            tpe = opi("tensor", "pe", lambda e, s=emask_tm[:, :, e8]:
                      e.transpose(ms_ps[0:16, 0:128], s, ident_sb[:, :]))
            wait("vector", "pe", tpe)
            tdv = opi("vector", "dv", lambda e, d=emt[:, e8, :]:
                      e.tensor_copy(d, ms_ps[0:16, 0:128]))
            wait("tensor", "dv", tdv)
        for e8 in range(8):
            opi("vector", "dv", lambda e, e_=e8: e.reduce_sum(
                rvec[0:16, e_:e_ + 1], emt[:, e_, :], axis=AX))
        t_tot = cnt["dv"]
        wait("tensor", "dv", t_tot)
        tpe_c = opi("tensor", "pe", lambda e: e.matmul(
            ms_ps[0:8, 136:137], rvec[0:16, 0:8], ones_sb[0:16, 0:1],
            start=True, stop=True))
        wait("vector", "pe", tpe_c)
        t_cv = opi("vector", "dv", lambda e: e.tensor_copy(
            small[0:8, 10:11], ms_ps[0:8, 136:137]))
        wait("scalar", "dv", t_cv)
        dma("scalar", "sct", cnt_i.rearrange("(a c) -> a c", c=1), small[0:8, 10:11])
        wait("gpsimd", "sct", 16)
        t_agc = opi("gpsimd", "cc", lambda e: e.collective_compute(
            "AllGather", OP.bypass, replica_groups=[list(range(8))],
            ins=[cnt_i.ap().opt()], outs=[cnt_o.ap().opt()])._wait_ge(sems["sct"], 16))
        wait("gpsimd", "cc", t_agc)
        mark("C1")
        t_gpc = dma("gpsimd", "gpc", cnts_ag[:, :], cnt_o.rearrange("(c e) -> c e", e=8))
        wait("gpsimd", "gpc", t_gpc)
        wait("tensor", "gpc", t_gpc)
        op0("tensor", lambda e: e.matmul(
            ms_ps[0:16, 144:152], rvec[0:16, 48:64], rvec[0:16, 0:8],
            start=True, stop=False))
        tpe_o = opi("tensor", "pe", lambda e: e.matmul(
            ms_ps[0:16, 144:152], meta_sb[0:8, 0:16], cnts_ag[:, :],
            start=False, stop=True))
        wait("vector", "pe", tpe_o)
        opi("vector", "dv", lambda e: e.tensor_copy(
            rvec[0:16, 8:16], ms_ps[0:16, 144:152]))
        opi("vector", "dv", lambda e: e.memset(racc[:, 3, :], 0.0))
        for e8 in range(8):
            opi("vector", "dv", lambda e, e_=e8: e.tensor_tensor_scan(
                posg[:, e_, :], emt[:, e_, :], racc[0:16, 3, :],
                rvec[0:16, 8 + e_:9 + e_], OP.add, OP.add))
        opi("vector", "dv", lambda e: e.memset(racc[:, 0, :], 0.0))
        opi("vector", "dv", lambda e: e.memset(racc[:, 1, :], 0.0))
        opi("vector", "dv", lambda e: e.memset(racc[:, 2, :], 0.0))
        t_fit = 0
        for e8 in range(8):
            opi("vector", "dv", lambda e, e_=e8: e.tensor_scalar(
                racc[:, 3, :], posg[:, e_, :], float(e_ * SEG - 1),
                float(e_ * SEG + SEG - 1), OP.add, OP.min))
            opi("vector", "dv", lambda e, e_=e8: e.tensor_tensor(
                racc[:, 3, :], racc[:, 3, :], emt[:, e_, :], OP.mult))
            opi("vector", "dv", lambda e: e.tensor_tensor(
                racc[:, 0, :], racc[:, 0, :], racc[:, 3, :], OP.add))
            opi("vector", "dv", lambda e, e_=e8: e.tensor_scalar(
                racc[:, 3, :], posg[:, e_, :], float(e_ * 2048 - 1),
                float(e_ * 2048 + 2047), OP.add, OP.min))
            opi("vector", "dv", lambda e, e_=e8: e.tensor_tensor(
                racc[:, 3, :], racc[:, 3, :], emt[:, e_, :], OP.mult))
            opi("vector", "dv", lambda e: e.tensor_tensor(
                racc[:, 1, :], racc[:, 1, :], racc[:, 3, :], OP.add))
            opi("vector", "dv", lambda e, e_=e8: e.tensor_scalar(
                racc[:, 3, :], posg[:, e_, :], 2048.0, None, OP.is_le))
            opi("vector", "dv", lambda e, e_=e8: e.tensor_tensor(
                racc[:, 3, :], racc[:, 3, :], emt[:, e_, :], OP.mult))
            t_fit = opi("vector", "dv", lambda e: e.tensor_tensor(
                racc[:, 2, :], racc[:, 2, :], racc[:, 3, :], OP.add))
        wait("scalar", "dv", t_fit)
        dma("scalar", "srt", rt[0:1, :].rearrange("x (p c) -> p x c", c=128),
            racc[:, 0:1, :])
        dma("scalar", "srt", rt[1:2, :].rearrange("x (p c) -> p x c", c=128),
            racc[:, 1:2, :])
        wait("sync", "srt", 32)
        mark("C2")
        dma("sync", "lrt", rta_sb[:, :, :],
            rt.rearrange("x (b g) -> b x g", g=16))
        wait("tensor", "lrt", 16)
        mark("C2a")
        for which, dst_i16 in ((0, posm_sb), (1, gidx_sb)):
            tpe = opi("tensor", "pe", lambda e, s=rta_sb[:, which, :]:
                      e.transpose(ms_ps[0:16, 0:128], s, ident_sb[:, :]))
            wait("vector", "pe", tpe)
            tdv = opi("vector", "dv", lambda e, d=dst_i16[0:16, :]:
                      e.tensor_copy(d, ms_ps[0:16, 0:128]))
            wait("tensor", "dv", tdv)
        mark("C2b")
        tpe_f = opi("tensor", "pe", lambda e: e.transpose(
            ms_ps[:, 384:400], racc[0:16, 2, :], ident_sb[0:16, 0:16]))
        wait("vector", "pe", tpe_f)
        t_ge = opi("vector", "dv", lambda e: e.tensor_tensor(
            gact_tm[:, :], gate_tm[:, :], ms_ps[:, 384:400], OP.mult))
        if KDEBUG:
            dbgs = {
                "dbg_posm": (posm_sb[0:16, :], I16, [16, 128]),
                "dbg_gidx": (gidx_sb[0:16, :], I16, [16, 128]),
                "dbg_xon": (None, None, None),
            }
            del dbgs["dbg_xon"]
            dbg_list = [
                ("dbg_posm", posm_sb[0:16, :], I16, [16, 128]),
                ("dbg_gidx", gidx_sb[0:16, :], I16, [16, 128]),
                ("dbg_racc", racc[:, :, :], F32, [16, 4, 128]),
                ("dbg_posg", posg[:, :, :], F32, [16, 8, 128]),
                ("dbg_emt", emt[:, :, :], F32, [16, 8, 128]),
                ("dbg_rvec", rvec[0:16, 0:16], F32, [16, 16]),
                ("dbg_cnts", cnts_ag[:, :], F32, [8, 8]),
                ("dbg_small", small[:, 8:10], F32, [128, 2]),
                ("dbg_gate", gate_tm[:, :], F32, [128, 16]),
                ("dbg_gact", gact_tm[:, :], F32, [128, 16]),
                ("dbg_emask", emask_tm[:, :, :], F32, [128, 16, 8]),
            ]
            wait("sync", "dv", cnt["dv"])
            wait("sync", "ac", cnt["ac"])
            for nm, ap_, dt_, shp in dbg_list:
                od = nc.dram_tensor(nm, shp, dt_, kind="ExternalOutput").ap()
                dma("sync", "lx1", od[tuple(slice(None) for _ in shp)], ap_)
            wait("sync", "lx1", cnt["lx1"])
        mark("C2c")
        wait("tensor", "dv", t_ge)
        tpe_g = opi("tensor", "pe", lambda e: e.matmul(
            ms_ps[0:8, 408:409], cnts_ag[:, :], ones_sb[0:8, 0:1],
            start=True, stop=True))
        wait("vector", "pe", tpe_g)
        t_gt = opi("vector", "dv", lambda e: e.tensor_copy(
            small[0:8, 13:14], ms_ps[0:8, 408:409]))
        wait("tensor", "dv", t_gt)
        mark("C2d")
        tpe_m = opi("tensor", "pe", lambda e: e.matmul(
            ms_ps[0:1, 400:408], small[0:8, 13:14], meta_sb[0:8, 16:24],
            start=True, stop=True))
        wait("vector", "pe", tpe_m)
        opi("vector", "dv", lambda e: e.tensor_copy(small[0:1, 8:16],
                                                    ms_ps[0:1, 400:408]))
        mark("C2e")
        opi("vector", "dv", lambda e: e.tensor_scalar(
            ccntf_sb[:, :], k512_sb[:, :], -1.0, small[0:1, 8:9], OP.mult, OP.add))
        opi("vector", "dv", lambda e: e.tensor_scalar_max(
            ccntf_sb[:, :], ccntf_sb[:, :], 0.0))
        opi("vector", "dv", lambda e: e.tensor_scalar_min(
            ccntf_sb[:, :], ccntf_sb[:, :], 512.0))
        t_ccnt = opi("vector", "dv", lambda e: e.tensor_copy(ccnt_sb[:, :],
                                                             ccntf_sb[:, :]))
        mark("C3")
        opi("vector", "dv", lambda e: e.memset(zer_sb[:, :], 0.0))
        opi("vector", "dv", lambda e: e.memset(vals_sb[:, :, :], 0.0))
        t_vals = opi("vector", "dv", lambda e: e.tensor_copy(vals_sb[:, :, 0],
                                                             ramp_sb[:, :]))
        wait("gpsimd", "dv", cnt["dv"])
        for qq in range(4):
            dma("gpsimd", "gpc",
                listtab.rearrange("t e -> (t e)").rearrange(
                    "(q p f) -> q p f", q=4, p=128)[qq, :, :],
                zer_sb[:, :])
        for rr in range(1, 8):
            dma("gpsimd", "gpc", posm_sb[16 * rr:16 * (rr + 1), :], posm_sb[0:16, :])
            dma("gpsimd", "gpc", gidx_sb[16 * rr:16 * (rr + 1), :], gidx_sb[0:16, :])
        wait("gpsimd", "gpc", cnt["gpc"])
        for kk in range(4):
            cnt["gsc"] += 16
            semh = sems["gsc"]
            emit("gpsimd", lambda e, k_=kk, sh=semh, w=16 * kk: e.dma_scatter_add(
                listtab.ap(), vals_sb[:, k_ * 4:(k_ + 1) * 4, :],
                posm_sb[:, k_ * 32:(k_ + 1) * 32], 512, 512, 64)
                .then_inc(sh, 16)._wait_ge(sh, w))
        wait("gpsimd", "gsc", 64)
        t_rs = opi("gpsimd", "cc", lambda e: e.collective_compute(
            "ReduceScatter", OP.add, replica_groups=[list(range(8))],
            ins=[listtab.ap().rearrange("t e -> (t e)").opt()],
            outs=[seg_t.ap().opt()])._wait_ge(sems["gsc"], 64))
        wait("gpsimd", "cc", t_rs)
        wait("gpsimd", "lb2", 32)
        t_lst = dma("gpsimd", "gpc", lsta_sb[:, :, :],
                    seg_t.rearrange("(b a e) -> b a e", a=16, e=64)[0:128, :, :])
        wait("gpsimd", "gpc", t_lst)
        wait("tensor", "gpc", t_lst)
        tpe_l = opi("tensor", "pe", lambda e: e.transpose(
            ms_ps[0:16, 0:128], lsta_sb[:, :, 0], ident_sb[:, :]))
        wait("vector", "pe", tpe_l)
        t_xi = opi("vector", "dv", lambda e: e.tensor_scalar(
            xidx_sb[0:16, :], ms_ps[0:16, 0:128], -1.0, None, OP.add))
        wait("gpsimd", "dv", t_xi)
        for rr in range(1, 8):
            dma("gpsimd", "gpc", xidx_sb[16 * rr:16 * (rr + 1), :], xidx_sb[0:16, :])
        t_xirep = cnt["gpc"]
        wait("gpsimd", "gpc", t_xirep)
        stC.close()
        stAB.close()

        mark("D")
        stD = ExitStack()
        w1_sb = stD.enter_context(nc.sbuf_tensor("w1_sb", [128, 8, F], BF16))
        w2_sb = stD.enter_context(nc.sbuf_tensor("w2_sb", [128, 32, D], BF16))
        xet_sb = stD.enter_context(nc.sbuf_tensor("xet_sb", [128, 4, 8, 512], BF16))
        h_sb = stD.enter_context(nc.sbuf_tensor("h_sb", [128, 32, 512], BF16))
        yt_sb = stD.enter_context(nc.sbuf_tensor("yt_sb", [128, 2, D], BF16))

        phase_barrier(["gpsimd", "vector", "scalar", "tensor", "sync"])
        dma("gpsimd", "gw1", w1_sb[:, :, :],
            w1_in.rearrange("(c p) f -> p c f", p=128))
        dma("gpsimd", "gw2", w2_sb[:, :, :],
            w2_in.rearrange("(c p) d -> p c d", p=128))
        t_xz = opi("vector", "dv", lambda e: e.memset(xet_sb[:, :, :, :], 0.0))
        wait("gpsimd", "cc", t_agx)
        wait("gpsimd", "dv", t_xz)
        for kk in range(4):
            gname = "gxg%d" % kk
            cnt[gname] += 16
            semh = sems[gname]

            def _mk_gather(e, k_=kk, sh=semh):
                cv = e.value_load(ccnt_sb[0:1, k_:k_ + 1], min_val=0, max_val=512)
                return e.dma_gather(
                    xet_sb[:, k_, :, :], xg_o.rearrange("(t d) -> t d", d=D),
                    xidx_sb[:, k_ * 32:(k_ + 1) * 32], 512, cv, D,
                    transpose=True).then_inc(sh, 16)
            emit("gpsimd", _mk_gather)

        wait("tensor", "gw1", 16)
        wait("tensor", "lb2", 32)
        y_evac = {0: 0, 1: 0}
        t_yst = [0] * 16
        for kk in range(4):
            wait("tensor", "gxg%d" % kk, 16)
            t_hrelu = [0] * 32
            for fb in range(32):
                ps = sc_ps[fb % 2]
                if fb >= 2:
                    wait("tensor", "ac", t_hrelu[fb - 2])
                for dc in range(8):
                    fn = (lambda e, l=w1_sb[:, dc, fb * 128:(fb + 1) * 128],
                          r=xet_sb[:, kk, dc, :], p=ps, s=(dc == 0), t=(dc == 7):
                          e.matmul(p[:, :], l, r, start=s, stop=t))
                    if dc < 7:
                        op0("tensor", fn)
                    else:
                        tpe_h = opi("tensor", "pe", fn)
                wait("scalar", "pe", tpe_h)
                t_hrelu[fb] = opi("scalar", "ac", lambda e, d=h_sb[:, fb, :],
                                  p=ps, b=b1_sb[:, fb:fb + 1]: e.activation(
                                      d, p[:, :], AF.Relu, bias=b, scale=1.0))
            if kk == 0:
                wait("tensor", "gw2", 16)
            wait("tensor", "ac", t_hrelu[31])
            for sb4 in range(4):
                islot = kk * 4 + sb4
                if islot >= 2:
                    wait("tensor", "syi%d" % (islot - 2), 16)
                if y_evac[islot % 2]:
                    wait("tensor", "dv", y_evac[islot % 2])
                ybanks = [ao_ps[0], ao_ps[1]] if islot % 2 == 0 else [z_ps[0], z_ps[1]]
                for half in range(2):
                    ps = ybanks[half]
                    for fb in range(32):
                        op0("tensor", lambda e, l=h_sb[:, fb, sb4 * 128:(sb4 + 1) * 128],
                            r=w2_sb[:, fb, half * 512:(half + 1) * 512], p=ps,
                            s=(fb == 0), t=False:
                            e.matmul(p[:, :], l, r, start=s, stop=t))
                    tpe_y = opi("tensor", "pe",
                                lambda e, r=b2hl_sb[:, half * 512:(half + 1) * 512],
                                p=ps: e.matmul(p[:, :], onesb_sb[:, :], r,
                                               start=False, stop=True))
                wait("vector", "pe", tpe_y)
                tdv_y = 0
                for half in range(2):
                    tdv_y = opi("vector", "dv", lambda e,
                                d=yt_sb[:, islot % 2, half * 512:(half + 1) * 512],
                                p=ybanks[half]: e.tensor_copy(d, p[:, :]))
                y_evac[islot % 2] = tdv_y
                wait("scalar", "dv", tdv_y)
                dma("scalar", "syi%d" % islot,
                    yg_i.rearrange("(t d) -> t d", d=D)[islot * 128:(islot + 1) * 128, :],
                    yt_sb[:, islot % 2, :])
        stD.close()

        mark("E")
        stE = ExitStack()
        ytok_sb = stE.enter_context(nc.sbuf_tensor("ytok_sb", [128, 16, D], BF16))
        x2_sb = stE.enter_context(nc.sbuf_tensor("x2_sb", [128, D], F32))
        sq2_sb = stE.enter_context(nc.sbuf_tensor("sq2_sb", [128, D], F32))

        for ii in range(16):
            wait("gpsimd", "syi%d" % ii, 16)
        t_gss = opi("gpsimd", "cc", lambda e: e.nop())
        t_agy = opi("gpsimd", "cc", lambda e: e.collective_compute(
            "AllGather", OP.bypass, replica_groups=[list(range(8))],
            ins=[yg_i.ap().opt()], outs=[yg_o.ap().opt()])._wait_ge(sems["cc"], t_gss))
        wait("gpsimd", "cc", t_agy)
        phase_barrier(["gpsimd", "vector", "scalar", "tensor", "sync"])
        for kk in range(4):
            gname = "gyg%d" % kk
            cnt[gname] += 16
            semh = sems[gname]
            emit("gpsimd", lambda e, k_=kk, sh=semh: e.dma_gather(
                ytok_sb[:, k_ * 4:(k_ + 1) * 4, :], yg_o.rearrange("(t d) -> t d", d=D),
                gidx_sb[:, k_ * 32:(k_ + 1) * 32], 512, 512, D).then_inc(sh, 16))

        for tb in range(16):
            if tb > 0:
                wait("sync", "soute" if (tb - 1) % 2 == 0 else "souto",
                     16 * ((tb - 1) // 2 + 1))
            t_x1r = dma("sync", "lx1", x2_sb[:, :], x1f[tb * 128:(tb + 1) * 128, :])
            wait("vector", "lx1", t_x1r)
            wait("vector", "gyg%d" % (tb // 4), 16)
            opi("vector", "dv", lambda e, tb_=tb: e.scalar_tensor_tensor(
                x2_sb[:, :], ytok_sb[:, tb_, :], gact_tm[:, tb_:tb_ + 1], x2_sb[:, :],
                OP.mult, OP.add))
            opi("vector", "dv", lambda e: e.reduce_sum(
                lnv_sb[:, 0:1], x2_sb[:, :], axis=AX))
            opi("vector", "dv", lambda e: e.tensor_scalar(
                lnv_sb[:, 1:2], lnv_sb[:, 0:1], 1.0 / D, None, OP.mult))
            opi("vector", "dv", lambda e: e.scalar_tensor_tensor(
                sq2_sb[:, :], x2_sb[:, :], 1.0, x2_sb[:, :], OP.mult, OP.mult,
                accum_out=lnv_sb[:, 2:3]))
            opi("vector", "dv", lambda e: e.tensor_scalar(
                lnv_sb[:, 3:4], lnv_sb[:, 2:3], 1.0 / D, EPS, OP.mult, OP.add))
            opi("vector", "dv", lambda e: e.tensor_tensor(
                lnv_sb[:, 4:5], lnv_sb[:, 1:2], lnv_sb[:, 1:2], OP.mult))
            t_var2 = opi("vector", "dv", lambda e: e.tensor_sub(
                lnv_sb[:, 3:4], lnv_sb[:, 3:4], lnv_sb[:, 4:5]))
            wait("scalar", "dv", t_var2)
            opi("scalar", "ac", lambda e: e.activation(
                lnv_sb[:, 5:6], lnv_sb[:, 3:4], AF.Ln, bias=0.0, scale=1.0))
            t_rs2 = opi("scalar", "ac", lambda e: e.activation(
                lnv_sb[:, 6:7], lnv_sb[:, 5:6], AF.Exp, bias=0.0, scale=-0.5))
            wait("vector", "ac", t_rs2)
            t_x2 = opi("vector", "dv", lambda e: e.tensor_scalar(
                x2_sb[:, :], x2_sb[:, :], lnv_sb[:, 1:2], lnv_sb[:, 6:7],
                OP.subtract, OP.mult))
            wait("scalar", "dv", t_x2)
            dma("scalar", "soute" if tb % 2 == 0 else "souto",
                out_t[tb * 128:(tb + 1) * 128, :], x2_sb[:, :])
        stE.close()

        wait("scalar", "soute", cnt["soute"])
        wait("scalar", "souto", cnt["souto"])
        for nm in ("gxg0", "gxg1", "gxg2", "gxg3", "gyg0", "gyg1", "gyg2", "gyg3",
                   "gw1", "gw2", "gpc", "gsc", "gpw", "gpwt"):
            wait("gpsimd", nm, cnt[nm])
        wait("gpsimd", "cc", cnt["cc"])
        for nm in ("ldc", "lx0e", "lx0o", "lb2", "lx1", "srt", "sxf", "sxg"):
            wait("sync", nm, cnt[nm])
        wait("scalar", "sx0e", cnt["sx0e"])
        wait("scalar", "sx0o", cnt["sx0o"])
        for ii in range(16):
            wait("scalar", "syi%d" % ii, cnt["syi%d" % ii])
        wait("scalar", "sct", cnt["sct"])

        if PHASE_LIMIT in marks:
            cut, csnap = marks[PHASE_LIMIT]
            for k in plan:
                del plan[k][cut[k]:]
            eng_for = {"soute": "scalar", "souto": "scalar", "sct": "scalar",
                       "srt": "scalar", "sxf": "scalar", "sxg": "scalar",
                       "sx0e": "scalar", "sx0o": "scalar",
                       "ldc": "sync", "lx0e": "sync", "lx0o": "sync",
                       "lb2": "sync", "lx1": "sync", "lrt": "sync",
                       "cc": "gpsimd"}
            for nm, v in csnap.items():
                if v > 0 and not nm.startswith(("pe", "dv", "ac")):
                    eng = eng_for.get(nm, "gpsimd")
                    if nm.startswith("syi"):
                        eng = "scalar"
                    semh = sems[nm]
                    plan[eng].append(lambda e, s=semh, vv=v: e.wait_ge(s, vv))
        blk = ex(nc.Block())

        @blk.tensor
        def _(e):
            for fn in plan["tensor"]:
                fn(e)

        @blk.vector
        def _(e):
            for fn in plan["vector"]:
                fn(e)

        @blk.scalar
        def _(e):
            for fn in plan["scalar"]:
                fn(e)

        @blk.gpsimd
        def _(e):
            for fn in plan["gpsimd"]:
                fn(e)

        @blk.sync
        def _(e):
            for fn in plan["sync"]:
                fn(e)

    nc.compile()
    return nc


_CACHE = {}


def kernel(L0_attn_x, L0_attn_W_Q, L0_attn_W_K, L0_attn_W_V, L0_attn_W_O,
           L0_ffn_router_w, L0_ffn_w1, L0_ffn_b1, L0_ffn_w2, L0_ffn_b2):
    x = np.ascontiguousarray(np.asarray(L0_attn_x, dtype=np.float32))
    wq = np.ascontiguousarray(np.asarray(L0_attn_W_Q, np.float32).reshape(D, D).T)
    wk = np.ascontiguousarray(np.asarray(L0_attn_W_K, np.float32).reshape(D, D).T)
    wv = np.ascontiguousarray(np.asarray(L0_attn_W_V, np.float32).reshape(D, D).T)
    wo = np.ascontiguousarray(
        np.asarray(L0_attn_W_O, np.float32).transpose(0, 2, 1).reshape(D, D))
    rw = np.ascontiguousarray(np.asarray(L0_ffn_router_w, np.float32))
    w1 = np.ascontiguousarray(np.asarray(L0_ffn_w1, np.float32))
    b1 = np.ascontiguousarray(np.asarray(L0_ffn_b1, np.float32))
    w2 = np.ascontiguousarray(np.asarray(L0_ffn_w2, np.float32))
    b2 = np.ascontiguousarray(np.asarray(L0_ffn_b2, np.float32))

    if "nc" not in _CACHE:
        _CACHE["nc"] = build_nc()
    nc = _CACHE["nc"]

    in_maps = []
    sidx = np.arange(S)
    for c in range(8):
        masklt = np.zeros((8, 16), np.float32)
        masklt[:c, :] = 1.0
        onehot = np.zeros((8, 8), np.float32)
        onehot[c, :] = 1.0
        ramp = np.zeros((128, 16), np.float32)
        ramp[sidx % 128, sidx // 128] = c * S + sidx + 1
        in_maps.append({
            "x_in": x[c], "wq_in": wq, "wk_in": wk, "wv_in": wv, "wo_in": wo,
            "rw_in": rw, "w1_in": w1[c], "b1_in": b1[c], "w2_in": w2[c],
            "b2_in": b2[c:c + 1], "masklt_in": masklt, "onehot_in": onehot,
            "ramp_in": ramp,
        })
    res = run_bass_kernel_spmd(nc, in_maps, core_ids=list(range(8)))
    return np.stack([res.results[c]["out"] for c in range(8)], axis=0)


# version 13 (final)
# speedup vs baseline: 1.2607x; 1.0132x over previous
import os as _os
import numpy as np
import concourse.bacc as bacc_mod
import concourse.mybir as mybir
from concourse.bass_utils import run_bass_kernel_spmd

F32 = mybir.dt.float32
BF16 = mybir.dt.bfloat16
I16 = mybir.dt.int16
I32 = mybir.dt.int32
OP = mybir.AluOpType
AF = mybir.ActivationFunctionType
AX = mybir.AxisListType.X

S = 2048
D = 1024
NH = 16
F = 4096
T = 16384
SEG = 2080
EPS = 1e-5
BIGE = 65536.0


PHASE_LIMIT = _os.environ.get("KPHASE", "Z")
KDEBUG = _os.environ.get("KDEBUG", "") == "1"


def build_nc():
    nc = bacc_mod.Bacc("TRN2", target_bir_lowering=False)

    x_in = nc.dram_tensor("x_in", [S, D], F32, kind="ExternalInput").ap()
    wq_in = nc.dram_tensor("wq_in", [D, D], F32, kind="ExternalInput").ap()
    wk_in = nc.dram_tensor("wk_in", [D, D], F32, kind="ExternalInput").ap()
    wv_in = nc.dram_tensor("wv_in", [D, D], F32, kind="ExternalInput").ap()
    wo_in = nc.dram_tensor("wo_in", [D, D], F32, kind="ExternalInput").ap()
    rw_in = nc.dram_tensor("rw_in", [D, 8], F32, kind="ExternalInput").ap()
    w1_in = nc.dram_tensor("w1_in", [D, F], F32, kind="ExternalInput").ap()
    b1_in = nc.dram_tensor("b1_in", [F], F32, kind="ExternalInput").ap()
    w2_in = nc.dram_tensor("w2_in", [F, D], F32, kind="ExternalInput").ap()
    b2_in = nc.dram_tensor("b2_in", [1, D], F32, kind="ExternalInput").ap()
    masklt_in = nc.dram_tensor("masklt_in", [8, 16], F32, kind="ExternalInput").ap()
    onehot_in = nc.dram_tensor("onehot_in", [8, 8], F32, kind="ExternalInput").ap()
    ramp_in = nc.dram_tensor("ramp_in", [128, 16], F32, kind="ExternalInput").ap()
    out_t = nc.dram_tensor("out", [S, D], F32, kind="ExternalOutput").ap()

    x1f = nc.dram_tensor("x1f", [S, D], F32)
    xg_i = nc.dram_tensor("xg_i", [S * D], BF16)
    xg_o = nc.dram_tensor("xg_o", [T * D], BF16)
    cnt_i = nc.dram_tensor("cnt_i", [8], F32)
    cnt_o = nc.dram_tensor("cnt_o", [64], F32)
    listtab = nc.dram_tensor("listtab", [8 * SEG, 64], F32)
    seg_t = nc.dram_tensor("seg_t", [SEG * 64], F32)
    rt = nc.dram_tensor("rt", [2, S], F32)
    yg_i = nc.dram_tensor("yg_i", [S * D], BF16)
    yg_o = nc.dram_tensor("yg_o", [T * D], BF16)

    ident = nc.inline_tensor(np.eye(128, dtype=np.float32), name="cident").ap()
    sl16_np = np.zeros((16, 16), np.float32)
    for k in range(16):
        sl16_np[k, k + 1:] = 1.0
    sl16_d = nc.inline_tensor(sl16_np, name="csl16").ap()
    masks_np = np.zeros((128, 4, 512), np.float32)
    for dd in range(4):
        for p in range(128):
            if p + dd * 128 < 512:
                masks_np[p, dd, p + dd * 128:] = 1.0
    masks_d = nc.inline_tensor(masks_np, name="cmasks").ap()
    erowp_d = nc.inline_tensor(
        np.tile(np.arange(8, dtype=np.float32)[None, :] + BIGE, (128, 1)),
        name="cerowp").ap()
    erow_d = nc.inline_tensor(
        np.tile(np.arange(8, dtype=np.float32)[None, :], (128, 1)), name="cerow").ap()
    k512_d = nc.inline_tensor(
        np.array([[0.0, 512.0, 1024.0, 1536.0, 0.0, 0.0, 0.0, 0.0]], np.float32),
        name="ck512").ap()

    from contextlib import ExitStack
    with ExitStack() as stk:
        ex = stk.enter_context

        ident_sb = ex(nc.sbuf_tensor("ident_sb", [128, 128], F32))
        ones_sb = ex(nc.sbuf_tensor("ones_sb", [128, 64], F32))
        onesb_sb = ex(nc.sbuf_tensor("onesb_sb", [2, 128], BF16))
        erow_sb = ex(nc.sbuf_tensor("erow_sb", [128, 8], F32))
        erowp_sb = ex(nc.sbuf_tensor("erowp_sb", [128, 8], F32))
        k512_sb = ex(nc.sbuf_tensor("k512_sb", [1, 8], F32))
        gate_tm = ex(nc.sbuf_tensor("gate_tm", [128, 16], F32))
        gact_tm = ex(nc.sbuf_tensor("gact_tm", [128, 16], F32))
        emask_tm = ex(nc.sbuf_tensor("emask_tm", [128, 16, 8], F32))
        small = ex(nc.sbuf_tensor("small", [128, 32], F32))
        lnv_sb = ex(nc.sbuf_tensor("lnv_sb", [128, 8], F32))
        posm_sb = ex(nc.sbuf_tensor("posm_sb", [128, 128], I16))
        gidx_sb = ex(nc.sbuf_tensor("gidx_sb", [128, 128], I16))
        xidx_sb = ex(nc.sbuf_tensor("xidx_sb", [128, 128], I16))
        ccnt_sb = ex(nc.sbuf_tensor("ccnt_sb", [1, 8], I32))
        ccntf_sb = ex(nc.sbuf_tensor("ccntf_sb", [1, 8], F32))
        b1_sb = ex(nc.sbuf_tensor("b1_sb", [128, 32], F32))
        b2hl_sb = ex(nc.sbuf_tensor("b2hl_sb", [2, D], BF16))
        meta_sb = ex(nc.sbuf_tensor("meta_sb", [8, 24], F32))
        rvec = ex(nc.sbuf_tensor("rvec", [16, 64], F32))
        ramp_sb = ex(nc.sbuf_tensor("ramp_sb", [128, 16], F32))
        cnts_ag = ex(nc.sbuf_tensor("cnts_ag", [8, 8], F32))

        sc_ps = [ex(nc.psum_tensor("sc_ps%d" % i, [128, 512], F32)) for i in range(2)]
        z_ps = [ex(nc.psum_tensor("z_ps%d" % i, [128, 512], F32)) for i in range(2)]
        b_ps = ex(nc.psum_tensor("b_ps", [64, 512], F32))
        ao_ps = [ex(nc.psum_tensor("ao_ps%d" % i, [128, 512], F32)) for i in range(2)]
        ms_ps = ex(nc.psum_tensor("ms_ps", [128, 512], F32))

        sem_names = ["syi%d" % i for i in range(16)] + ["ldc", "lx0e", "lx0o", "lb2", "lx1", "gpw", "gpwt", "gpc",
                     "gsc", "gw1", "gw2", "lrt",
                     "gxg0", "gxg1", "gxg2", "gxg3", "gyg0", "gyg1", "gyg2", "gyg3",
                     "sx0e", "sx0o", "sxf", "sxg", "sct", "srt",
                     "soute", "souto",
                     "pe", "dv", "ac", "cc"]
        sems = {n: ex(nc.semaphore(n)) for n in sem_names}
        cnt = {n: 0 for n in sem_names}
        plan = {k: [] for k in ("tensor", "vector", "scalar", "gpsimd", "sync")}

        def emit(eng, fn):
            plan[eng].append(fn)

        def dma(eng, sem, out_ap, in_ap, **kw):
            cnt[sem] += 16
            semh = sems[sem]
            emit(eng, lambda e, o=out_ap, i=in_ap: e.dma_start(out=o, in_=i, **kw)
                 .then_inc(semh, 16))
            return cnt[sem]

        def wait(eng, sem, val):
            semh = sems[sem]
            emit(eng, lambda e, v=val: e.wait_ge(semh, v))

        ENG_SEM = {"vector": "dv", "scalar": "ac"}

        def opi(eng, sem, fn):
            own = ENG_SEM.get(eng)
            if own is not None and cnt[own] > 0:
                semo = sems[own]
                emit(eng, lambda e, v=cnt[own]: e.wait_ge(semo, v))
            cnt[sem] += 1
            semh = sems[sem]
            emit(eng, lambda e, f=fn: f(e).then_inc(semh, 1))
            return cnt[sem]

        def op0(eng, fn):
            emit(eng, lambda e, f=fn: f(e))

        marks = {}

        def mark(name):
            marks[name] = ({k: len(v) for k, v in plan.items()}, dict(cnt))

        def phase_barrier(engines):
            snap = [(nm, cnt[nm]) for nm in sem_names if cnt[nm] > 0]
            for eng in engines:
                for nm, v in snap:
                    wait(eng, nm, v)

        stAB = ExitStack()
        v_tok = stAB.enter_context(nc.sbuf_tensor("v_tok", [128, 16, NH * 65], BF16))
        q_sb = stAB.enter_context(nc.sbuf_tensor("q_sb", [128, 8, S], BF16))
        k_sb = stAB.enter_context(nc.sbuf_tensor("k_sb", [128, 8, S], BF16))
        wo_sb = stAB.enter_context(nc.sbuf_tensor("wo_sb", [128, 8, D], BF16))
        masks_sb = stAB.enter_context(nc.sbuf_tensor("masks_sb", [128, 4, 512], BF16))
        rw_sb = stAB.enter_context(nc.sbuf_tensor("rw_sb", [128, 8, 8], F32))

        stA = ExitStack()
        wtmp_sb = stA.enter_context(nc.sbuf_tensor("wtmp_sb", [128, 8, D], BF16))
        x0t_sb = stA.enter_context(nc.sbuf_tensor("x0t_sb", [128, 8, S], BF16))
        x0_sb = stA.enter_context(nc.sbuf_tensor("x0_sb", [128, 2, D], F32))
        b2s_sb = stA.enter_context(nc.sbuf_tensor("b2s_sb", [1, 2 * D], F32))
        b1t_sb = stA.enter_context(nc.sbuf_tensor("b1t_sb", [32, 128], F32))
        b2w_sb = stA.enter_context(nc.sbuf_tensor("b2w_sb", [1, 2, D], BF16))

        dma("sync", "ldc", ident_sb[:, :], ident[:, :])
        dma("sync", "ldc", erow_sb[:, :], erow_d[:, :])
        dma("sync", "ldc", erowp_sb[:, :], erowp_d[:, :])
        dma("sync", "ldc", k512_sb[:, :], k512_d[:, :])
        dma("sync", "ldc", rw_sb[:, :, :], rw_in.rearrange("(c p) e -> p c e", p=128))
        dma("sync", "ldc", meta_sb[:, 0:16], masklt_in[:, :])
        dma("sync", "ldc", meta_sb[:, 16:24], onehot_in[:, :])
        dma("sync", "ldc", b1t_sb[:, :], b1_in.rearrange("(a p) -> a p", p=128))
        dma("sync", "ldc", b2s_sb[0:1, 0:D], b2_in[:, :])
        dma("sync", "ldc", rvec[0:16, 48:64], sl16_d[:, :])
        T_LDC = dma("sync", "ldc", ramp_sb[:, :], ramp_in[:, :])
        assert T_LDC == 176
        dma("gpsimd", "gpw", masks_sb[:, :, :], masks_d[:, :, :])
        T_GPW = dma("gpsimd", "gpw", wo_sb[:, :, :], wo_in.rearrange("(c p) e -> p c e", p=128))

        opi("vector", "dv", lambda e: e.memset(ones_sb[:, :], 1.0))
        opi("vector", "dv", lambda e: e.memset(onesb_sb[:, :], 1.0))
        t_vones = opi("vector", "dv", lambda e: e.memset(
            v_tok.ap().rearrange("p a (h c) -> p a h c", c=65)[:, :, :, 64:65], 1.0))
        wait("vector", "ldc", T_LDC)
        opi("vector", "dv", lambda e: e.tensor_copy(b2w_sb[0:1, 0, :], b2s_sb[0:1, 0:D]))
        opi("vector", "dv", lambda e: e.tensor_copy(b2s_sb[0:1, D:2 * D], b2w_sb[0:1, 0, :]))
        opi("vector", "dv", lambda e: e.tensor_sub(
            b2s_sb[0:1, D:2 * D], b2s_sb[0:1, 0:D], b2s_sb[0:1, D:2 * D]))
        t_b2pre = opi("vector", "dv", lambda e: e.tensor_copy(
            b2w_sb[0:1, 1, :], b2s_sb[0:1, D:2 * D]))
        wait("sync", "dv", t_b2pre)
        dma("sync", "lb2", b2hl_sb[0:1, :], b2w_sb[0:1, 0, :])
        t_b2l_dma = dma("sync", "lb2", b2hl_sb[1:2, :], b2w_sb[0:1, 1, :])

        wait("tensor", "ldc", T_LDC)
        tpe_b1 = opi("tensor", "pe", lambda e: e.transpose(
            ms_ps[:, 0:32], b1t_sb[:, :], ident_sb[0:32, 0:32]))
        wait("vector", "pe", tpe_b1)
        tdv_b1 = opi("vector", "dv", lambda e: e.tensor_copy(b1_sb[:, :], ms_ps[:, 0:32]))
        wait("tensor", "dv", tdv_b1)
        t_xtr_done = [0] * 16
        trA_banks = [ms_ps, z_ps[0], z_ps[1]]
        t_trA = {0: 0, 1: 0, 2: 0}
        for tb in range(16):
            if tb >= 2:
                wait("sync", "dv", t_xtr_done[tb - 2])
            t_ld = dma("sync", "lx0e" if tb % 2 == 0 else "lx0o",
                       x0_sb[:, tb % 2, :], x_in[tb * 128:(tb + 1) * 128, :])
            wait("tensor", "lx0e" if tb % 2 == 0 else "lx0o", 16 * (tb // 2 + 1))
            for dc in range(8):
                pb = trA_banks[dc % 3]
                if t_trA[dc % 3]:
                    wait("tensor", "dv", t_trA[dc % 3])
                tpe = opi("tensor", "pe", lambda e, p=pb,
                          s=x0_sb[:, tb % 2, dc * 128:(dc + 1) * 128]:
                          e.transpose(p[:, 0:128], s, ident_sb[:, :]))
                wait("vector", "pe", tpe)
                t_trA[dc % 3] = opi("vector", "dv",
                                    lambda e, p=pb,
                                    d=x0t_sb[:, dc, tb * 128:(tb + 1) * 128]:
                                    e.tensor_copy(d, p[:, 0:128]))
            t_xtr_done[tb] = cnt["dv"]
        wait("tensor", "dv", max(t_trA.values()))

        dma("gpsimd", "gpwt", wtmp_sb[:, :, :],
            wv_in.rearrange("(c p) a -> p c a", p=128))
        wait("tensor", "gpwt", 16)
        wait("tensor", "dv", t_vones)
        last_evac = {0: 0, 1: 0}
        for tb in range(16):
            for half in range(2):
                ps = sc_ps[half]
                if last_evac[half]:
                    wait("tensor", "dv", last_evac[half])
                for dc in range(8):
                    fn = (lambda e, l=x0t_sb[:, dc, tb * 128:(tb + 1) * 128],
                          r=wtmp_sb[:, dc, half * 512:(half + 1) * 512], p=ps,
                          s=(dc == 0), t=(dc == 7):
                          e.matmul(p[:, :], l, r, start=s, stop=t))
                    if dc < 7:
                        op0("tensor", fn)
                    else:
                        tpe = opi("tensor", "pe", fn)
                dst = v_tok.ap().rearrange("p a (h c) -> p a h c", c=65)[
                    :, tb, half * 8:(half + 1) * 8, 0:64]
                wait("vector", "pe", tpe)
                last_evac[half] = opi("vector", "dv",
                                      lambda e, d=dst, p=ps: e.tensor_copy(d, p[:, :]))
        for wi, (w_dram, dsts) in enumerate(((wq_in, q_sb), (wk_in, k_sb))):
            wait("gpsimd", "dv", cnt["dv"])
            dma("gpsimd", "gpwt", wtmp_sb[:, :, :],
                w_dram.rearrange("(c p) a -> p c a", p=128))
            wait("tensor", "gpwt", 16 * (wi + 2))
            for ac in range(8):
                for tt in range(4):
                    ps = sc_ps[tt % 2]
                    if last_evac[tt % 2]:
                        wait("tensor", "dv", last_evac[tt % 2])
                    for dc in range(8):
                        fn = (lambda e, l=wtmp_sb[:, dc, ac * 128:(ac + 1) * 128],
                              r=x0t_sb[:, dc, tt * 512:(tt + 1) * 512], p=ps,
                              s=(dc == 0), t=(dc == 7):
                              e.matmul(p[:, :], l, r, start=s, stop=t))
                        if dc < 7:
                            op0("tensor", fn)
                        else:
                            tpe = opi("tensor", "pe", fn)
                    wait("vector", "pe", tpe)
                    last_evac[tt % 2] = opi(
                        "vector", "dv", lambda e, d=dsts[:, ac, tt * 512:(tt + 1) * 512],
                        p=ps: e.tensor_copy(d, p[:, :]))
        stA.close()

        mark("B")
        strip = stAB.enter_context(nc.sbuf_tensor("strip", [128, 16, 512], BF16))
        z_sb = stAB.enter_context(nc.sbuf_tensor("z_sb", [128, 8, 512], BF16))
        recip_sb = stAB.enter_context(nc.sbuf_tensor("recip_sb", [1, 1024], F32))
        bsb_sb = stAB.enter_context(nc.sbuf_tensor("bsb_sb", [64, 512], F32))
        x1_sb = stAB.enter_context(nc.sbuf_tensor("x1_sb", [128, D], F32))
        x1b_sb = stAB.enter_context(nc.sbuf_tensor("x1b_sb", [128, D], BF16))
        x1t_sb = stAB.enter_context(nc.sbuf_tensor("x1t_sb", [128, 8, 128], F32))
        sq_sb = stAB.enter_context(nc.sbuf_tensor("sq_sb", [128, D], F32))

        wait("vector", "lb2", 32)
        wait("scalar", "lb2", 32)
        wait("tensor", "gpw", T_GPW)
        wait("tensor", "dv", max(last_evac[0], last_evac[1]))
        t_eq_prev = 0
        t_zn_par = {0: t_trA[1], 1: t_trA[2]}
        t_zn_prev = 0
        banks4 = [sc_ps[0], sc_ps[1], ao_ps[0], ao_ps[1]]
        last_sc_exp = {0: 0, 1: 0, 2: 0, 3: 0}
        ao_guard = {2: 0, 3: 0}
        pend = [None]
        t_zn_last = [0]
        tr_banks = [ms_ps, z_ps[0]]
        t_trcp = {0: 0, 1: 0}

        def emit_tail():
            t_rc_p, zp_p, rowb_p, ahc_p, par_p = pend[0]
            pend[0] = None
            wait("tensor", "dv", max(t_rc_p, t_zn_last[0]))
            tpe_b = opi("tensor", "pe", lambda e,
                        rp=recip_sb[0:1, par_p * 512:(par_p + 1) * 512]: e.matmul(
                            b_ps[:, :], ones_sb[0:1, 0:64], rp, start=True, stop=True))
            wait("vector", "pe", tpe_b)
            opi("vector", "dv", lambda e: e.tensor_copy(bsb_sb[:, :], b_ps[:, :]))
            t_zn = opi("vector", "dv", lambda e, d=z_sb[rowb_p:rowb_p + 64, ahc_p, :],
                       p=zp_p: e.tensor_tensor(d, p[0:64, :], bsb_sb[:, :], OP.mult))
            t_zn_par[par_p] = t_zn
            t_zn_last[0] = t_zn
            return t_zn

        for j in range(4):
            nch = 4 * j + 4
            for a in range(NH):
                par = (j * 16 + a) % 2
                rowb = 64 * (a % 2)
                ahc = a // 2
                t_exp = [0] * nch
                t_msk = [0] * nch
                for i in range(nch):
                    b4 = i % 4
                    ps = banks4[b4]
                    if b4 >= 2 and ao_guard[b4]:
                        wait("tensor", "dv", ao_guard[b4])
                        ao_guard[b4] = 0
                    if last_sc_exp[b4]:
                        wait("tensor", "ac", last_sc_exp[b4])
                    tpe = opi("tensor", "pe", lambda e,
                              l=q_sb[rowb:rowb + 64, ahc, i * 128:(i + 1) * 128],
                              r=k_sb[rowb:rowb + 64, ahc, j * 512:(j + 1) * 512],
                              p=ps: e.matmul(p[:, :], l, r, start=True, stop=True))
                    wait("scalar", "pe", tpe)
                    t_exp[i] = opi("scalar", "ac", lambda e, d=strip[:, i, :], p=ps:
                                   e.activation(d, p[:, :], AF.Exp, bias=0.0, scale=0.125))
                    last_sc_exp[b4] = t_exp[i]
                    if i >= 4 * j:
                        wait("vector", "ac", t_exp[i])
                        t_msk[i] = opi("vector", "dv", lambda e, s=strip[:, i, :],
                                       m=masks_sb[:, i - 4 * j, :]:
                                       e.tensor_tensor(s, s, m, OP.mult))
                zp = z_ps[par]
                if t_zn_par[par]:
                    wait("tensor", "dv", t_zn_par[par])
                for i in range(nch):
                    if t_msk[i]:
                        wait("tensor", "dv", t_msk[i])
                    else:
                        wait("tensor", "ac", t_exp[i])
                    fn = (lambda e, l=v_tok[:, i, a * 65:(a + 1) * 65],
                          r=strip[:, i, :], p=zp, s=(i == 0), t=(i == nch - 1):
                          e.matmul(p[0:65, :], l, r, start=s, stop=t))
                    if i < nch - 1:
                        op0("tensor", fn)
                    else:
                        tpe_z = opi("tensor", "pe", fn)
                wait("vector", "pe", tpe_z)
                t_rc = opi("vector", "dv", lambda e, p=zp, pr=par:
                           e.reciprocal(recip_sb[0:1, pr * 512:(pr + 1) * 512],
                                        p[64:65, :]))
                if pend[0] is not None:
                    t_zn_prev = emit_tail()
                pend[0] = (t_rc, zp, rowb, ahc, par)
            t_zn_prev = emit_tail()

            wait("tensor", "gpw", T_GPW)
            wait("tensor", "dv", t_zn_prev)
            for s4 in range(4):
                tb = j * 4 + s4
                for half in range(2):
                    ps = ao_ps[half]
                    for ahc in range(8):
                        fn = (lambda e, l=z_sb[:, ahc, s4 * 128:(s4 + 1) * 128],
                              r=wo_sb[:, ahc, half * 512:(half + 1) * 512], p=ps,
                              s=(ahc == 0), t=(ahc == 7):
                              e.matmul(p[:, :], l, r, start=s, stop=t))
                        if ahc < 7:
                            op0("tensor", fn)
                        else:
                            tpe_ao = opi("tensor", "pe", fn)
                if tb > 0:
                    wait("scalar", "pe", t_rtr_last)
                    wait("scalar", "sxf", 16 * tb)
                sx0 = "sx0e" if tb % 2 == 0 else "sx0o"
                t_x0r = dma("scalar", sx0, x1_sb[:, :], x_in[tb * 128:(tb + 1) * 128, :])
                wait("vector", sx0, 16 * (tb // 2 + 1))
                wait("vector", "pe", tpe_ao)
                for half in range(2):
                    opi("vector", "dv", lambda e, h=half: e.tensor_tensor(
                        x1_sb[:, h * 512:(h + 1) * 512], x1_sb[:, h * 512:(h + 1) * 512],
                        ao_ps[h][:, :], OP.add))
                t_res = cnt["dv"]
                wait("tensor", "dv", t_res)
                opi("vector", "dv", lambda e: e.reduce_sum(
                    lnv_sb[:, 0:1], x1_sb[:, :], axis=AX))
                opi("vector", "dv", lambda e: e.tensor_scalar(
                    lnv_sb[:, 1:2], lnv_sb[:, 0:1], 1.0 / D, None, OP.mult))
                t_sq = opi("vector", "dv", lambda e: e.scalar_tensor_tensor(
                    sq_sb[:, :], x1_sb[:, :], 1.0, x1_sb[:, :], OP.mult, OP.mult,
                    accum_out=lnv_sb[:, 2:3]))
                opi("vector", "dv", lambda e: e.tensor_scalar(
                    lnv_sb[:, 3:4], lnv_sb[:, 2:3], 1.0 / D, EPS, OP.mult, OP.add))
                opi("vector", "dv", lambda e: e.tensor_tensor(
                    lnv_sb[:, 4:5], lnv_sb[:, 1:2], lnv_sb[:, 1:2], OP.mult))
                t_var = opi("vector", "dv", lambda e: e.tensor_sub(
                    lnv_sb[:, 3:4], lnv_sb[:, 3:4], lnv_sb[:, 4:5]))
                wait("scalar", "dv", t_var)
                opi("scalar", "ac", lambda e: e.activation(
                    lnv_sb[:, 5:6], lnv_sb[:, 3:4], AF.Ln, bias=0.0, scale=1.0))
                t_rs = opi("scalar", "ac", lambda e: e.activation(
                    lnv_sb[:, 6:7], lnv_sb[:, 5:6], AF.Exp, bias=0.0, scale=-0.5))
                wait("vector", "ac", t_rs)
                t_x1 = opi("vector", "dv", lambda e: e.tensor_scalar(
                    x1_sb[:, :], x1_sb[:, :], lnv_sb[:, 1:2], lnv_sb[:, 6:7],
                    OP.subtract, OP.mult))
                wait("scalar", "dv", t_x1)
                dma("scalar", "sxf", x1f[tb * 128:(tb + 1) * 128, :], x1_sb[:, :])
                if tb > 0:
                    wait("vector", "sxg", 16 * tb)
                t_x1b = opi("vector", "dv", lambda e: e.tensor_copy(
                    x1b_sb[:, :], x1_sb[:, :]))
                wait("scalar", "dv", t_x1b)
                dma("scalar", "sxg",
                    xg_i.rearrange("(t d) -> t d", d=D)[tb * 128:(tb + 1) * 128, :],
                    x1b_sb[:, :])
                wait("tensor", "dv", t_x1)
                for dc in range(8):
                    pb = tr_banks[dc % 2]
                    if t_trcp[dc % 2]:
                        wait("tensor", "dv", t_trcp[dc % 2])
                    tpe_t = opi("tensor", "pe", lambda e, p=pb,
                                s=x1_sb[:, dc * 128:(dc + 1) * 128]:
                                e.transpose(p[:, 0:128], s, ident_sb[:, :]))
                    wait("vector", "pe", tpe_t)
                    t_trcp[dc % 2] = opi("vector", "dv", lambda e, p=pb,
                                         d=x1t_sb[:, dc, :]:
                                         e.tensor_copy(d, p[:, 0:128]))
                wait("tensor", "dv", t_trcp[0])
                for dc in range(8):
                    fn = (lambda e, l=x1t_sb[:, dc, :], r=rw_sb[:, dc, :],
                          s=(dc == 0), t=(dc == 7):
                          e.matmul(ms_ps[:, 128:136], l, r, start=s, stop=t))
                    if dc < 7:
                        op0("tensor", fn)
                    else:
                        tpe_rt = opi("tensor", "pe", fn)
                t_rtr_last = tpe_rt
                wait("vector", "pe", tpe_rt)
                opi("vector", "dv", lambda e: e.reduce_max(
                    small[:, 12:13], ms_ps[:, 128:136], axis=AX, negate=True))
                t_nm = cnt["dv"]
                wait("scalar", "dv", t_nm)
                t_ex = opi("scalar", "ac", lambda e: e.activation(
                    small[:, 24:32], ms_ps[:, 128:136], AF.Exp, bias=small[:, 12:13],
                    scale=1.0, accum_out=small[:, 8:9]))
                wait("vector", "ac", t_ex)
                opi("vector", "dv", lambda e, tb_=tb: e.reciprocal(
                    gate_tm[:, tb_:tb_ + 1], small[:, 8:9]))
                opi("vector", "dv", lambda e: e.tensor_scalar(
                    small[:, 0:8], ms_ps[:, 128:136], small[:, 12:13], None, OP.add))
                opi("vector", "dv", lambda e: e.tensor_scalar(
                    small[:, 0:8], small[:, 0:8], 0.0, -BIGE, OP.is_ge, OP.mult))
                opi("vector", "dv", lambda e: e.tensor_tensor(
                    small[:, 0:8], small[:, 0:8], erowp_sb[:, :], OP.add))
                opi("vector", "dv", lambda e: e.tensor_reduce(
                    small[:, 9:10], small[:, 0:8], AX, OP.min))
                t_eq_prev = opi("vector", "dv", lambda e, tb_=tb: e.tensor_scalar(
                    emask_tm[:, tb_, :], erow_sb[:, :], small[:, 9:10], None,
                    OP.is_equal))
            ao_guard[2] = t_res
            ao_guard[3] = t_res
            t_zn_par[0] = max(t_zn_par[0], t_trcp[1])

        mark("C")
        stC = ExitStack()
        emt = stC.enter_context(nc.sbuf_tensor("emt", [16, 8, 128], F32))
        posg = stC.enter_context(nc.sbuf_tensor("posg", [16, 8, 128], F32))
        racc = stC.enter_context(nc.sbuf_tensor("racc", [16, 4, 128], F32))
        rta_sb = stC.enter_context(nc.sbuf_tensor("rta_sb", [128, 2, 16], F32))
        vals_sb = stC.enter_context(nc.sbuf_tensor("vals_sb", [128, 16, 64], F32))
        zer_sb = stC.enter_context(nc.sbuf_tensor("zer_sb", [128, 2080], F32))
        lsta_sb = stC.enter_context(nc.sbuf_tensor("lsta_sb", [128, 16, 64], F32))

        wait("gpsimd", "sxg", 256)
        t_agx = opi("gpsimd", "cc", lambda e: e.collective_compute(
            "AllGather", OP.bypass, replica_groups=[list(range(8))],
            ins=[xg_i.ap().opt()], outs=[xg_o.ap().opt()])._wait_ge(sems["sxg"], 256))


        wait("tensor", "dv", t_eq_prev)
        for e8 in range(8):
            tpe = opi("tensor", "pe", lambda e, s=emask_tm[:, :, e8]:
                      e.transpose(ms_ps[0:16, 0:128], s, ident_sb[:, :]))
            wait("vector", "pe", tpe)
            tdv = opi("vector", "dv", lambda e, d=emt[:, e8, :]:
                      e.tensor_copy(d, ms_ps[0:16, 0:128]))
            wait("tensor", "dv", tdv)
        for e8 in range(8):
            opi("vector", "dv", lambda e, e_=e8: e.reduce_sum(
                rvec[0:16, e_:e_ + 1], emt[:, e_, :], axis=AX))
        t_tot = cnt["dv"]
        wait("tensor", "dv", t_tot)
        tpe_c = opi("tensor", "pe", lambda e: e.matmul(
            ms_ps[0:8, 136:137], rvec[0:16, 0:8], ones_sb[0:16, 0:1],
            start=True, stop=True))
        wait("vector", "pe", tpe_c)
        t_cv = opi("vector", "dv", lambda e: e.tensor_copy(
            small[0:8, 10:11], ms_ps[0:8, 136:137]))
        wait("scalar", "dv", t_cv)
        dma("scalar", "sct", cnt_i.rearrange("(a c) -> a c", c=1), small[0:8, 10:11])
        wait("gpsimd", "sct", 16)
        t_agc = opi("gpsimd", "cc", lambda e: e.collective_compute(
            "AllGather", OP.bypass, replica_groups=[list(range(8))],
            ins=[cnt_i.ap().opt()], outs=[cnt_o.ap().opt()])._wait_ge(sems["sct"], 16))
        wait("gpsimd", "cc", t_agc)
        mark("C1")
        t_gpc = dma("gpsimd", "gpc", cnts_ag[:, :], cnt_o.rearrange("(c e) -> c e", e=8))
        wait("gpsimd", "gpc", t_gpc)
        wait("tensor", "gpc", t_gpc)
        op0("tensor", lambda e: e.matmul(
            ms_ps[0:16, 144:152], rvec[0:16, 48:64], rvec[0:16, 0:8],
            start=True, stop=False))
        tpe_o = opi("tensor", "pe", lambda e: e.matmul(
            ms_ps[0:16, 144:152], meta_sb[0:8, 0:16], cnts_ag[:, :],
            start=False, stop=True))
        wait("vector", "pe", tpe_o)
        opi("vector", "dv", lambda e: e.tensor_copy(
            rvec[0:16, 8:16], ms_ps[0:16, 144:152]))
        opi("vector", "dv", lambda e: e.memset(racc[:, 3, :], 0.0))
        for e8 in range(8):
            opi("vector", "dv", lambda e, e_=e8: e.tensor_tensor_scan(
                posg[:, e_, :], emt[:, e_, :], racc[0:16, 3, :],
                rvec[0:16, 8 + e_:9 + e_], OP.add, OP.add))
        opi("vector", "dv", lambda e: e.memset(racc[:, 0, :], 0.0))
        opi("vector", "dv", lambda e: e.memset(racc[:, 1, :], 0.0))
        opi("vector", "dv", lambda e: e.memset(racc[:, 2, :], 0.0))
        t_fit = 0
        for e8 in range(8):
            opi("vector", "dv", lambda e, e_=e8: e.tensor_scalar(
                racc[:, 3, :], posg[:, e_, :], float(e_ * SEG - 1),
                float(e_ * SEG + SEG - 1), OP.add, OP.min))
            opi("vector", "dv", lambda e, e_=e8: e.tensor_tensor(
                racc[:, 3, :], racc[:, 3, :], emt[:, e_, :], OP.mult))
            opi("vector", "dv", lambda e: e.tensor_tensor(
                racc[:, 0, :], racc[:, 0, :], racc[:, 3, :], OP.add))
            opi("vector", "dv", lambda e, e_=e8: e.tensor_scalar(
                racc[:, 3, :], posg[:, e_, :], float(e_ * 2048 - 1),
                float(e_ * 2048 + 2047), OP.add, OP.min))
            opi("vector", "dv", lambda e, e_=e8: e.tensor_tensor(
                racc[:, 3, :], racc[:, 3, :], emt[:, e_, :], OP.mult))
            opi("vector", "dv", lambda e: e.tensor_tensor(
                racc[:, 1, :], racc[:, 1, :], racc[:, 3, :], OP.add))
            opi("vector", "dv", lambda e, e_=e8: e.tensor_scalar(
                racc[:, 3, :], posg[:, e_, :], 2048.0, None, OP.is_le))
            opi("vector", "dv", lambda e, e_=e8: e.tensor_tensor(
                racc[:, 3, :], racc[:, 3, :], emt[:, e_, :], OP.mult))
            t_fit = opi("vector", "dv", lambda e: e.tensor_tensor(
                racc[:, 2, :], racc[:, 2, :], racc[:, 3, :], OP.add))
        wait("scalar", "dv", t_fit)
        dma("scalar", "srt", rt[0:1, :].rearrange("x (p c) -> p x c", c=128),
            racc[:, 0:1, :])
        dma("scalar", "srt", rt[1:2, :].rearrange("x (p c) -> p x c", c=128),
            racc[:, 1:2, :])
        wait("sync", "srt", 32)
        mark("C2")
        dma("sync", "lrt", rta_sb[:, :, :],
            rt.rearrange("x (b g) -> b x g", g=16))
        wait("tensor", "lrt", 16)
        mark("C2a")
        for which, dst_i16 in ((0, posm_sb), (1, gidx_sb)):
            tpe = opi("tensor", "pe", lambda e, s=rta_sb[:, which, :]:
                      e.transpose(ms_ps[0:16, 0:128], s, ident_sb[:, :]))
            wait("vector", "pe", tpe)
            tdv = opi("vector", "dv", lambda e, d=dst_i16[0:16, :]:
                      e.tensor_copy(d, ms_ps[0:16, 0:128]))
            wait("tensor", "dv", tdv)
        mark("C2b")
        tpe_f = opi("tensor", "pe", lambda e: e.transpose(
            ms_ps[:, 384:400], racc[0:16, 2, :], ident_sb[0:16, 0:16]))
        wait("vector", "pe", tpe_f)
        t_ge = opi("vector", "dv", lambda e: e.tensor_tensor(
            gact_tm[:, :], gate_tm[:, :], ms_ps[:, 384:400], OP.mult))
        if KDEBUG:
            dbgs = {
                "dbg_posm": (posm_sb[0:16, :], I16, [16, 128]),
                "dbg_gidx": (gidx_sb[0:16, :], I16, [16, 128]),
                "dbg_xon": (None, None, None),
            }
            del dbgs["dbg_xon"]
            dbg_list = [
                ("dbg_posm", posm_sb[0:16, :], I16, [16, 128]),
                ("dbg_gidx", gidx_sb[0:16, :], I16, [16, 128]),
                ("dbg_racc", racc[:, :, :], F32, [16, 4, 128]),
                ("dbg_posg", posg[:, :, :], F32, [16, 8, 128]),
                ("dbg_emt", emt[:, :, :], F32, [16, 8, 128]),
                ("dbg_rvec", rvec[0:16, 0:16], F32, [16, 16]),
                ("dbg_cnts", cnts_ag[:, :], F32, [8, 8]),
                ("dbg_small", small[:, 8:10], F32, [128, 2]),
                ("dbg_gate", gate_tm[:, :], F32, [128, 16]),
                ("dbg_gact", gact_tm[:, :], F32, [128, 16]),
                ("dbg_emask", emask_tm[:, :, :], F32, [128, 16, 8]),
            ]
            wait("sync", "dv", cnt["dv"])
            wait("sync", "ac", cnt["ac"])
            for nm, ap_, dt_, shp in dbg_list:
                od = nc.dram_tensor(nm, shp, dt_, kind="ExternalOutput").ap()
                dma("sync", "lx1", od[tuple(slice(None) for _ in shp)], ap_)
            wait("sync", "lx1", cnt["lx1"])
        mark("C2c")
        wait("tensor", "dv", t_ge)
        tpe_g = opi("tensor", "pe", lambda e: e.matmul(
            ms_ps[0:8, 408:409], cnts_ag[:, :], ones_sb[0:8, 0:1],
            start=True, stop=True))
        wait("vector", "pe", tpe_g)
        t_gt = opi("vector", "dv", lambda e: e.tensor_copy(
            small[0:8, 13:14], ms_ps[0:8, 408:409]))
        wait("tensor", "dv", t_gt)
        mark("C2d")
        tpe_m = opi("tensor", "pe", lambda e: e.matmul(
            ms_ps[0:1, 400:408], small[0:8, 13:14], meta_sb[0:8, 16:24],
            start=True, stop=True))
        wait("vector", "pe", tpe_m)
        opi("vector", "dv", lambda e: e.tensor_copy(small[0:1, 8:16],
                                                    ms_ps[0:1, 400:408]))
        mark("C2e")
        opi("vector", "dv", lambda e: e.tensor_scalar(
            ccntf_sb[:, :], k512_sb[:, :], -1.0, small[0:1, 8:9], OP.mult, OP.add))
        opi("vector", "dv", lambda e: e.tensor_scalar_max(
            ccntf_sb[:, :], ccntf_sb[:, :], 0.0))
        opi("vector", "dv", lambda e: e.tensor_scalar_min(
            ccntf_sb[:, :], ccntf_sb[:, :], 512.0))
        t_ccnt = opi("vector", "dv", lambda e: e.tensor_copy(ccnt_sb[:, :],
                                                             ccntf_sb[:, :]))
        mark("C3")
        opi("vector", "dv", lambda e: e.memset(zer_sb[:, :], 0.0))
        opi("vector", "dv", lambda e: e.memset(vals_sb[:, :, :], 0.0))
        t_vals = opi("vector", "dv", lambda e: e.tensor_copy(vals_sb[:, :, 0],
                                                             ramp_sb[:, :]))
        wait("gpsimd", "dv", cnt["dv"])
        for qq in range(4):
            dma("gpsimd", "gpc",
                listtab.rearrange("t e -> (t e)").rearrange(
                    "(q p f) -> q p f", q=4, p=128)[qq, :, :],
                zer_sb[:, :])
        for rr in range(1, 8):
            dma("gpsimd", "gpc", posm_sb[16 * rr:16 * (rr + 1), :], posm_sb[0:16, :])
            dma("gpsimd", "gpc", gidx_sb[16 * rr:16 * (rr + 1), :], gidx_sb[0:16, :])
        wait("gpsimd", "gpc", cnt["gpc"])
        for kk in range(4):
            cnt["gsc"] += 16
            semh = sems["gsc"]
            emit("gpsimd", lambda e, k_=kk, sh=semh, w=16 * kk: e.dma_scatter_add(
                listtab.ap(), vals_sb[:, k_ * 4:(k_ + 1) * 4, :],
                posm_sb[:, k_ * 32:(k_ + 1) * 32], 512, 512, 64)
                .then_inc(sh, 16)._wait_ge(sh, w))
        wait("gpsimd", "gsc", 64)
        t_rs = opi("gpsimd", "cc", lambda e: e.collective_compute(
            "ReduceScatter", OP.add, replica_groups=[list(range(8))],
            ins=[listtab.ap().rearrange("t e -> (t e)").opt()],
            outs=[seg_t.ap().opt()])._wait_ge(sems["gsc"], 64))
        wait("gpsimd", "cc", t_rs)
        wait("gpsimd", "lb2", 32)
        t_lst = dma("gpsimd", "gpc", lsta_sb[:, :, :],
                    seg_t.rearrange("(b a e) -> b a e", a=16, e=64)[0:128, :, :])
        wait("gpsimd", "gpc", t_lst)
        wait("tensor", "gpc", t_lst)
        tpe_l = opi("tensor", "pe", lambda e: e.transpose(
            ms_ps[0:16, 0:128], lsta_sb[:, :, 0], ident_sb[:, :]))
        wait("vector", "pe", tpe_l)
        t_xi = opi("vector", "dv", lambda e: e.tensor_scalar(
            xidx_sb[0:16, :], ms_ps[0:16, 0:128], -1.0, None, OP.add))
        wait("gpsimd", "dv", t_xi)
        for rr in range(1, 8):
            dma("gpsimd", "gpc", xidx_sb[16 * rr:16 * (rr + 1), :], xidx_sb[0:16, :])
        t_xirep = cnt["gpc"]
        wait("gpsimd", "gpc", t_xirep)
        stC.close()
        stAB.close()

        mark("D")
        stD = ExitStack()
        w1_sb = stD.enter_context(nc.sbuf_tensor("w1_sb", [128, 8, F], BF16))
        w2_sb = stD.enter_context(nc.sbuf_tensor("w2_sb", [128, 32, D], BF16))
        xet_sb = stD.enter_context(nc.sbuf_tensor("xet_sb", [128, 4, 8, 512], BF16))
        h_sb = stD.enter_context(nc.sbuf_tensor("h_sb", [128, 32, 512], BF16))
        yt_sb = stD.enter_context(nc.sbuf_tensor("yt_sb", [128, 2, D], BF16))

        phase_barrier(["gpsimd", "vector", "scalar", "tensor", "sync"])
        dma("gpsimd", "gw1", w1_sb[:, :, :],
            w1_in.rearrange("(c p) f -> p c f", p=128))
        dma("gpsimd", "gw2", w2_sb[:, :, :],
            w2_in.rearrange("(c p) d -> p c d", p=128))
        t_xz = opi("vector", "dv", lambda e: e.memset(xet_sb[:, :, :, :], 0.0))
        wait("gpsimd", "cc", t_agx)
        wait("gpsimd", "dv", t_xz)
        for kk in range(4):
            gname = "gxg%d" % kk
            cnt[gname] += 16
            semh = sems[gname]

            def _mk_gather(e, k_=kk, sh=semh):
                cv = e.value_load(ccnt_sb[0:1, k_:k_ + 1], min_val=0, max_val=512)
                return e.dma_gather(
                    xet_sb[:, k_, :, :], xg_o.rearrange("(t d) -> t d", d=D),
                    xidx_sb[:, k_ * 32:(k_ + 1) * 32], 512, cv, D,
                    transpose=True).then_inc(sh, 16)
            emit("gpsimd", _mk_gather)

        wait("tensor", "gw1", 16)
        wait("tensor", "lb2", 32)
        y_evac = {0: 0, 1: 0}
        t_yst = [0] * 16
        for kk in range(4):
            wait("tensor", "gxg%d" % kk, 16)
            t_hrelu = [0] * 32
            for fb in range(32):
                ps = sc_ps[fb % 2]
                if fb >= 2:
                    wait("tensor", "ac", t_hrelu[fb - 2])
                for dc in range(8):
                    fn = (lambda e, l=w1_sb[:, dc, fb * 128:(fb + 1) * 128],
                          r=xet_sb[:, kk, dc, :], p=ps, s=(dc == 0), t=(dc == 7):
                          e.matmul(p[:, :], l, r, start=s, stop=t))
                    if dc < 7:
                        op0("tensor", fn)
                    else:
                        tpe_h = opi("tensor", "pe", fn)
                wait("scalar", "pe", tpe_h)
                t_hrelu[fb] = opi("scalar", "ac", lambda e, d=h_sb[:, fb, :],
                                  p=ps, b=b1_sb[:, fb:fb + 1]: e.activation(
                                      d, p[:, :], AF.Relu, bias=b, scale=1.0))
            if kk == 0:
                wait("tensor", "gw2", 16)
            wait("tensor", "ac", t_hrelu[31])
            for sb4 in range(4):
                islot = kk * 4 + sb4
                if islot >= 2:
                    wait("tensor", "syi%d" % (islot - 2), 16)
                if y_evac[islot % 2]:
                    wait("tensor", "dv", y_evac[islot % 2])
                ybanks = [ao_ps[0], ao_ps[1]] if islot % 2 == 0 else [z_ps[0], z_ps[1]]
                for half in range(2):
                    ps = ybanks[half]
                    for fb in range(32):
                        op0("tensor", lambda e, l=h_sb[:, fb, sb4 * 128:(sb4 + 1) * 128],
                            r=w2_sb[:, fb, half * 512:(half + 1) * 512], p=ps,
                            s=(fb == 0), t=False:
                            e.matmul(p[:, :], l, r, start=s, stop=t))
                    tpe_y = opi("tensor", "pe",
                                lambda e, r=b2hl_sb[:, half * 512:(half + 1) * 512],
                                p=ps: e.matmul(p[:, :], onesb_sb[:, :], r,
                                               start=False, stop=True))
                wait("vector", "pe", tpe_y)
                tdv_y = 0
                for half in range(2):
                    tdv_y = opi("vector", "dv", lambda e,
                                d=yt_sb[:, islot % 2, half * 512:(half + 1) * 512],
                                p=ybanks[half]: e.tensor_copy(d, p[:, :]))
                y_evac[islot % 2] = tdv_y
                wait("scalar", "dv", tdv_y)
                dma("scalar", "syi%d" % islot,
                    yg_i.rearrange("(t d) -> t d", d=D)[islot * 128:(islot + 1) * 128, :],
                    yt_sb[:, islot % 2, :])
        stD.close()

        mark("E")
        stE = ExitStack()
        ytok_sb = stE.enter_context(nc.sbuf_tensor("ytok_sb", [128, 16, D], BF16))
        x2_sb = stE.enter_context(nc.sbuf_tensor("x2_sb", [128, D], F32))
        sq2_sb = stE.enter_context(nc.sbuf_tensor("sq2_sb", [128, D], F32))

        for ii in range(16):
            wait("gpsimd", "syi%d" % ii, 16)
        t_gss = opi("gpsimd", "cc", lambda e: e.nop())
        t_agy = opi("gpsimd", "cc", lambda e: e.collective_compute(
            "AllGather", OP.bypass, replica_groups=[list(range(8))],
            ins=[yg_i.ap().opt()], outs=[yg_o.ap().opt()])._wait_ge(sems["cc"], t_gss))
        wait("gpsimd", "cc", t_agy)
        phase_barrier(["gpsimd", "vector", "scalar", "tensor", "sync"])
        for kk in range(4):
            gname = "gyg%d" % kk
            cnt[gname] += 16
            semh = sems[gname]
            emit("gpsimd", lambda e, k_=kk, sh=semh: e.dma_gather(
                ytok_sb[:, k_ * 4:(k_ + 1) * 4, :], yg_o.rearrange("(t d) -> t d", d=D),
                gidx_sb[:, k_ * 32:(k_ + 1) * 32], 512, 512, D).then_inc(sh, 16))

        for tb in range(16):
            if tb > 0:
                wait("sync", "soute" if (tb - 1) % 2 == 0 else "souto",
                     16 * ((tb - 1) // 2 + 1))
            t_x1r = dma("sync", "lx1", x2_sb[:, :], x1f[tb * 128:(tb + 1) * 128, :])
            wait("vector", "lx1", t_x1r)
            wait("vector", "gyg%d" % (tb // 4), 16)
            opi("vector", "dv", lambda e, tb_=tb: e.scalar_tensor_tensor(
                x2_sb[:, :], ytok_sb[:, tb_, :], gact_tm[:, tb_:tb_ + 1], x2_sb[:, :],
                OP.mult, OP.add))
            opi("vector", "dv", lambda e: e.reduce_sum(
                lnv_sb[:, 0:1], x2_sb[:, :], axis=AX))
            opi("vector", "dv", lambda e: e.tensor_scalar(
                lnv_sb[:, 1:2], lnv_sb[:, 0:1], 1.0 / D, None, OP.mult))
            opi("vector", "dv", lambda e: e.scalar_tensor_tensor(
                sq2_sb[:, :], x2_sb[:, :], 1.0, x2_sb[:, :], OP.mult, OP.mult,
                accum_out=lnv_sb[:, 2:3]))
            opi("vector", "dv", lambda e: e.tensor_scalar(
                lnv_sb[:, 3:4], lnv_sb[:, 2:3], 1.0 / D, EPS, OP.mult, OP.add))
            opi("vector", "dv", lambda e: e.tensor_tensor(
                lnv_sb[:, 4:5], lnv_sb[:, 1:2], lnv_sb[:, 1:2], OP.mult))
            t_var2 = opi("vector", "dv", lambda e: e.tensor_sub(
                lnv_sb[:, 3:4], lnv_sb[:, 3:4], lnv_sb[:, 4:5]))
            wait("scalar", "dv", t_var2)
            opi("scalar", "ac", lambda e: e.activation(
                lnv_sb[:, 5:6], lnv_sb[:, 3:4], AF.Ln, bias=0.0, scale=1.0))
            t_rs2 = opi("scalar", "ac", lambda e: e.activation(
                lnv_sb[:, 6:7], lnv_sb[:, 5:6], AF.Exp, bias=0.0, scale=-0.5))
            wait("vector", "ac", t_rs2)
            t_x2 = opi("vector", "dv", lambda e: e.tensor_scalar(
                x2_sb[:, :], x2_sb[:, :], lnv_sb[:, 1:2], lnv_sb[:, 6:7],
                OP.subtract, OP.mult))
            wait("scalar", "dv", t_x2)
            dma("scalar", "soute" if tb % 2 == 0 else "souto",
                out_t[tb * 128:(tb + 1) * 128, :], x2_sb[:, :])
        stE.close()

        wait("scalar", "soute", cnt["soute"])
        wait("scalar", "souto", cnt["souto"])
        for nm in ("gxg0", "gxg1", "gxg2", "gxg3", "gyg0", "gyg1", "gyg2", "gyg3",
                   "gw1", "gw2", "gpc", "gsc", "gpw", "gpwt"):
            wait("gpsimd", nm, cnt[nm])
        wait("gpsimd", "cc", cnt["cc"])
        for nm in ("ldc", "lx0e", "lx0o", "lb2", "lx1", "srt", "sxf", "sxg"):
            wait("sync", nm, cnt[nm])
        wait("scalar", "sx0e", cnt["sx0e"])
        wait("scalar", "sx0o", cnt["sx0o"])
        for ii in range(16):
            wait("scalar", "syi%d" % ii, cnt["syi%d" % ii])
        wait("scalar", "sct", cnt["sct"])

        if PHASE_LIMIT in marks:
            cut, csnap = marks[PHASE_LIMIT]
            for k in plan:
                del plan[k][cut[k]:]
            eng_for = {"soute": "scalar", "souto": "scalar", "sct": "scalar",
                       "srt": "scalar", "sxf": "scalar", "sxg": "scalar",
                       "sx0e": "scalar", "sx0o": "scalar",
                       "ldc": "sync", "lx0e": "sync", "lx0o": "sync",
                       "lb2": "sync", "lx1": "sync", "lrt": "sync",
                       "cc": "gpsimd"}
            for nm, v in csnap.items():
                if v > 0 and not nm.startswith(("pe", "dv", "ac")):
                    eng = eng_for.get(nm, "gpsimd")
                    if nm.startswith("syi"):
                        eng = "scalar"
                    semh = sems[nm]
                    plan[eng].append(lambda e, s=semh, vv=v: e.wait_ge(s, vv))
        blk = ex(nc.Block())

        @blk.tensor
        def _(e):
            for fn in plan["tensor"]:
                fn(e)

        @blk.vector
        def _(e):
            for fn in plan["vector"]:
                fn(e)

        @blk.scalar
        def _(e):
            for fn in plan["scalar"]:
                fn(e)

        @blk.gpsimd
        def _(e):
            for fn in plan["gpsimd"]:
                fn(e)

        @blk.sync
        def _(e):
            for fn in plan["sync"]:
                fn(e)

    nc.compile()
    return nc


_CACHE = {}


def kernel(L0_attn_x, L0_attn_W_Q, L0_attn_W_K, L0_attn_W_V, L0_attn_W_O,
           L0_ffn_router_w, L0_ffn_w1, L0_ffn_b1, L0_ffn_w2, L0_ffn_b2):
    x = np.ascontiguousarray(np.asarray(L0_attn_x, dtype=np.float32))
    wq = np.ascontiguousarray(np.asarray(L0_attn_W_Q, np.float32).reshape(D, D).T)
    wk = np.ascontiguousarray(np.asarray(L0_attn_W_K, np.float32).reshape(D, D).T)
    wv = np.ascontiguousarray(np.asarray(L0_attn_W_V, np.float32).reshape(D, D).T)
    wo = np.ascontiguousarray(
        np.asarray(L0_attn_W_O, np.float32).transpose(0, 2, 1).reshape(D, D))
    rw = np.ascontiguousarray(np.asarray(L0_ffn_router_w, np.float32))
    w1 = np.ascontiguousarray(np.asarray(L0_ffn_w1, np.float32))
    b1 = np.ascontiguousarray(np.asarray(L0_ffn_b1, np.float32))
    w2 = np.ascontiguousarray(np.asarray(L0_ffn_w2, np.float32))
    b2 = np.ascontiguousarray(np.asarray(L0_ffn_b2, np.float32))

    if "nc" not in _CACHE:
        _CACHE["nc"] = build_nc()
    nc = _CACHE["nc"]

    in_maps = []
    sidx = np.arange(S)
    for c in range(8):
        masklt = np.zeros((8, 16), np.float32)
        masklt[:c, :] = 1.0
        onehot = np.zeros((8, 8), np.float32)
        onehot[c, :] = 1.0
        ramp = np.zeros((128, 16), np.float32)
        ramp[sidx % 128, sidx // 128] = c * S + sidx + 1
        in_maps.append({
            "x_in": x[c], "wq_in": wq, "wk_in": wk, "wv_in": wv, "wo_in": wo,
            "rw_in": rw, "w1_in": w1[c], "b1_in": b1[c], "w2_in": w2[c],
            "b2_in": b2[c:c + 1], "masklt_in": masklt, "onehot_in": onehot,
            "ramp_in": ramp,
        })
    res = run_bass_kernel_spmd(nc, in_maps, core_ids=list(range(8)))
    return np.stack([res.results[c]["out"] for c in range(8)], axis=0)
